# version 04
# speedup vs baseline: 1.6563x; 1.0056x over previous
import numpy as np
from contextlib import ExitStack

import concourse.bass as bass
import concourse.mybir as mybir
from concourse.bass_utils import run_bass_kernel_spmd
from concourse.bass_core_rust import IndirectOffsetOnAxis

F32 = mybir.dt.float32
BF16 = mybir.dt.bfloat16
I32 = mybir.dt.int32
AF = mybir.ActivationFunctionType
ALU = mybir.AluOpType
AX = mybir.AxisListType

ENG_ATTR = {"t": "tensor", "v": "vector", "s": "scalar", "g": "gpsimd", "y": "sync"}

NCORES = 8
TOK = 4096
NB = 2
S = 2048
D = 1024
NT = TOK // 128
NC = D // 128


class _NCP:

    def __init__(self, nc, pfx):
        self._nc = nc
        self._pfx = pfx

    def dram_tensor(self, name, *a, **k):
        return self._nc.dram_tensor(self._pfx + name, *a, **k)

    def sbuf_tensor(self, name, *a, **k):
        return self._nc.sbuf_tensor(self._pfx + name, *a, **k)

    def psum_tensor(self, name, *a, **k):
        return self._nc.psum_tensor(self._pfx + name, *a, **k)

    def __getattr__(self, n):
        return getattr(self._nc, n)


class Prog:
    AUTO_CHAIN = ("v", "s", "g", "y")

    def __init__(self, nc):
        self.nc = nc
        self.ops = {e: [] for e in ENG_ATTR}
        self.cnt = {}
        self.seen = {e: {} for e in ENG_ATTR}
        self.last_ev = {e: None for e in ENG_ATTR}
        self.slot_alias = {}

    def block_boundary(self):
        for e in ENG_ATTR:
            self.barrier_all(e)
        self.slot_alias = {}

    def run(self, eng, fn, deps=(), dma=False, slot=None, ev=True):
        if dma:
            key = (eng, slot if slot is not None else eng)
            neng = sum(1 for k in self.slot_alias if k[0] == eng)
            canon = self.slot_alias.setdefault(key, "%s%d" % (eng, neng))
            sem = "D_" + canon
        else:
            sem = eng
        if eng in self.AUTO_CHAIN:
            deps = list(deps) + [self.last_ev[eng]]
        waits = {}
        for d in deps:
            if d is None:
                continue
            ds, dv = d
            if dv > self.seen[eng].get(ds, 0):
                waits[ds] = max(waits.get(ds, 0), dv)
        for ds, dv in waits.items():
            self.seen[eng][ds] = dv
        myev = None
        amt = 0
        if ev:
            amt = 16 if dma else 1
            self.cnt[sem] = self.cnt.get(sem, 0) + amt
            myev = (sem, self.cnt[sem])
        self.ops[eng].append((fn, sorted(waits.items()), sem if ev else None, amt))
        if myev is not None:
            self.last_ev[eng] = myev
        return myev

    def barrier(self, eng, deps):
        return self.run(eng, lambda e: e.nop(), deps=deps, ev=True)

    def barrier_all(self, eng):
        deps = [(s, c) for s, c in self.cnt.items()]
        return self.run(eng, lambda e: e.nop(), deps=deps, ev=True)

    def emit(self):
        nc = self.nc
        sems = sorted(self.cnt.keys())
        with ExitStack() as st:
            h = {sn: st.enter_context(nc.semaphore(sn)) for sn in sems}
            with nc.Block() as block:
                for e, attr in ENG_ATTR.items():
                    ops = self.ops[e]
                    if not ops:
                        continue

                    def mkbody(ops):
                        def body(engobj):
                            for fn, waits, sem, amt in ops:
                                for ds, dv in waits:
                                    engobj.wait_ge(h[ds], dv)
                                inst = fn(engobj)
                                if sem is not None:
                                    inst.then_inc(h[sem], amt)
                        return body

                    getattr(block, attr)(mkbody(ops))


def _shard_x(x):
    x = np.ascontiguousarray(np.asarray(x), dtype=np.float32).reshape(16, 2048, 1024)
    return [np.ascontiguousarray(x[2 * c:2 * c + 2].reshape(TOK, D)) for c in range(NCORES)]


def _gather_x(shards):
    return np.concatenate([np.asarray(s).reshape(2, 2048, 1024) for s in shards], axis=0)


def _f32(a):
    return np.ascontiguousarray(np.asarray(a), dtype=np.float32)


def emit_ln_stats(p, tiles, src_ev, src, prev_ln):
    sm = p.run("v", lambda e: e.reduce_sum(tiles["sums"][:], src[:], axis=AX.X), deps=[src_ev, prev_ln])
    sq = p.run("s", lambda e: e.activation(out=tiles["scr0"][:], in_=src[:], func=AF.Square,
                                           accum_out=tiles["sqs"][:]), deps=[src_ev, prev_ln])
    mn = p.run("v", lambda e: e.tensor_scalar(tiles["mean"][:], tiles["sums"][:], 1.0 / D, None,
                                              op0=ALU.mult), deps=[sm])
    m2 = p.run("v", lambda e: e.tensor_tensor(tiles["m2"][:], tiles["mean"][:], tiles["mean"][:],
                                              op=ALU.mult), deps=[mn])
    vr = p.run("v", lambda e: e.scalar_tensor_tensor(tiles["var"][:], tiles["sqs"][:], 1.0 / D,
                                                     tiles["m2"][:], op0=ALU.mult, op1=ALU.subtract),
               deps=[sq, m2])
    sd = p.run("s", lambda e: e.activation(out=tiles["std"][:], in_=tiles["var"][:], func=AF.Sqrt,
                                           bias=tiles["eps"][:, 0:1]), deps=[vr, tiles.get("eps_ev")])
    rs = p.run("v", lambda e: e.reciprocal(tiles["rstd"][:], tiles["std"][:]), deps=[sd])
    return rs


def emit_ln_prologue(p, tiles, x_in, xnT_sb, identb, pst, dep0=(),
                     x_out_plus_xn=None, xn_hbm=None):
    xev = [None, None]
    prev_xpx = [None, None]
    prev_ln = None
    tr_done = [None] * len(pst)
    last_store = None
    last_xn = None
    for tt in range(NT):
        b = tt % 2
        xt = tiles["xt"][b]
        ld = p.run("y", lambda e, xt=xt, tt=tt: e.dma_start(
            out=xt[:], in_=x_in[tt * 128:(tt + 1) * 128, :]),
            deps=[xev[b]] + list(dep0 if tt < 2 else ()), dma=True, slot="x%d" % b)
        rs = emit_ln_stats(p, tiles, ld, xt, prev_ln)
        xnf = tiles["xnf"][b]
        lnev = p.run("v", lambda e, xt=xt, xnf=xnf: e.tensor_scalar(
            xnf[:], xt[:], tiles["mean"][:, 0:1], tiles["rstd"][:, 0:1],
            op0=ALU.subtract, op1=ALU.mult), deps=[rs])
        prev_ln = lnev
        xnb = tiles["xnb"][b]
        cb = p.run("v", lambda e, xnf=xnf, xnb=xnb: e.tensor_copy(xnb[:], xnf[:]), deps=[lnev])
        if xn_hbm is not None:
            last_xn = p.run("y", lambda e, xnb=xnb, tt=tt: e.dma_start(
                out=xn_hbm[tt * 128:(tt + 1) * 128, :], in_=xnb[:]), deps=[cb], dma=True,
                slot="xnh")
        if x_out_plus_xn is not None:
            so = p.run("v", lambda e, xt=xt, xnf=xnf, b=b: e.tensor_add(
                tiles["xpx"][b][:], xt[:], xnf[:]), deps=[lnev, prev_xpx[b]])
            last_store = p.run("y", lambda e, tt=tt, b=b: e.dma_start(
                out=x_out_plus_xn[tt * 128:(tt + 1) * 128, :], in_=tiles["xpx"][b][:]),
                deps=[so], dma=True, slot="xo%d" % b)
            prev_xpx[b] = last_store
            xev[b] = so
        else:
            xev[b] = lnev
        if xnT_sb is not None:
            for c in range(NC):
                pi = (tt * NC + c) % len(pst)
                tr = p.run("t", lambda e, xnb=xnb, c=c, pi=pi: e.transpose(
                    pst[pi][:, 0:128], xnb[:, c * 128:(c + 1) * 128], identb[:]),
                    deps=[cb, tr_done[pi]] + list(dep0 if tt == 0 else ()))
                tr_done[pi] = p.run("v", lambda e, tt=tt, c=c, pi=pi: e.tensor_copy(
                    xnT_sb[:, c * TOK + tt * 128: c * TOK + (tt + 1) * 128], pst[pi][:, 0:128]),
                    deps=[tr])
        else:
            tr_done = [cb, cb]
    done = list(tr_done) + [prev_ln]
    if last_store is not None:
        done.append(last_store)
    if last_xn is not None:
        done.append(last_xn)
    return done


def mk_tiles(st, nc, with_xpx=False):
    def sb(name, shape, dt=BF16):
        return st.enter_context(nc.sbuf_tensor(name, shape, dt))
    return dict(
        xt=[sb("xt0", [128, D], F32), sb("xt1", [128, D], F32)],
        xnf=[sb("xnf0", [128, D], F32), sb("xnf1", [128, D], F32)],
        xnb=[sb("xnb0", [128, D]), sb("xnb1", [128, D])],
        xpx=[sb("xpx0", [128, D], F32), sb("xpx1", [128, D], F32)] if with_xpx else None,
        scr0=sb("lnscr0", [128, D], F32),
        sums=sb("sums", [128, 1], F32), sqs=sb("sqs", [128, 1], F32),
        mean=sb("mean", [128, 1], F32), m2=sb("m2s", [128, 1], F32),
        var=sb("var", [128, 1], F32), std=sb("std", [128, 1], F32),
        rstd=sb("rstd", [128, 1], F32), eps=sb("epsc", [128, 1], F32),
    )


def build_block3(phase=2, sub=9, fused=None):
    nc = (bass.Bass("TRN2", target_bir_lowering=False) if fused is None
          else _NCP(fused[0], fused[2]))
    H, DH, R = 8, 128, 32
    SCALE = 1.0 / np.sqrt(DH)
    FF = 4096
    TCH = 256
    NTC = TOK // TCH

    if fused is None:
        x_in = nc.dram_tensor("x_in", [TOK, D], F32, kind="ExternalInput").ap()
        x_out = nc.dram_tensor("x_out", [TOK, D], F32, kind="ExternalOutput").ap()
    else:
        x_in, x_out = fused[3], fused[4]

    vqkvT = nc.dram_tensor("vqkvT", [128, H * 96], F32, kind="ExternalInput").ap()
    uqkvT = nc.dram_tensor("uqkvT", [96, H * 128], F32, kind="ExternalInput").ap()
    bqkT = nc.dram_tensor("bqkT", [128, 2 * H], F32, kind="ExternalInput").ap()
    bvfl = nc.dram_tensor("bvfl", [1, D], F32, kind="ExternalInput").ap()
    identf = nc.dram_tensor("identf", [128, 128], F32, kind="ExternalInput").ap()
    onesf = nc.dram_tensor("onesf", [128, 512], F32, kind="ExternalInput").ap()
    g1bc = nc.dram_tensor("g1bc", [128, D], F32, kind="ExternalInput").ap()
    b1bc = nc.dram_tensor("b1bc", [128, D], F32, kind="ExternalInput").ap()
    w1T_h = nc.dram_tensor("w1T", [D, FF], F32, kind="ExternalInput").ap()
    b1pp_h = nc.dram_tensor("b1pp", [128, FF // 128], F32, kind="ExternalInput").ap()
    ln2gb_h = nc.dram_tensor("ln2gb", [2, FF], F32, kind="ExternalInput").ap()
    w2T_h = nc.dram_tensor("w2T", [FF, D], F32, kind="ExternalInput").ap()
    b2row_h = nc.dram_tensor("b2row", [1, D], F32, kind="ExternalInput").ap()

    with ExitStack() as st:
        def sb(name, shape, dt=BF16):
            return st.enter_context(nc.sbuf_tensor(name, shape, dt))

        bufA = sb("bufA", [128, NC * TOK])
        tiles = mk_tiles(st, nc)
        identb = sb("identb", [128, 128])
        onesb_r = sb("onesb_r", [1, 128])
        onesb_c = sb("onesb_c", [128, 1])
        onesf_c = sb("onesf_c", [128, 1], F32)
        onesf_r = sb("onesf_r", [1, 128], F32)
        st2 = ExitStack()

        def sb2(name, shape, dt=BF16):
            return st2.enter_context(nc.sbuf_tensor(name, shape, dt))
        vqkv_sb = sb2("vqkv_sb", [128, H * 96])
        uqkv_sb = sb2("uqkv_sb", [96, H * 128])
        bqk_sb = sb2("bqk_sb", [128, 2 * H], F32)
        bv_sb = sb2("bv_sb", [1, D])
        tqkv = sb2("tqkv", [96, S])
        qT = sb2("qT", [128, S])
        kT = sb2("kT", [128, S])
        V = sb2("V", [128, 16 * 129])
        PT = [sb2("PT0", [128, 512]), sb2("PT1", [128, 512])]
        sqscr = sb2("sqscr", [128, S])
        normrow = sb2("normrow", [1, S], F32)
        sc1 = sb2("sc1", [1, 1], F32)
        sc2 = sb2("sc2", [1, 1], F32)
        negb = sb2("negb", [128, 1], F32)
        rz = sb2("rz", [128, 1], F32)
        at = [sb2("at%d" % i, [128, 128], F32) for i in range(4)]
        bvb = sb2("bvb", [128, 128])

        PS = [st.enter_context(nc.psum_tensor("ps%d" % i, [128, 512], F32)) for i in range(6)]
        PST = [st.enter_context(nc.psum_tensor("pstt%d" % i, [128, 1024], BF16)) for i in range(2)]

        p = Prog(nc) if fused is None else fused[1]
        tiles["eps_ev"] = p.run("v", lambda e: e.memset(tiles["eps"][:], 1e-5))

        ev_init = p.run("y", lambda e: e.dma_start(out=x_out, in_=x_in), dma=True, slot="init")

        for ld_fn in [
            lambda e: e.dma_start(out=identb[:], in_=identf),
            lambda e: e.dma_start(out=onesb_r[:], in_=onesf[0:1, 0:128]),
            lambda e: e.dma_start(out=onesb_c[:], in_=onesf[:, 0:1], allow_slow_non_contiguous=True),
            lambda e: e.dma_start(out=onesf_c[:], in_=onesf[:, 0:1], allow_slow_non_contiguous=True),
            lambda e: e.dma_start(out=onesf_r[:], in_=onesf[0:1, 0:128]),
            lambda e: e.dma_start(out=vqkv_sb[:], in_=vqkvT),
            lambda e: e.dma_start(out=uqkv_sb[:], in_=uqkvT),
            lambda e: e.dma_start(out=bqk_sb[:], in_=bqkT),
            lambda e: e.dma_start(out=bv_sb[:], in_=bvfl),
            lambda e: e.dma_start(out=g1b_sb[:], in_=g1bc),
            lambda e: e.dma_start(out=b1b_sb[:], in_=b1bc),
            lambda e: e.dma_start(out=b1pp[:], in_=b1pp_h),
            lambda e: e.dma_start(out=ln2gb[:], in_=ln2gb_h),
            lambda e: e.dma_start(out=b2row[:], in_=b2row_h),
        ]:
            ev_const = p.run("g", ld_fn, dma=True, slot="const")
        ev_idb = ev_or = ev_oc = ev_ofc = ev_ofr = ev_const
        ev_w1 = ev_w2 = ev_w3 = ev_w4 = ev_const
        ev_g1 = ev_b1 = ev_b1pp = ev_l2 = ev_b2r = ev_const

        ln_done = emit_ln_prologue(p, tiles, x_in, bufA, identb, PST, dep0=[ev_idb])

        ev_vones = None
        for kt in range(16):
            ev_vones = p.run("v", lambda e, kt=kt: e.memset(V[:, kt * 129 + 128:kt * 129 + 129], 1.0))

        attn_accums = []
        xnh_evs = []
        iter_dep = [None]
        at_free = [None] * 4
        for b in (range(NB) if phase >= 1 else []):
            for h in range(H):
                deps0 = ln_done + [ev_w1] + iter_dep
                ev_tq = []
                prev = None
                for cqc in range(4):
                    mm = p.run("t", lambda e, cqc=cqc, h=h, b=b: e.matmul(
                        PS[4][0:96, 0:512], vqkv_sb[:, h * 96:(h + 1) * 96],
                        bufA[:, h * TOK + b * S + cqc * 512: h * TOK + b * S + (cqc + 1) * 512],
                        start=True, stop=True),
                        deps=(deps0 if cqc == 0 else []) + [prev])
                    cv = p.run("v", lambda e, cqc=cqc: e.tensor_copy(
                        tqkv[:, cqc * 512:(cqc + 1) * 512], PS[4][0:96, 0:512]), deps=[mm])
                    ev_tq.append(cv)
                    prev = cv
                ev_q = None
                ev_k = None
                for cqc in range(4):
                    mq = p.run("t", lambda e, cqc=cqc, h=h: e.matmul(
                        PS[0][:, 0:512], uqkv_sb[0:32, h * 128:(h + 1) * 128],
                        tqkv[0:32, cqc * 512:(cqc + 1) * 512], start=True, stop=True),
                        deps=[ev_tq[cqc], ev_w2, ev_q])
                    ev_q = p.run("s", lambda e, cqc=cqc, h=h: e.activation(
                        out=qT[:, cqc * 512:(cqc + 1) * 512], in_=PS[0][:, 0:512], func=AF.Identity,
                        bias=bqk_sb[:, h:h + 1]), deps=[mq, ev_w3])
                    mk = p.run("t", lambda e, cqc=cqc, h=h: e.matmul(
                        PS[1][:, 0:512], uqkv_sb[32:64, h * 128:(h + 1) * 128],
                        tqkv[32:64, cqc * 512:(cqc + 1) * 512], start=True, stop=True),
                        deps=[ev_tq[cqc], ev_k])
                    ev_k = p.run("s", lambda e, cqc=cqc, h=h: e.activation(
                        out=kT[:, cqc * 512:(cqc + 1) * 512], in_=PS[1][:, 0:512], func=AF.Identity,
                        bias=bqk_sb[:, H + h:H + h + 1]), deps=[mk, ev_w3])
                mvb = p.run("t", lambda e, h=h: e.matmul(
                    PS[3][:, 0:128], onesb_r[0:1, :], bv_sb[0:1, h * 128:(h + 1) * 128],
                    start=True, stop=True), deps=[ev_w4, ev_or] + deps0)
                ev_bvb = p.run("v", lambda e: e.tensor_copy(bvb[:], PS[3][:, 0:128]), deps=[mvb])
                ev_v = None
                for kt in range(16):
                    cqc, off = kt // 4, (kt % 4) * 128
                    mv2 = p.run("t", lambda e, cqc=cqc, off=off, h=h: e.matmul(
                        PS[2][:, 0:128], tqkv[64:96, cqc * 512 + off: cqc * 512 + off + 128],
                        uqkv_sb[64:96, h * 128:(h + 1) * 128], start=True, stop=True),
                        deps=[ev_tq[cqc], ev_v] + (deps0 if kt == 0 else []))
                    ev_v = p.run("v", lambda e, kt=kt: e.tensor_tensor(
                        V[:, kt * 129:kt * 129 + 128], PS[2][:, 0:128], bvb[:], op=ALU.add),
                        deps=[mv2, ev_bvb])
                if sub < 3:
                    iter_dep = [ev_q, ev_k, ev_v]
                    continue
                sq_q = p.run("s", lambda e: e.activation(out=sqscr[:], in_=qT[:], func=AF.Square),
                             deps=[ev_q])
                ev_nq = ev_bvb
                for cqc in range(4):
                    mm = p.run("t", lambda e, cqc=cqc: e.matmul(
                        PS[3][0:1, 0:512], onesb_c[:, 0:1], sqscr[:, cqc * 512:(cqc + 1) * 512],
                        start=True, stop=True), deps=[sq_q, ev_oc, ev_nq])
                    ev_nq = p.run("v", lambda e, cqc=cqc: e.tensor_copy(
                        normrow[0:1, cqc * 512:(cqc + 1) * 512], PS[3][0:1, 0:512]), deps=[mm])
                mq_ev = p.run("v", lambda e: e.reduce_max(sc1[0:1, 0:1], normrow[0:1, :], axis=AX.X),
                              deps=[ev_nq])
                sq_k = p.run("s", lambda e: e.activation(out=sqscr[:], in_=kT[:], func=AF.Square),
                             deps=[ev_k, ev_nq])
                ev_nk = None
                for cqc in range(4):
                    mm = p.run("t", lambda e, cqc=cqc: e.matmul(
                        PS[3][0:1, 0:512], onesb_c[:, 0:1], sqscr[:, cqc * 512:(cqc + 1) * 512],
                        start=True, stop=True), deps=[sq_k, ev_nk, mq_ev])
                    ev_nk = p.run("v", lambda e, cqc=cqc: e.tensor_copy(
                        normrow[0:1, cqc * 512:(cqc + 1) * 512], PS[3][0:1, 0:512]), deps=[mm])
                mk_ev = p.run("v", lambda e: e.reduce_max(sc2[0:1, 0:1], normrow[0:1, :], axis=AX.X),
                              deps=[ev_nk])
                bb = p.run("v", lambda e: e.tensor_tensor(sc1[0:1, 0:1], sc1[0:1, 0:1], sc2[0:1, 0:1],
                                                          op=ALU.mult), deps=[mq_ev, mk_ev])
                s1 = p.run("s", lambda e: e.activation(out=sc1[0:1, 0:1], in_=sc1[0:1, 0:1],
                                                       func=AF.Sqrt), deps=[bb])
                bb2 = p.run("v", lambda e: e.tensor_scalar(sc1[0:1, 0:1], sc1[0:1, 0:1], -SCALE, None,
                                                           op0=ALU.mult), deps=[s1])
                mmb = p.run("t", lambda e: e.matmul(PS[3][:, 0:1], onesf_r[0:1, :], sc1[0:1, 0:1],
                                                    start=True, stop=True), deps=[bb2, ev_ofr, ev_nk])
                ev_negb = p.run("v", lambda e: e.tensor_copy(negb[:], PS[3][:, 0:1]), deps=[mmb])

                if sub < 4:
                    iter_dep = [ev_negb, ev_v]
                    continue
                av_psum_free = [None, ev_negb, ev_tq[3], None]
                s_free = [ev_negb, None]
                pt_free = [None, None]
                last_av = {}
                for qc in range(4):
                    for kt in range(16):
                        pi = kt % 2
                        ms = p.run("t", lambda e, qc=qc, kt=kt, pi=pi: e.matmul(
                            PS[pi][:, 0:512], kT[:, kt * 128:(kt + 1) * 128],
                            qT[:, qc * 512:(qc + 1) * 512], start=True, stop=True),
                            deps=[ev_q, ev_k, s_free[pi]])
                        pe = p.run("s", lambda e, pi=pi: e.activation(
                            out=PT[pi][:], in_=PS[pi][:, 0:512], func=AF.Exp, scale=SCALE,
                            bias=negb[:, 0:1]), deps=[ms, pt_free[pi], ev_negb])
                        s_free[pi] = pe
                        for qt in (range(4) if sub >= 5 else []):
                            last_av[qt] = p.run("t", lambda e, kt=kt, qt=qt, pi=pi: e.matmul(
                                PS[2 + qt][:, 0:129], PT[pi][:, qt * 128:(qt + 1) * 128],
                                V[:, kt * 129:(kt + 1) * 129],
                                start=(kt == 0), stop=(kt == 15)),
                                deps=([pe, av_psum_free[qt], ev_v, ev_vones] if kt == 0 else [pe]))
                        pt_free[pi] = last_av[3] if sub >= 5 else pe
                    for qt in (range(4) if sub >= 6 else []):
                        rc = p.run("v", lambda e, qt=qt: e.reciprocal(
                            rz[:], PS[2 + qt][:, 128:129]), deps=[last_av[qt]])
                        sc = p.run("v", lambda e, qt=qt: e.tensor_scalar(
                            at[qt][:], PS[2 + qt][:, 0:128], rz[:, 0:1], None, op0=ALU.mult),
                            deps=[rc, at_free[qt]])
                        av_psum_free[qt] = sc
                        row0 = b * S + qc * 512 + qt * 128
                        ac = p.run("g", lambda e, row0=row0, h=h, qt=qt: e.dma_start(
                            out=x_out[row0:row0 + 128, h * 128:(h + 1) * 128], in_=at[qt][:],
                            accum_op=ALU.add), deps=[sc, ev_init], dma=True, slot="a%d" % qt)
                        at_free[qt] = ac
                        attn_accums.append(ac)
                iter_dep = [s_free[0], s_free[1]] + [x for x in av_psum_free if x]

        attn_done = (attn_accums[-4:] if attn_accums else [ev_vones]) + iter_dep + ln_done

        st2.close()
        QCH = 256
        bufB = sb("bufB", [128, 32 * D])
        wf32 = [sb("wf0", [128, QCH], F32), sb("wf1", [128, QCH], F32)]
        qscr1 = sb("qscr1", [128, QCH], F32)
        qscr2 = sb("qscr2", [128, QCH], F32)
        asum = sb("asum", [128, 1], F32)
        acc1 = sb("acc1", [128, 1], F32)
        mrow = sb("mrow", [1, 1], F32)
        m_pp = sb("m_pp", [128, 1], F32)
        s_pp = sb("s_pp", [128, 1], F32)
        m2_pp = sb("m2_pp", [128, 1], F32)
        s2_pp = sb("s2_pp", [128, 1], F32)
        g1b_sb = sb("g1b", [128, D], F32)
        b1b_sb = sb("b1b", [128, D], F32)
        b1pp = sb("b1pp_sb", [128, FF // 128], F32)
        ln2gb = sb("ln2gb_sb", [2, FF])
        b2row = sb("b2row_sb", [1, D])
        ln1T = sb("ln1T", [128, NC * TCH])
        hg = sb("hg", [128, 32 * TCH])
        hn = hg
        hsc = sb("hsc", [128, TCH])
        s1row = sb("s1row", [1, TCH], F32)
        mu2 = sb("mu2", [1, TCH], F32)
        var2 = sb("var2", [1, TCH], F32)
        s2row = sb("s2row", [1, TCH], F32)
        rstd2 = sb("rstd2", [1, TCH], F32)
        rstd2m = sb("rstd2m", [1, TCH])
        cv2 = sb("cv2", [2, TCH])
        cvU = sb("cvU", [2, TCH], F32)
        sqb = sb("sqb", [128, TCH])
        fout = [sb("fout%d" % i, [128, 512], F32) for i in range(2)]

        def quantize(wT_ap, nchunk, dst, dst_cols, m_ppt, s_ppt, pre_deps):
            nq = dst_cols // QCH
            ev_ld = [None, None]
            ev_acc = None
            z = p.run("v", lambda e: e.memset(acc1[:], 0.0), deps=pre_deps)
            first = True
            for c in range(nchunk):
                for q in range(nq):
                    bsel = (c * nq + q) % 2
                    ld = p.run("y", lambda e, c=c, q=q, bsel=bsel: e.dma_start(
                        out=wf32[bsel][:], in_=wT_ap[c * 128:(c + 1) * 128, q * QCH:(q + 1) * QCH]),
                        deps=[ev_ld[bsel]] + list(pre_deps if first else ()), dma=True,
                        slot="w%d" % bsel)
                    first = False
                    sm = p.run("v", lambda e, bsel=bsel: e.reduce_sum(
                        asum[:], wf32[bsel][:], axis=AX.X, apply_absolute_value=True),
                        deps=[ld, ev_acc])
                    ev_acc = p.run("v", lambda e: e.tensor_add(acc1[:], acc1[:], asum[:]),
                                   deps=[sm, z])
                    ev_ld[bsel] = sm
            mm = p.run("t", lambda e: e.matmul(PS[5][0:1, 0:1], onesf_c[:, 0:1], acc1[:, 0:1],
                                               start=True, stop=True),
                       deps=[ev_acc, ev_ofc] + pre_deps)
            e1a = p.run("v", lambda e, n=nchunk: e.tensor_scalar(
                mrow[0:1, 0:1], PS[5][0:1, 0:1], 1.0 / (n * 128 * dst_cols), None,
                op0=ALU.mult), deps=[mm])
            e1 = p.run("v", lambda e: e.tensor_scalar(
                mrow[0:1, 0:1], mrow[0:1, 0:1], 1e-5, None, op0=ALU.max), deps=[e1a])
            mb = p.run("t", lambda e: e.matmul(PS[5][:, 0:1], onesf_r[0:1, :], mrow[0:1, 0:1],
                                               start=True, stop=True), deps=[e1, ev_ofr])
            e2 = p.run("v", lambda e, m_ppt=m_ppt: e.tensor_copy(m_ppt[:], PS[5][:, 0:1]), deps=[mb])
            e3 = p.run("v", lambda e, m_ppt=m_ppt, s_ppt=s_ppt: e.reciprocal(s_ppt[:], m_ppt[:]),
                       deps=[e2])
            ev_q = None
            ev_ld = [None, None]
            for c in range(nchunk):
                for q in range(nq):
                    bsel = (c * nq + q) % 2
                    ld = p.run("y", lambda e, c=c, q=q, bsel=bsel: e.dma_start(
                        out=wf32[bsel][:], in_=wT_ap[c * 128:(c + 1) * 128, q * QCH:(q + 1) * QCH]),
                        deps=[ev_ld[bsel], e3], dma=True, slot="w%d" % bsel)
                    a1 = p.run("s", lambda e, bsel=bsel: e.activation(
                        out=qscr1[:], in_=wf32[bsel][:], func=AF.Abs), deps=[ld, ev_q])
                    a2a = p.run("v", lambda e, s_ppt=s_ppt: e.tensor_scalar(
                        qscr1[:], qscr1[:], s_ppt[:, 0:1], None, op0=ALU.mult), deps=[a1])
                    a2 = p.run("v", lambda e: e.tensor_scalar(
                        qscr1[:], qscr1[:], 0.5, None, op0=ALU.is_gt), deps=[a2a])
                    a3 = p.run("s", lambda e, bsel=bsel: e.sign(
                        qscr2[:], wf32[bsel][:]), deps=[ld, a1])
                    ev_q = p.run("v", lambda e, c=c, q=q, dst=dst: e.tensor_tensor(
                        dst[:, c * dst_cols + q * QCH: c * dst_cols + (q + 1) * QCH], qscr1[:],
                        qscr2[:], op=ALU.mult), deps=[a2, a3])
                    ev_ld[bsel] = ev_q
            return ev_q

        if phase < 2:
            p.barrier_all("y")
            p.emit()
            return nc
        evq1 = quantize(w1T_h, NC, bufA, FF, m_pp, s_pp,
                        attn_done + [ev_b1pp, ev_l2, ev_b2r, ev_g1, ev_b1])
        evq2 = quantize(w2T_h, 32, bufB, D, m2_pp, s2_pp, [evq1])

        ev_ot = p.run("g", lambda e: e.dma_start(out=cvU[1:2, :], in_=onesf[0:1, 0:TCH]),
                      deps=[evq2], dma=True, slot="ot")

        ffn_accums = []
        ev_ln1T = None
        ev_hg = {0: None, 1: None}
        fout_free = {(0, 0): None, (0, 1): None, (1, 0): None, (1, 1): None}
        ff2_ps_free = {}
        ev_stats = None
        prev_AB = None
        pst_free = [None, None]
        prev_ln = None
        for tc in range(NTC):
            trdep = ev_ln1T
            for tt in range(2):
                row0 = tc * TCH + tt * 128
                xt = tiles["xt"][tt]
                ld = p.run("y", lambda e, row0=row0, xt=xt: e.dma_start(
                    out=xt[:], in_=x_in[row0:row0 + 128, :]),
                    deps=[trdep, evq2, prev_ln, prev_AB], dma=True, slot="x%d" % tt)
                rs = emit_ln_stats(p, tiles, ld, xt, prev_ln)
                xnf = tiles["xnf"][tt]
                e_xn = p.run("v", lambda e, xt=xt, xnf=xnf: e.tensor_scalar(
                    xnf[:], xt[:], tiles["mean"][:, 0:1], tiles["rstd"][:, 0:1],
                    op0=ALU.subtract, op1=ALU.mult), deps=[rs])
                rs1 = emit_ln_stats(p, tiles, e_xn, xnf, None)
                e_l1 = p.run("v", lambda e, xnf=xnf: e.tensor_scalar(
                    tiles["scr0"][:], xnf[:], tiles["mean"][:, 0:1], tiles["rstd"][:, 0:1],
                    op0=ALU.subtract, op1=ALU.mult), deps=[rs1])
                e_g = p.run("v", lambda e: e.tensor_tensor(tiles["scr0"][:], tiles["scr0"][:],
                                                           g1b_sb[:], op=ALU.mult),
                            deps=[e_l1, ev_g1])
                ln1tok = tiles["xnb"][tt]
                e_bt = p.run("v", lambda e, ln1tok=ln1tok: e.tensor_tensor(
                    ln1tok[:], tiles["scr0"][:], b1b_sb[:], op=ALU.add), deps=[e_g, ev_b1])
                prev_ln = e_bt
                for c in range(NC):
                    pi = c % 2
                    tr = p.run("t", lambda e, ln1tok=ln1tok, c=c, pi=pi: e.transpose(
                        PST[pi][:, 0:128], ln1tok[:, c * 128:(c + 1) * 128], identb[:]),
                        deps=[e_bt, pst_free[pi]])
                    ev_ln1T = p.run("v", lambda e, c=c, tt=tt, pi=pi: e.tensor_copy(
                        ln1T[:, c * TCH + tt * 128: c * TCH + (tt + 1) * 128], PST[pi][:, 0:128]),
                        deps=[tr])
                    pst_free[pi] = ev_ln1T
            for oc in range(32):
                mme = None
                for c in range(NC):
                    mme = p.run("t", lambda e, oc=oc, c=c: e.matmul(
                        PS[oc % 2][:, 0:TCH], bufA[:, c * FF + oc * 128: c * FF + (oc + 1) * 128],
                        ln1T[:, c * TCH:(c + 1) * TCH], start=(c == 0), stop=(c == NC - 1)),
                        deps=([ev_ln1T, ev_hg[oc % 2], evq1] if c == 0 else []),
                        ev=(c == NC - 1))
                ev_hg[oc % 2] = p.run("s", lambda e, oc=oc: e.activation(
                    out=hg[:, oc * TCH:(oc + 1) * TCH], in_=PS[oc % 2][:, 0:TCH], func=AF.Gelu,
                    scale=m_pp[:, 0:1], bias=b1pp[:, oc:oc + 1]), deps=[mme])
            hg_done = [ev_hg[0], ev_hg[1]]
            ev_s1 = None
            for oc in range(32):
                ev_s1 = p.run("t", lambda e, oc=oc: e.matmul(
                    PS[2][0:1, 0:TCH], onesb_c[:, 0:1], hg[:, oc * TCH:(oc + 1) * TCH],
                    start=(oc == 0), stop=(oc == 31)),
                    deps=(hg_done + [ev_stats, prev_AB, ev_oc] if oc == 0 else []),
                    ev=(oc == 31))
            ev_s2 = None
            sq_prev = None
            for oc in range(32):
                sqe = p.run("s", lambda e, oc=oc: e.activation(
                    out=sqb[:], in_=hg[:, oc * TCH:(oc + 1) * TCH], func=AF.Square),
                    deps=(hg_done + [ev_stats, prev_AB] if oc == 0 else [ev_s2]))
                ev_s2 = p.run("t", lambda e, oc=oc: e.matmul(
                    PS[3][0:1, 0:TCH], onesb_c[:, 0:1], sqb[:],
                    start=(oc == 0), stop=(oc == 31)), deps=[sqe])
            c1 = p.run("v", lambda e: e.tensor_scalar(mu2[0:1, :], PS[2][0:1, 0:TCH], 1.0 / FF,
                                                      None, op0=ALU.mult), deps=[ev_s1])
            c2 = p.run("v", lambda e: e.tensor_scalar(var2[0:1, :], PS[3][0:1, 0:TCH], 1.0 / FF,
                                                      None, op0=ALU.mult), deps=[ev_s2])
            c3 = p.run("v", lambda e: e.tensor_tensor(s1row[0:1, :], mu2[0:1, :], mu2[0:1, :],
                                                      op=ALU.mult), deps=[c1])
            c4 = p.run("v", lambda e: e.tensor_tensor(var2[0:1, :], var2[0:1, :], s1row[0:1, :],
                                                      op=ALU.subtract), deps=[c2, c3])
            c5 = p.run("s", lambda e: e.activation(out=s2row[0:1, :], in_=var2[0:1, :],
                                                   func=AF.Sqrt, bias=tiles["eps"][0:1, 0:1]),
                       deps=[c4, tiles.get("eps_ev")])
            c6 = p.run("v", lambda e: e.reciprocal(rstd2[0:1, :], s2row[0:1, :]), deps=[c5])
            c7 = p.run("v", lambda e: e.tensor_scalar(rstd2m[0:1, :], rstd2[0:1, :],
                                                      m2_pp[0:1, 0:1], None, op0=ALU.mult),
                       deps=[c6, evq2])
            c8 = p.run("v", lambda e: e.tensor_tensor(s1row[0:1, :], mu2[0:1, :], rstd2[0:1, :],
                                                      op=ALU.mult), deps=[c6, c3])
            c9 = p.run("v", lambda e: e.tensor_scalar(cvU[0:1, :], s1row[0:1, :], -1.0, None,
                                                      op0=ALU.mult), deps=[c8, ev_ot])
            c10 = p.run("v", lambda e: e.tensor_scalar(cv2[0:2, :], cvU[0:2, :],
                                                       m2_pp[0:2, 0:1], None, op0=ALU.mult),
                        deps=[c9])
            ev_stats = c10
            ev_AB = None
            prevA = None
            prevB = None
            for oc in range(32):
                mA = p.run("t", lambda e, oc=oc: e.matmul(
                    PS[2][:, 0:TCH], ln2gb[0:1, oc * 128:(oc + 1) * 128], rstd2m[0:1, 0:TCH],
                    start=True, stop=True), deps=[c9, c10, ev_l2, prevA])
                mB = p.run("t", lambda e, oc=oc: e.matmul(
                    PS[3][:, 0:TCH], ln2gb[0:2, oc * 128:(oc + 1) * 128], cv2[0:2, 0:TCH],
                    start=True, stop=True), deps=[prevB, c10])
                h1 = p.run("v", lambda e, oc=oc: e.tensor_tensor(
                    hsc[:], hg[:, oc * TCH:(oc + 1) * TCH], PS[2][:, 0:TCH], op=ALU.mult),
                    deps=[mA, ev_AB])
                prevA = h1
                ev_AB = p.run("v", lambda e, oc=oc: e.tensor_tensor(
                    hn[:, oc * TCH:(oc + 1) * TCH], hsc[:], PS[3][:, 0:TCH], op=ALU.add),
                    deps=[h1, mB])
                prevB = ev_AB
            ev_hn = ev_AB
            prev_AB = ev_AB
            for tt in range(2):
                for ocol in range(2):
                    fb = tt
                    for hc in range(32):
                        p.run("t", lambda e, tt=tt, ocol=ocol, hc=hc: e.matmul(
                            PS[4 + tt][:, 0:512],
                            hn[:, hc * TCH + tt * 128: hc * TCH + (tt + 1) * 128],
                            bufB[:, hc * D + ocol * 512: hc * D + (ocol + 1) * 512],
                            start=(hc == 0), stop=False),
                            deps=([ev_hn, fout_free[(tt, ocol)], evq2,
                                   ff2_ps_free.get(tt)] if hc == 0 else []),
                            ev=False)
                    mme = p.run("t", lambda e, ocol=ocol, tt=tt: e.matmul(
                        PS[4 + tt][:, 0:512], onesb_r[0:1, :],
                        b2row[0:1, ocol * 512:(ocol + 1) * 512],
                        start=False, stop=True), deps=[ev_b2r, ev_or])
                    ev_f = p.run("v", lambda e, fb=fb, tt=tt: e.tensor_copy(
                        fout[fb][:], PS[4 + tt][:, 0:512]), deps=[mme, fout_free[(tt, 1 - ocol)]])
                    ff2_ps_free[tt] = ev_f
                    row0 = tc * TCH + tt * 128
                    ac = p.run("g", lambda e, row0=row0, ocol=ocol, fb=fb: e.dma_start(
                        out=x_out[row0:row0 + 128, ocol * 512:(ocol + 1) * 512], in_=fout[fb][:],
                        accum_op=ALU.add),
                        deps=[ev_f, ev_init] + (attn_accums[-4:] if tc == 0 else []), dma=True,
                        slot="f%d" % fb)
                    fout_free[(tt, ocol)] = ac
                    ffn_accums.append(ac)

        if fused is None:
            p.barrier_all("y")
            p.emit()
    return nc


def prep_block3(i, x_shards):
    H = 8
    Vq, Vk, Vv = _f32(i["L3_attn_Vq"]), _f32(i["L3_attn_Vk"]), _f32(i["L3_attn_Vv"])
    Uq, Uk, Uv = _f32(i["L3_attn_Uq"]), _f32(i["L3_attn_Uk"]), _f32(i["L3_attn_Uv"])
    bq, bk, bv = _f32(i["L3_attn_bq"]), _f32(i["L3_attn_bk"]), _f32(i["L3_attn_bv"])
    cat = np.concatenate([Vq.transpose(0, 2, 1), Vk.transpose(0, 2, 1), Vv.transpose(0, 2, 1)],
                         axis=2)
    vqkvT = np.ascontiguousarray(cat.transpose(1, 0, 2).reshape(128, H * 96))
    def _urows(U):
        return U.transpose(0, 2, 1).transpose(1, 0, 2).reshape(32, H * 128)
    uqkvT = np.ascontiguousarray(np.concatenate([_urows(Uq), _urows(Uk), _urows(Uv)], axis=0))
    bqkT = np.ascontiguousarray(np.concatenate([bq.T, bk.T], axis=1))
    bvfl = np.ascontiguousarray(bv.reshape(1, 1024))
    identf = np.eye(128, dtype=np.float32)
    onesf = np.ones((128, 512), dtype=np.float32)
    g1bc = np.ascontiguousarray(np.broadcast_to(_f32(i["L3_ffn_ln1_g"])[None, :], (128, 1024)))
    b1bc = np.ascontiguousarray(np.broadcast_to(_f32(i["L3_ffn_ln1_b"])[None, :], (128, 1024)))
    w1T = np.ascontiguousarray(_f32(i["L3_ffn_w1"]).T)
    b1pp = np.ascontiguousarray(_f32(i["L3_ffn_b1"]).reshape(32, 128).T)
    ln2gb = np.ascontiguousarray(np.stack([_f32(i["L3_ffn_ln2_g"]), _f32(i["L3_ffn_ln2_b"])],
                                          axis=0))
    w2T = np.ascontiguousarray(_f32(i["L3_ffn_w2"]).T)
    b2row = np.ascontiguousarray(_f32(i["L3_ffn_b2"]).reshape(1, 1024))
    common = dict(vqkvT=vqkvT, uqkvT=uqkvT, bqkT=bqkT, bvfl=bvfl, identf=identf, onesf=onesf,
                  g1bc=g1bc, b1bc=b1bc, w1T=w1T, b1pp=b1pp, ln2gb=ln2gb, w2T=w2T, b2row=b2row)
    return [dict(common, x_in=x_shards[c]) for c in range(NCORES)]


def run_block3(i, x):
    shards = _shard_x(x)
    nc = build_block3()
    maps = prep_block3(i, shards)
    res = run_bass_kernel_spmd(nc, maps, core_ids=list(range(NCORES)))
    return _gather_x([r["x_out"] for r in res.results])




def build_block0(with_ffn=True, fused=None):
    nc = (bass.Bass("TRN2", target_bir_lowering=False) if fused is None
          else _NCP(fused[0], fused[2]))
    H, HD = 16, 64
    SCALE = 1.0 / np.sqrt(HD)

    if fused is None:
        x_in = nc.dram_tensor("x_in", [TOK, D], F32, kind="ExternalInput").ap()
        x_out = nc.dram_tensor("x_out", [TOK, D], F32, kind="ExternalOutput").ap()
    else:
        x_in, x_out = fused[3], fused[4]
    xn_hbm = nc.dram_tensor("xn_hbm", [TOK, D], BF16, kind="Internal").ap()

    wqkT_h = nc.dram_tensor("wqkT", [H, 128, 1024], F32, kind="ExternalInput").ap()
    wvT_h = nc.dram_tensor("wvT", [128, 8 * 1024], F32, kind="ExternalInput").ap()
    woutT_h = nc.dram_tensor("woutT", [128, 8 * 1024], F32, kind="ExternalInput").ap()
    boutb_h = nc.dram_tensor("boutb", [128, D], F32, kind="ExternalInput").ap()
    cosT_h = nc.dram_tensor("cosT", [64, S], F32, kind="ExternalInput").ap()
    sinT_h = nc.dram_tensor("sinT", [64, S], F32, kind="ExternalInput").ap()
    identf = nc.dram_tensor("identf", [128, 128], F32, kind="ExternalInput").ap()
    onesf = nc.dram_tensor("onesf", [128, 512], F32, kind="ExternalInput").ap()
    wupT_h = nc.dram_tensor("wupT", [128, 8 * 64], F32, kind="ExternalInput").ap()
    wdnT_h = nc.dram_tensor("wdnT", [128, 32 * 64], F32, kind="ExternalInput").ap()
    iota64_h = nc.dram_tensor("iota64", [128, 64], F32, kind="ExternalInput").ap()
    scud_h = nc.dram_tensor("scud", [1, 2], F32, kind="ExternalInput").ap()
    bup_h = nc.dram_tensor("bup", [128, 4096], F32, kind="ExternalInput").ap()
    bdn_h = nc.dram_tensor("bdn", [128, 1024], F32, kind="ExternalInput").ap()
    ABupT_h = nc.dram_tensor("ABupT", [4096, 64], F32, kind="ExternalInput").ap()
    ABdnT_h = nc.dram_tensor("ABdnT", [8192, 32], F32, kind="ExternalInput").ap()
    selup_h = nc.dram_tensor("selup", [32, 1024], F32, kind="ExternalInput").ap()
    repbu_h = nc.dram_tensor("repbu", [64, 128], F32, kind="ExternalInput").ap()
    seldn_h = nc.dram_tensor("seldn", [64, 4096], F32, kind="ExternalInput").ap()
    repbd_h = nc.dram_tensor("repbd", [128, 128], F32, kind="ExternalInput").ap()
    triu_h = nc.dram_tensor("triuf", [64, 64], F32, kind="ExternalInput").ap()
    aenc_h = nc.dram_tensor("aenc", [128, 64], F32, kind="ExternalInput").ap()
    iotap_h = nc.dram_tensor("iotap", [128, 1], F32, kind="ExternalInput").ap()
    tstart_h = nc.dram_tensor("tstart", [1, 128], F32, kind="ExternalInput").ap()
    sentp_h = nc.dram_tensor("sentp", [128, 256], F32, kind="ExternalInput").ap()
    ABup16 = nc.dram_tensor("ABup16", [4096, 64], BF16, kind="Internal").ap()
    ABdn16 = nc.dram_tensor("ABdn16", [8192, 32], BF16, kind="Internal").ap()
    h_hbm = nc.dram_tensor("h_hbm", [TOK, 4096], BF16, kind="Internal").ap()
    h01_hbm = nc.dram_tensor("h01_hbm", [2 * TOK + 1024, 4096], BF16, kind="Internal").ap()
    o01_hbm = nc.dram_tensor("o01_hbm", [2 * TOK + 1024, 1024], BF16, kind="Internal").ap()
    srt_hbm = nc.dram_tensor("srt_hbm", [8, 2 * TOK], F32, kind="Internal").ap()
    scp_hbm = nc.dram_tensor("scp_hbm", [32768, 2], F32, kind="Internal").ap()

    with ExitStack() as st:
        def sb(name, shape, dt=BF16):
            return st.enter_context(nc.sbuf_tensor(name, shape, dt))

        onesb_r = sb("onesb_r", [1, 128])
        onesb_c = sb("onesb_c", [128, 1])
        onesf_r = sb("onesf_r", [1, 128], F32)
        boutb = sb("boutb_sb", [128, D], F32)
        identb = sb("identb", [128, 128])
        st2 = ExitStack()

        def sb2(name, shape, dt=BF16):
            return st2.enter_context(nc.sbuf_tensor(name, shape, dt))
        xnT = sb2("xnT", [128, NC * S])
        tiles = mk_tiles(st2, nc, with_xpx=True)
        wqk_sb = [sb2("wqk0", [128, 1024]), sb2("wqk1", [128, 1024])]
        wv_sb = sb2("wv_sb", [128, 8 * 1024])
        wout_sb = sb2("wout_sb", [128, 8 * 1024])
        V = sb2("V0", [128, 16 * 1040])
        cosT = sb2("cosT_sb", [64, S])
        sinT = sb2("sinT_sb", [64, S])
        qf = sb2("qf", [64, S])
        kf = qf
        qrot = sb2("qrot", [64, S])
        krot = sb2("krot", [64, S])
        rs1 = sb2("rs1", [32, S // 2])
        rs2 = sb2("rs2", [32, S // 2])
        sqscr0 = sb2("sqscr0", [64, S])
        normrow = sb2("normrow0", [1, 512], F32)
        scm = sb2("scm_0", [1, 1], F32)
        sc1 = sb2("sc1_0", [1, 1], F32)
        sc2 = sb2("sc2_0", [1, 1], F32)
        negb = sb2("negb0", [128, 1], F32)
        PT = [sb2("PT0_0", [128, 512]), sb2("PT1_0", [128, 512])]
        avsb = sb2("avsb", [64, 512])
        rzrow = sb2("rzrow", [1, 512], F32)
        AVn = sb2("AVn", [128, 8 * S])
        at2 = [tiles["xpx"][0], tiles["xpx"][1]]

        PS = [st.enter_context(nc.psum_tensor("b0ps%d" % i, [128, 512], F32)) for i in range(6)]
        PST = [st.enter_context(nc.psum_tensor("b0pst%d" % i, [128, 1024], BF16)) for i in range(2)]

        p = Prog(nc) if fused is None else fused[1]
        tiles["eps_ev"] = p.run("v", lambda e: e.memset(tiles["eps"][:], 1e-5))

        for ld_fn in [
            lambda e: e.dma_start(out=boutb[:], in_=boutb_h),
            lambda e: e.dma_start(out=identb[:], in_=identf),
            lambda e: e.dma_start(out=onesb_r[:], in_=onesf[0:1, 0:128]),
            lambda e: e.dma_start(out=onesb_c[:], in_=onesf[:, 0:1], allow_slow_non_contiguous=True),
            lambda e: e.dma_start(out=onesf_r[:], in_=onesf[0:1, 0:128]),
            lambda e: e.dma_start(out=wv_sb[:], in_=wvT_h),
            lambda e: e.dma_start(out=wout_sb[:], in_=woutT_h),
            lambda e: e.dma_start(out=cosT[:], in_=cosT_h),
            lambda e: e.dma_start(out=sinT[:], in_=sinT_h),
        ]:
            ev_const = p.run("g", ld_fn, dma=True, slot="const")

        Vap = V[:, :].rearrange("p (a s) -> p a s", s=65)
        ev_vones = p.run("v", lambda e: e.memset(Vap[:, :, 64:65], 1.0))

        xev = [None, None]
        prev_lnp = [None]
        tr_done = [None, None]
        init_stores = []

        def prologue_batch(b, dep_extra):
            for tloc in range(16):
                tt = b * 16 + tloc
                bb = tt % 2
                xt = tiles["xt"][bb]
                ld = p.run("y", lambda e, xt=xt, tt=tt: e.dma_start(
                    out=xt[:], in_=x_in[tt * 128:(tt + 1) * 128, :]),
                    deps=[xev[bb]] + dep_extra, dma=True, slot="x%d" % bb)
                rsx = emit_ln_stats(p, tiles, ld, xt, prev_lnp[0])
                xnf = tiles["xnf"][bb]
                lnev = p.run("v", lambda e, xt=xt, xnf=xnf: e.tensor_scalar(
                    xnf[:], xt[:], tiles["mean"][:, 0:1], tiles["rstd"][:, 0:1],
                    op0=ALU.subtract, op1=ALU.mult), deps=[rsx])
                prev_lnp[0] = lnev
                xnb = tiles["xnb"][bb]
                cb = p.run("v", lambda e, xnf=xnf, xnb=xnb: e.tensor_copy(xnb[:], xnf[:]), deps=[lnev])
                xnh_evs.append(p.run("y", lambda e, xnb=xnb, tt=tt: e.dma_start(
                    out=xn_hbm[tt * 128:(tt + 1) * 128, :], in_=xnb[:]), deps=[cb], dma=True,
                    slot="xnh"))
                so = p.run("v", lambda e, xt=xt, bb=bb: e.tensor_add(
                    tiles["xpx"][bb][:], xt[:], boutb[:]), deps=[lnev, ev_const, xev[bb]])
                stv = p.run("y", lambda e, tt=tt, bb=bb: e.dma_start(
                    out=x_out[tt * 128:(tt + 1) * 128, :], in_=tiles["xpx"][bb][:]),
                    deps=[so], dma=True, slot="xo%d" % bb)
                init_stores.append(stv)
                xev[bb] = so
                for c in range(NC):
                    pi = (tloc * NC + c) % 2
                    tr = p.run("t", lambda e, xnb=xnb, c=c, pi=pi: e.transpose(
                        PST[pi][:, 0:128], xnb[:, c * 128:(c + 1) * 128], identb[:]),
                        deps=[cb, tr_done[pi], ev_const])
                    tr_done[pi] = p.run("v", lambda e, tloc=tloc, c=c, pi=pi: e.tensor_copy(
                        xnT[:, c * S + tloc * 128: c * S + (tloc + 1) * 128], PST[pi][:, 0:128]),
                        deps=[tr])
            return tr_done + [prev_lnp[0]]

        attn_accums = []
        xnh_evs = []
        iter_dep = [None]
        at_free = [None, None]
        wqk_ev = [None, None]
        for b in range(NB):
            ln_done = prologue_batch(b, iter_dep)
            if b == 0:
                ev_init = [init_stores[-1], init_stores[-2]]
            at_free = [init_stores[-1], init_stores[-2]]
            ev_v = None
            for kt in range(16):
                for vc in range(2):
                    mme = None
                    for c in range(NC):
                        mme = p.run("t", lambda e, kt=kt, vc=vc, c=c: e.matmul(
                            PS[4][:, 0:512], xnT[:, c * S + kt * 128: c * S + (kt + 1) * 128],
                            wv_sb[:, c * 1024 + vc * 512: c * 1024 + (vc + 1) * 512],
                            start=(c == 0), stop=(c == NC - 1)),
                            deps=(ln_done + [ev_v, ev_const] + iter_dep if c == 0 else []),
                            ev=(c == NC - 1))
                    ev_v = p.run("v", lambda e, kt=kt, vc=vc: e.tensor_copy(
                        Vap[:, kt * 16 + vc * 8:kt * 16 + (vc + 1) * 8, 0:64],
                        PS[4][:, 0:512].rearrange("p (a s) -> p a s", s=64)),
                        deps=[mme, ev_vones])
            for h in range(H):
                hb = h % 2
                wl = p.run("g", lambda e, h=h, hb=hb: e.dma_start(
                    out=wqk_sb[hb][:], in_=wqkT_h[h]), deps=[wqk_ev[hb]], dma=True,
                    slot="wqk%d" % hb)
                ev_q = None
                ev_k = None
                for qc in range(4):
                    mmq = None
                    for c in range(NC):
                        mmq = p.run("t", lambda e, qc=qc, c=c, hb=hb: e.matmul(
                            PS[0][0:64, 0:512], wqk_sb[hb][:, c * 128: c * 128 + 64],
                            xnT[:, c * S + qc * 512: c * S + (qc + 1) * 512],
                            start=(c == 0), stop=(c == NC - 1)),
                            deps=([wl, ev_q] + iter_dep if c == 0 else []), ev=(c == NC - 1))
                    ev_q = p.run("v", lambda e, qc=qc: e.tensor_copy(
                        qf[:, qc * 512:(qc + 1) * 512], PS[0][0:64, 0:512]), deps=[mmq])

                def rope(srcb, dstb, dep):
                    HS = S // 2
                    e6 = dep
                    for hc in range(2):
                        sl = slice(hc * HS, (hc + 1) * HS)
                        e1 = p.run("v", lambda e, sl=sl: e.tensor_tensor(
                            rs1[:], srcb[0:32, sl], cosT[0:32, sl], op=ALU.mult), deps=[e6])
                        e2 = p.run("v", lambda e, sl=sl: e.tensor_tensor(
                            rs2[:], srcb[32:64, sl], sinT[32:64, sl], op=ALU.mult), deps=[e6])
                        e3 = p.run("v", lambda e, sl=sl: e.tensor_tensor(
                            dstb[0:32, sl], rs1[:], rs2[:], op=ALU.subtract), deps=[e1, e2])
                        e4 = p.run("v", lambda e, sl=sl: e.tensor_tensor(
                            rs1[:], srcb[0:32, sl], sinT[0:32, sl], op=ALU.mult), deps=[e3])
                        e5 = p.run("v", lambda e, sl=sl: e.tensor_tensor(
                            rs2[:], srcb[32:64, sl], cosT[32:64, sl], op=ALU.mult), deps=[e3])
                        e6 = p.run("v", lambda e, sl=sl: e.tensor_tensor(
                            dstb[32:64, sl], rs1[:], rs2[:], op=ALU.add), deps=[e4, e5])
                    return e6
                ev_qr = rope(qf, qrot, ev_q)
                for qc in range(4):
                    mmk = None
                    for c in range(NC):
                        mmk = p.run("t", lambda e, qc=qc, c=c, hb=hb: e.matmul(
                            PS[1][0:64, 0:512], wqk_sb[hb][:, c * 128 + 64: c * 128 + 128],
                            xnT[:, c * S + qc * 512: c * S + (qc + 1) * 512],
                            start=(c == 0), stop=(c == NC - 1)),
                            deps=([wl, ev_k] if c == 0 else []), ev=(c == NC - 1))
                    ev_k = p.run("v", lambda e, qc=qc: e.tensor_copy(
                        kf[:, qc * 512:(qc + 1) * 512], PS[1][0:64, 0:512]),
                        deps=[mmk, ev_qr if qc == 0 else None])
                wqk_ev[hb] = ev_k
                ev_kr = rope(kf, krot, ev_k)
                def maxnorm(srcb, dst_sc, dep0, extra):
                    z = p.run("v", lambda e, dst_sc=dst_sc: e.memset(dst_sc[0:1, 0:1], 0.0),
                              deps=[extra])
                    sq = p.run("s", lambda e, srcb=srcb: e.activation(
                        out=sqscr0[:], in_=srcb[:], func=AF.Square), deps=[dep0, extra])
                    evn = extra
                    for qc in range(4):
                        mm = p.run("t", lambda e, qc=qc: e.matmul(
                            PS[3][0:1, 0:512], onesb_c[0:64, 0:1],
                            sqscr0[:, qc * 512:(qc + 1) * 512],
                            start=True, stop=True), deps=[sq, ev_const, evn])
                        cpn = p.run("v", lambda e: e.tensor_copy(normrow[0:1, :], PS[3][0:1, 0:512]),
                                    deps=[mm, z])
                        rm = p.run("v", lambda e: e.reduce_max(scm[0:1, 0:1], normrow[0:1, :],
                                                               axis=AX.X), deps=[cpn])
                        evn = p.run("v", lambda e, dst_sc=dst_sc: e.tensor_tensor(
                            dst_sc[0:1, 0:1], dst_sc[0:1, 0:1], scm[0:1, 0:1], op=ALU.max),
                            deps=[rm])
                    return evn
                mq_ev = maxnorm(qrot, sc1, ev_qr, iter_dep[0])
                mk_ev = maxnorm(krot, sc2, ev_kr, mq_ev)
                ev_nk = mk_ev
                bb2e = p.run("v", lambda e: e.tensor_tensor(sc1[0:1, 0:1], sc1[0:1, 0:1],
                                                            sc2[0:1, 0:1], op=ALU.mult),
                             deps=[mq_ev, mk_ev])
                s1e = p.run("s", lambda e: e.activation(out=sc1[0:1, 0:1], in_=sc1[0:1, 0:1],
                                                        func=AF.Sqrt), deps=[bb2e])
                bb3 = p.run("v", lambda e: e.tensor_scalar(sc1[0:1, 0:1], sc1[0:1, 0:1], -SCALE,
                                                           None, op0=ALU.mult), deps=[s1e])
                mmb = p.run("t", lambda e: e.matmul(PS[3][:, 0:1], onesf_r[0:1, :], sc1[0:1, 0:1],
                                                    start=True, stop=True), deps=[bb3, ev_nk])
                ev_negb = p.run("v", lambda e: e.tensor_copy(negb[:], PS[3][:, 0:1]), deps=[mmb])

                s_free = [ev_negb, None]
                pt_free = [None, None]
                av_free = ev_negb
                for qc in range(4):
                    last_av = None
                    for kt in range(16):
                        pi = kt % 2
                        ms = p.run("t", lambda e, qc=qc, kt=kt, pi=pi: e.matmul(
                            PS[pi][:, 0:512], krot[:, kt * 128:(kt + 1) * 128],
                            qrot[:, qc * 512:(qc + 1) * 512], start=True, stop=True),
                            deps=[ev_qr, ev_kr, s_free[pi]])
                        pe = p.run("s", lambda e, pi=pi: e.activation(
                            out=PT[pi][:], in_=PS[pi][:, 0:512], func=AF.Exp, scale=SCALE,
                            bias=negb[:, 0:1]), deps=[ms, pt_free[pi], ev_negb])
                        s_free[pi] = pe
                        last_av = p.run("t", lambda e, kt=kt, pi=pi, h=h: e.matmul(
                            PS[2][0:65, 0:512],
                            V[:, kt * 1040 + h * 65: kt * 1040 + (h + 1) * 65],
                            PT[pi][:], start=(kt == 0), stop=(kt == 15)),
                            deps=([pe, av_free, ev_v] if kt == 0 else [pe]))
                        pt_free[pi] = last_av
                    rce = p.run("v", lambda e: e.reciprocal(
                        rzrow[0:1, 0:512], PS[2][64:65, 0:512]), deps=[last_av, av_free])
                    cpe = p.run("v", lambda e: e.tensor_copy(
                        avsb[:, 0:512], PS[2][0:64, 0:512]), deps=[rce])
                    mbc = p.run("t", lambda e: e.matmul(
                        PS[3][0:64, 0:512], onesf_r[0:1, 0:64], rzrow[0:1, 0:512],
                        start=True, stop=True), deps=[cpe, ev_negb, av_free])
                    av_free = p.run("v", lambda e, qc=qc, h=h: e.tensor_tensor(
                        AVn[(h % 2) * 64:(h % 2) * 64 + 64,
                            (h // 2) * S + qc * 512:(h // 2) * S + (qc + 1) * 512],
                        avsb[:, 0:512], PS[3][0:64, 0:512], op=ALU.mult),
                        deps=[mbc])
                iter_dep = [av_free, s_free[0], s_free[1]]
            for qt in range(16):
                for oc in range(2):
                    m4 = None
                    for h2 in range(8):
                        m4 = p.run("t", lambda e, qt=qt, oc=oc, h2=h2: e.matmul(
                            PS[4][:, 0:512],
                            AVn[0:64, h2 * S + qt * 128: h2 * S + (qt + 1) * 128],
                            wout_sb[0:64, h2 * 1024 + oc * 512: h2 * 1024 + (oc + 1) * 512],
                            start=(h2 == 0), stop=(h2 == 7)),
                            deps=(iter_dep + [at_free[0]] if h2 == 0 else []), ev=(h2 == 7))
                    m5 = None
                    for h2 in range(8):
                        m5 = p.run("t", lambda e, qt=qt, oc=oc, h2=h2: e.matmul(
                            PS[5][:, 0:512],
                            AVn[64:128, h2 * S + qt * 128: h2 * S + (qt + 1) * 128],
                            wout_sb[64:128, h2 * 1024 + oc * 512: h2 * 1024 + (oc + 1) * 512],
                            start=(h2 == 0), stop=(h2 == 7)),
                            deps=([at_free[1]] if h2 == 0 else []), ev=(h2 == 7))
                    c4 = p.run("v", lambda e: e.tensor_copy(at2[0][:, 0:512], PS[4][:, 0:512]),
                               deps=[m4])
                    c5 = p.run("v", lambda e: e.tensor_tensor(at2[1][:, 0:512], at2[0][:, 0:512],
                                                              PS[5][:, 0:512], op=ALU.add),
                               deps=[m5, c4])
                    at_free[0] = c5
                    row0 = b * S + qt * 128
                    ac = p.run("g", lambda e, row0=row0, oc=oc: e.dma_start(
                        out=x_out[row0:row0 + 128, oc * 512:(oc + 1) * 512], in_=at2[1][:, 0:512],
                        accum_op=ALU.add), deps=[c5] + ev_init, dma=True, slot="ao")
                    at_free[1] = ac
                    attn_accums.append(ac)
            iter_dep = iter_dep + [at_free[1]]

        st2.close()
        if with_ffn:
            NTL = 128
            SENT = 12000.0
            stf = ExitStack()

            def sbf(name, shape, dt=BF16):
                return stf.enter_context(nc.sbuf_tensor(name, shape, dt))
            wupT = sbf("wupT_sb", [128, 8 * 64])
            wdnT = sbf("wdnT_sb", [128, 32 * 64])
            logitsb = sbf("logitsb", [128, 64], F32)
            m8k = sbf("m8k", [128, 8], F32)
            eqs0 = sbf("eqs0", [128, 64], F32)
            eqs1 = sbf("eqs1", [128, 64], F32)
            prb = sbf("prb", [128, 4], F32)
            idxf0 = sbf("idxf0", [128, 2], F32)
            iota64 = sbf("iota64_sb", [128, 64], F32)
            scup = sbf("scupk", [128, 2], F32)
            bupb = sbf("bupbk", [128, 4096], F32)
            bdnb = sbf("bdnbk", [128, 1024], F32)
            identfb = sbf("identfb", [128, 128], F32)
            selup_sb = sbf("selup_sb", [32, 1024])
            repbu_sb = sbf("repbu_sb", [64, 128])
            seldn_sb = sbf("seldn_sb", [64, 4096])
            repbd_sb = sbf("repbd_sb", [128, 128])
            triu_sb = sbf("triu_sb", [64, 64])
            aenc_sb = sbf("aenc_sb", [128, 64], F32)
            iotap_sb = sbf("iotap_sb", [128, 1], F32)
            tstart_sb = sbf("tstart_sb", [1, 128], F32)
            sentp_sb = sbf("sentp_sb", [128, 256], F32)
            onescf = sbf("onescf", [64, 1], F32)
            ET = sbf("ET_sb", [128, 128], F32)
            xnti = [sbf("xntik%d" % k, [128, 1024]) for k in range(2)]
            hTt = sbf("hTtk", [128, 32 * 128])
            tmpr = sbf("tmpr", [1, 128], F32)
            rowA = sbf("rowA", [1, 512], F32)
            rowB = sbf("rowB", [1, 512], F32)
            PL = [sbf("PLp%d" % k, [128, 2], F32) for k in range(2)]
            TIK = [stf.enter_context(nc.sbuf_tensor("TIKp%d" % k, [128, 1], I32))
                   for k in range(2)]
            SIK = [stf.enter_context(nc.sbuf_tensor("SIKp%d" % k, [128, 1], I32))
                   for k in range(2)]
            ABI = [stf.enter_context(nc.sbuf_tensor("ABIp%d" % k, [128, 1], I32))
                   for k in range(2)]
            idsc = [sbf("idscp%d" % k, [128, 4], F32) for k in range(2)]
            tot1 = sbf("tot1", [64, 1], F32)
            totg = sbf("totg", [64, 1], F32)
            pbase = sbf("pbase", [64, 1], F32)
            pendb = sbf("pendb", [64, 1], F32)
            pc128 = sbf("pc128", [64, 1], F32)
            pci = stf.enter_context(nc.sbuf_tensor("pci", [64, 1], I32))
            pc16 = sbf("pc16", [64, 1])
            scrE = sbf("scrE", [64, 128], F32)
            scrE2 = sbf("scrE2", [64, 128], F32)
            scrC = sbf("scrC", [64, 512], F32)
            PAY = sbf("PAYs", [128, 128], F32)
            CIDF = sbf("CIDF", [128, 64], F32)
            CIDX = stf.enter_context(nc.sbuf_tensor("CIDX", [128, 64], I32))
            PRL = sbf("PRLs", [128, 64], F32)

            ev_cf = None
            for ld_fn in [
                lambda e: e.dma_start(out=wupT[:], in_=wupT_h),
                lambda e: e.dma_start(out=wdnT[:], in_=wdnT_h),
                lambda e: e.dma_start(out=iota64[:], in_=iota64_h),
                lambda e: e.dma_start(out=scup[0:1, 0:2], in_=scud_h),
                lambda e: e.dma_start(out=bupb[:], in_=bup_h),
                lambda e: e.dma_start(out=bdnb[:], in_=bdn_h),
                lambda e: e.dma_start(out=identfb[:], in_=identf),
                lambda e: e.dma_start(out=selup_sb[:], in_=selup_h),
                lambda e: e.dma_start(out=repbu_sb[:], in_=repbu_h),
                lambda e: e.dma_start(out=seldn_sb[:], in_=seldn_h),
                lambda e: e.dma_start(out=repbd_sb[:], in_=repbd_h),
                lambda e: e.dma_start(out=triu_sb[:], in_=triu_h),
                lambda e: e.dma_start(out=aenc_sb[:], in_=aenc_h),
                lambda e: e.dma_start(out=iotap_sb[:], in_=iotap_h),
                lambda e: e.dma_start(out=tstart_sb[:], in_=tstart_h),
                lambda e: e.dma_start(out=sentp_sb[:], in_=sentp_h),
                lambda e: e.dma_start(out=onescf[:], in_=onesf[0:64, 0:1],
                                      allow_slow_non_contiguous=True),
            ]:
                ev_cf = p.run("g", ld_fn, deps=iter_dep, dma=True, slot="constf")
            mmsc = p.run("t", lambda e: e.matmul(PS[3][:, 0:2], onesf_r[0:1, :],
                                                 scup[0:1, 0:2], start=True, stop=True),
                         deps=[ev_cf] + iter_dep)
            ev_sc = p.run("v", lambda e: e.tensor_copy(scup[:, 0:2], PS[3][:, 0:2]),
                          deps=[mmsc])
            xnh_last = xnh_evs[-1]

            def top2idx(logit_ev, dep_extra):
                mx = p.run("v", lambda e: e.max(out=m8k[:], in_=logitsb[:]),
                           deps=[logit_ev] + dep_extra)
                d1 = p.run("v", lambda e: e.tensor_tensor(prb[:, 2:3], m8k[:, 1:2],
                                                          m8k[:, 0:1], op=ALU.subtract),
                           deps=[mx])
                d2 = p.run("s", lambda e: e.activation(out=prb[:, 3:4], in_=prb[:, 2:3],
                                                       func=AF.Exp), deps=[d1])
                d3 = p.run("v", lambda e: e.tensor_scalar(prb[:, 2:3], prb[:, 3:4], 1.0,
                                                          None, op0=ALU.add), deps=[d2])
                d4 = p.run("v", lambda e: e.reciprocal(prb[:, 0:1], prb[:, 2:3]), deps=[d3])
                d5 = p.run("v", lambda e: e.tensor_tensor(prb[:, 1:2], prb[:, 3:4],
                                                          prb[:, 0:1], op=ALU.mult),
                           deps=[d4])
                e0 = p.run("v", lambda e: e.tensor_scalar(eqs0[:], logitsb[:], m8k[:, 0:1],
                                                          None, op0=ALU.is_ge), deps=[mx])
                e0b = p.run("v", lambda e: e.tensor_tensor(eqs1[:], eqs0[:], iota64[:],
                                                           op=ALU.mult), deps=[e0, ev_cf])
                e0c = p.run("v", lambda e: e.reduce_max(idxf0[:, 0:1], eqs1[:], axis=AX.X),
                            deps=[e0b])
                e1 = p.run("v", lambda e: e.tensor_scalar(eqs1[:], logitsb[:], m8k[:, 1:2],
                                                          None, op0=ALU.is_ge), deps=[e0b])
                e1b = p.run("v", lambda e: e.tensor_tensor(eqs1[:], eqs1[:], eqs0[:],
                                                           op=ALU.subtract), deps=[e1])
                e1c = p.run("v", lambda e: e.tensor_tensor(eqs1[:], eqs1[:], iota64[:],
                                                           op=ALU.mult), deps=[e1b])
                e1d = p.run("v", lambda e: e.reduce_max(idxf0[:, 1:2], eqs1[:], axis=AX.X),
                            deps=[e1c])
                return [e0c, e1d, d5]

            def collect_tile(tt, OH, top_evs, rb):
                evs = []
                prev = top_evs
                for kk, (srcb, col, aoff) in enumerate([
                        (idxf0, 0, 0), (idxf0, 1, TOK), (prb, 0, 0), (prb, 1, TOK)]):
                    tr = p.run("t", lambda e, srcb=srcb, col=col: e.transpose(
                        PS[5][0:1, 0:128], srcb[:, col:col + 1], identfb[:]),
                        deps=prev + [ev_cf])
                    cp = p.run("v", lambda e: e.tensor_copy(tmpr[:], PS[5][0:1, 0:128]),
                               deps=[tr])
                    if kk < 2:
                        mm = p.run("t", lambda e: e.matmul(
                            PS[4][0:64, 0:128], onesf_r[0:1, 0:64], tmpr[:],
                            start=True, stop=True), deps=[cp])
                        oh = p.run("v", lambda e, aoff=aoff, tt=tt: e.tensor_tensor(
                            OH[:, aoff + tt * 128: aoff + (tt + 1) * 128],
                            PS[4][0:64, 0:128],
                            iotap_sb[0:64, 0:1].to_broadcast([64, 128]),
                            op=ALU.is_equal), deps=[mm])
                        evs.append(oh)
                        prev = [oh, mm]
                    else:
                        dw = p.run("y", lambda e, aoff=aoff, tt=tt, rb=rb: e.dma_start(
                            out=srt_hbm[rb + 2:rb + 3,
                                        aoff + tt * 128: aoff + (tt + 1) * 128],
                            in_=tmpr[:]), deps=[cp], dma=True, slot="pw")
                        evs.append(dw)
                        prev = [dw]
                return evs

            def sort_stage(OH, SCA, SCB, oh_evs, prob_evs, rb, so, et_free):
                HL = TOK
                ev_cnt = []
                ev_t1 = None
                for half in range(2):
                    sl = slice(half * HL, (half + 1) * HL)
                    ev = p.run("v", lambda e, sl=sl: e.tensor_copy(SCA[:], OH[:, sl]),
                               deps=oh_evs + [ev_t1] +
                               ([ev_cnt[-1]] if ev_cnt else []))
                    cur, nxt = SCA, SCB
                    for lv in range(12):
                        s_ = 1 << lv
                        ev = p.run("v", lambda e, cur=cur, nxt=nxt, s_=s_: e.tensor_tensor(
                            nxt[:, s_:HL], cur[:, s_:HL], cur[:, 0:HL - s_], op=ALU.add),
                            deps=[ev])
                        ev = p.run("v", lambda e, cur=cur, nxt=nxt, s_=s_: e.tensor_copy(
                            nxt[:, 0:s_], cur[:, 0:s_]), deps=[ev])
                        cur, nxt = nxt, cur
                    if half == 0:
                        ev_t1 = p.run("v", lambda e: e.tensor_copy(
                            tot1[:], SCA[:, HL - 1:HL]), deps=[ev])
                    else:
                        ev = p.run("v", lambda e: e.tensor_scalar(
                            SCA[:], SCA[:], tot1[:, 0:1], None, op0=ALU.add),
                            deps=[ev, ev_t1])
                        ev_t1 = p.run("v", lambda e: e.tensor_copy(
                            totg[:], SCA[:, HL - 1:HL]), deps=[ev])
                    evp = p.run("v", lambda e, sl=sl: e.tensor_copy(SCB[:], OH[:, sl]),
                                deps=[ev, ev_t1])
                    evp = p.run("v", lambda e: e.tensor_tensor(SCB[:], SCB[:], SCA[:],
                                                               op=ALU.mult), deps=[evp])
                    dw = None
                    for ch in range(8):
                        mm = p.run("t", lambda e, ch=ch: e.matmul(
                            PS[5][0:1, 0:512], onescf[:], SCB[:, ch * 512:(ch + 1) * 512],
                            start=True, stop=True), deps=[evp, dw, ev_cf])
                        cp = p.run("v", lambda e: e.tensor_copy(rowA[:], PS[5][0:1, 0:512]),
                                   deps=[mm])
                        dw = p.run("y", lambda e, half=half, ch=ch, rb=rb: e.dma_start(
                            out=srt_hbm[rb:rb + 1,
                                        half * HL + ch * 512: half * HL + (ch + 1) * 512],
                            in_=rowA[:]), deps=[cp], dma=True, slot="cw")
                        ev_cnt.append(dw)
                q1 = p.run("v", lambda e: e.tensor_scalar(pc128[:], totg[:], 127.0, None,
                                                          op0=ALU.add), deps=[ev_t1])
                q2 = p.run("v", lambda e: e.tensor_scalar(pc128[:], pc128[:], 1.0 / 128.0,
                                                          None, op0=ALU.mult), deps=[q1])
                q3 = p.run("v", lambda e: e.tensor_copy(pci[:], pc128[:]), deps=[q2])
                q4 = p.run("v", lambda e: e.tensor_copy(pc128[:], pci[:]), deps=[q3])
                q5 = p.run("v", lambda e: e.tensor_copy(pc16[:], pc128[:]), deps=[q4])
                mmq = p.run("t", lambda e: e.matmul(PS[5][0:64, 0:1], triu_sb[:], pc16[:],
                                                    start=True, stop=True),
                            deps=[q5, ev_cf])
                q6 = p.run("v", lambda e: e.tensor_scalar(pbase[:], PS[5][0:64, 0:1], 128.0,
                                                          None, op0=ALU.mult), deps=[mmq])
                q7 = p.run("v", lambda e: e.tensor_scalar(pendb[:], pc128[:], 128.0, None,
                                                          op0=ALU.mult), deps=[q4, q6])
                q8 = p.run("v", lambda e: e.tensor_tensor(pendb[:], pendb[:], pbase[:],
                                                          op=ALU.add), deps=[q7])
                mmt = p.run("t", lambda e: e.matmul(PS[4][0:64, 0:128], onesf_r[0:1, 0:64],
                                                    tstart_sb[:], start=True, stop=True),
                            deps=[q8, ev_cf])
                r1 = p.run("v", lambda e: e.tensor_scalar(scrE[:], PS[4][0:64, 0:128],
                                                          pbase[:, 0:1], None,
                                                          op0=ALU.subtract), deps=[mmt, q6])
                r2 = p.run("v", lambda e: e.tensor_scalar(scrE2[:], scrE[:], 0.0, None,
                                                          op0=ALU.is_ge), deps=[r1])
                r3 = p.run("v", lambda e: e.tensor_scalar(scrE[:], PS[4][0:64, 0:128],
                                                          pendb[:, 0:1], None,
                                                          op0=ALU.subtract), deps=[r2, q8])
                r4 = p.run("v", lambda e: e.tensor_scalar(scrE[:], scrE[:], 0.0, None,
                                                          op0=ALU.is_lt), deps=[r3])
                r5 = p.run("v", lambda e: e.tensor_tensor(scrE[:], scrE[:], scrE2[:],
                                                          op=ALU.mult), deps=[r4])
                r6 = p.run("v", lambda e: e.tensor_scalar(scrE[:], scrE[:],
                                                          iotap_sb[0:64, 0:1], None,
                                                          op0=ALU.mult), deps=[r5])
                mmr = p.run("t", lambda e: e.matmul(PS[4][0:1, 0:128], onescf[:], scrE[:],
                                                    start=True, stop=True), deps=[r6])
                r7 = p.run("v", lambda e: e.tensor_copy(tmpr[:], PS[4][0:1, 0:128]),
                           deps=[mmr])
                mrr = p.run("t", lambda e: e.matmul(PS[4][:, 0:128], onesf_r[0:1, :],
                                                    tmpr[:], start=True, stop=True),
                            deps=[r7])
                ev_et = p.run("v", lambda e: e.tensor_copy(ET[:], PS[4][:, 0:128]),
                              deps=[mrr] + et_free)
                ev_cpr = []
                dw = None
                lr = None
                for ch in range(16):
                    c1 = p.run("v", lambda e, ch=ch: e.tensor_copy(
                        scrC[:], OH[:, ch * 512:(ch + 1) * 512]), deps=[q6, dw] + oh_evs)
                    c2 = p.run("v", lambda e: e.tensor_scalar(scrC[:], scrC[:],
                                                              pbase[:, 0:1], None,
                                                              op0=ALU.mult), deps=[c1])
                    mm = p.run("t", lambda e: e.matmul(PS[5][0:1, 0:512], onescf[:],
                                                       scrC[:], start=True, stop=True),
                               deps=[c2])
                    lr = p.run("y", lambda e, ch=ch, rb=rb: e.dma_start(
                        out=rowA[:], in_=srt_hbm[rb:rb + 1, ch * 512:(ch + 1) * 512]),
                        deps=ev_cnt + [lr, dw], dma=True, slot="cr")
                    a1 = p.run("v", lambda e: e.tensor_tensor(rowB[:], rowA[:],
                                                              PS[5][0:1, 0:512],
                                                              op=ALU.add), deps=[mm, lr])
                    a2 = p.run("v", lambda e, so=so: e.tensor_scalar(
                        rowB[:], rowB[:], float(so - 1), None, op0=ALU.add), deps=[a1])
                    dw = p.run("y", lambda e, ch=ch, rb=rb: e.dma_start(
                        out=srt_hbm[rb + 1:rb + 2, ch * 512:(ch + 1) * 512], in_=rowB[:]),
                        deps=[a2], dma=True, slot="cw2")
                    ev_cpr.append(dw)
                ld1 = p.run("y", lambda e, rb=rb: e.dma_start(
                    out=CIDF[:],
                    in_=srt_hbm[rb + 1:rb + 2, :].rearrange("o (p c) -> (o p) c", p=128)),
                    deps=ev_cpr, dma=True, slot="rl1")
                ld2 = p.run("y", lambda e, rb=rb: e.dma_start(
                    out=PRL[:],
                    in_=srt_hbm[rb + 2:rb + 3, :].rearrange("o (p c) -> (o p) c", p=128)),
                    deps=prob_evs, dma=True, slot="rl2")
                ci = p.run("v", lambda e: e.tensor_copy(CIDX[:], CIDF[:]), deps=[ld1])
                pa1 = p.run("v", lambda e: e.tensor_copy(
                    PAY[:].rearrange("p (k two) -> p k two", two=2)[:, :, 0:1],
                    aenc_sb[:].rearrange("p (k two) -> p k two", two=1)), deps=[ev_cf])
                pa2 = p.run("v", lambda e: e.tensor_copy(
                    PAY[:].rearrange("p (k two) -> p k two", two=2)[:, :, 1:2],
                    PRL[:].rearrange("p (k two) -> p k two", two=1)), deps=[ld2, pa1])
                ms = p.run("g", lambda e, so=so: e.dma_start(
                    out=scp_hbm[so:so + 16384, :].rearrange("(p k) c -> p (k c)", p=128),
                    in_=sentp_sb[:]), deps=[ev_cf], dma=True, slot="sm")
                sc_last = [None] * 4
                for c in range(64):
                    sv = p.run("g", lambda e, c=c: e.indirect_dma_start(
                        out=scp_hbm, out_offset=IndirectOffsetOnAxis(CIDX[:, c:c + 1], 0),
                        in_=PAY[:, 2 * c:2 * c + 2], in_offset=None),
                        deps=[ci, pa2, ms, sc_last[c % 4]], dma=True, slot="sc%d" % (c % 4))
                    sc_last[c % 4] = sv
                return sc_last, ev_et

            def moe_tiles(tag, NCHK, ODIM, gsrc_hbm, out_hbm, so,
                          AB16, abrows, sel_sb, rep_sb, rep_base, arep_w,
                          sc_last, ev_et, gdep_extra):
                stt = ExitStack()
                IDIM = NCHK * 128
                NOC = ODIM // 512
                NG = max(1, NOC // 4)
                GOC = NOC // NG
                Ksb = stt.enter_context(nc.sbuf_tensor(tag + "K", [128, NCHK * ODIM], BF16))
                Gb = [stt.enter_context(nc.sbuf_tensor(tag + "G%d" % k, [128, IDIM], BF16))
                      for k in range(2)]
                GTb = [stt.enter_context(nc.sbuf_tensor(tag + "GT%d" % k, [128, IDIM],
                                                        BF16)) for k in range(2)]
                Yb = [stt.enter_context(nc.sbuf_tensor(tag + "Y%d" % k, [128, ODIM], BF16))
                      for k in range(2)]
                ABg = [stt.enter_context(nc.sbuf_tensor(tag + "AB%d" % k,
                                                        [128, arep_w], BF16))
                       for k in range(2)]
                arep = [stt.enter_context(nc.sbuf_tensor(tag + "ar%d" % k,
                                                         [128, NCHK * arep_w], BF16))
                        for k in range(2)]
                brep = [stt.enter_context(nc.sbuf_tensor(tag + "br%d" % k, [128, arep_w],
                                                         BF16)) for k in range(2)]
                ev_pl = [None, None]
                ev_g = [None, None]
                ev_gt = [None, None]
                ev_ab = [None, None]
                ev_y = [None, None]
                ev_K = [None] * NCHK
                ev_kw = [None] * NCHK
                ev_kwlast = [None, None]
                ev_psA = None
                ev_psB = None
                ev_psY = {}
                ev_trp = [None, None]
                sb_last = [None] * 8
                ev_a1last = [None]
                nhalf = (NCHK + 15) // 16
                for t in range(NTL):
                    pi = t % 2
                    dpl = p.run("y", lambda e, t=t, pi=pi: e.dma_start(
                        out=PL[pi][:], in_=scp_hbm[so + t * 128:so + (t + 1) * 128, :]),
                        deps=sc_last + [ev_pl[pi]], dma=True, slot="pl%d" % pi)
                    i1 = p.run("v", lambda e, pi=pi: e.tensor_scalar(
                        idsc[pi][:, 0:1], PL[pi][:, 0:1], 4096.0, None, op0=ALU.is_ge),
                        deps=[dpl])
                    i2 = p.run("v", lambda e, pi=pi: e.tensor_scalar(
                        idsc[pi][:, 1:2], idsc[pi][:, 0:1], -4096.0, None, op0=ALU.mult),
                        deps=[i1])
                    i3 = p.run("v", lambda e, pi=pi: e.tensor_tensor(
                        idsc[pi][:, 2:3], PL[pi][:, 0:1], idsc[pi][:, 1:2], op=ALU.add),
                        deps=[i2])
                    i4 = p.run("v", lambda e, pi=pi: e.tensor_scalar(
                        idsc[pi][:, 2:3], idsc[pi][:, 2:3], 4095.0, None, op0=ALU.min),
                        deps=[i3])
                    i5 = p.run("v", lambda e, pi=pi: e.tensor_copy(
                        TIK[pi][:], idsc[pi][:, 2:3]), deps=[i4])
                    i6 = p.run("v", lambda e, pi=pi: e.tensor_copy(
                        SIK[pi][:], PL[pi][:, 0:1]), deps=[i5])
                    a1 = p.run("v", lambda e, t=t, pi=pi, abrows=abrows: e.tensor_scalar(
                        idsc[pi][0:64, 3:4], ET[0:64, t:t + 1], float(abrows), None,
                        op0=ALU.mult), deps=[i6, ev_et, ev_ab[pi]])
                    a2 = p.run("v", lambda e, pi=pi: e.tensor_tensor(
                        idsc[pi][0:64, 3:4], idsc[pi][0:64, 3:4], iotap_sb[0:64, 0:1],
                        op=ALU.add), deps=[a1])
                    if abrows == 128:
                        a2b = p.run("v", lambda e, t=t, pi=pi: e.tensor_scalar(
                            idsc[pi][64:128, 3:4], ET[64:128, t:t + 1], 128.0, None,
                            op0=ALU.mult), deps=[a2])
                        a2c = p.run("v", lambda e, pi=pi: e.tensor_tensor(
                            idsc[pi][64:128, 3:4], idsc[pi][64:128, 3:4],
                            iotap_sb[64:128, 0:1], op=ALU.add), deps=[a2b])
                        a3 = p.run("v", lambda e, pi=pi: e.tensor_copy(
                            ABI[pi][:], idsc[pi][:, 3:4]), deps=[a2c])
                        nab = 128
                    else:
                        a3 = p.run("v", lambda e, pi=pi: e.tensor_copy(
                            ABI[pi][0:64, :], idsc[pi][0:64, 3:4]), deps=[a2])
                        nab = 64
                    ev_a1last[0] = a1
                    dab = p.run("g", lambda e, pi=pi, nab=nab, AB16=AB16:
                                e.indirect_dma_start(
                        out=ABg[pi][0:nab, :], out_offset=None, in_=AB16,
                        in_offset=IndirectOffsetOnAxis(ABI[pi][0:nab, 0:1], 0)),
                        deps=[a3, ev_ab[pi], ev_ab16], dma=True, slot="gab%d" % pi)
                    dg = p.run("g", lambda e, pi=pi, gsrc_hbm=gsrc_hbm:
                               e.indirect_dma_start(
                        out=Gb[pi][:], out_offset=None, in_=gsrc_hbm,
                        in_offset=IndirectOffsetOnAxis(TIK[pi][:, 0:1], 0)),
                        deps=[i5, ev_g[pi]] + gdep_extra, dma=True, slot="gg%d" % (t % 4))
                    fold = p.run("v", lambda e, pi=pi: e.tensor_scalar(
                        Gb[pi][:], Gb[pi][:], PL[pi][:, 1:2], None, op0=ALU.mult),
                        deps=[dg, i6])
                    ev_pl[pi] = fold
                    evt = None
                    for c in range(NCHK):
                        cb = c % 2
                        tr = p.run("t", lambda e, c=c, pi=pi, cb=cb: e.transpose(
                            PST[cb][:, 0:128], Gb[pi][:, c * 128:(c + 1) * 128],
                            identb[:]), deps=[fold, ev_trp[cb]])
                        evt = p.run("s", lambda e, c=c, pi=pi, cb=cb: e.activation(
                            out=GTb[pi][:, c * 128:(c + 1) * 128], in_=PST[cb][:, 0:128],
                            func=AF.Copy),
                            deps=[tr] + ([ev_gt[pi]] if c == 0 else []))
                        ev_trp[cb] = evt
                    ev_g[pi] = evt
                    aev = []
                    for hh in range(nhalf):
                        c0, c1_ = hh * 16, min((hh + 1) * 16, NCHK)
                        mmv = None
                        for c in range(c0, c1_):
                            mmv = p.run("t", lambda e, c=c, c0=c0, pi=pi: e.matmul(
                                PS[4][:, (c - c0) * arep_w:(c - c0 + 1) * arep_w],
                                sel_sb[:, c * 128:(c + 1) * 128],
                                ABg[pi][0:abrows // 2, :],
                                start=True, stop=True),
                                deps=([dab, ev_psA, ev_cf] if c == c0 else []),
                                ev=(c == c1_ - 1))
                        ev_ae = p.run("s", lambda e, hh=hh, pi=pi, c0=c0, c1_=c1_:
                                      e.activation(
                            out=arep[pi][:, c0 * arep_w:c1_ * arep_w],
                            in_=PS[4][:, 0:(c1_ - c0) * arep_w], func=AF.Copy),
                            deps=[mmv, ev_kwlast[pi]])
                        aev.append(ev_ae)
                        ev_psA = ev_ae
                    mmb2 = p.run("t", lambda e, pi=pi: e.matmul(
                        PS[5][:, 0:arep_w], rep_sb[rep_base:abrows, :],
                        ABg[pi][abrows // 2:abrows, :],
                        start=True, stop=True), deps=[dab, ev_psB, ev_cf])
                    ev_be = p.run("s", lambda e, pi=pi: e.activation(
                        out=brep[pi][:], in_=PS[5][:, 0:arep_w], func=AF.Copy),
                        deps=[mmb2, ev_kwlast[pi]])
                    ev_psB = ev_be
                    ev_ab[pi] = ev_be
                    for c in range(NCHK):
                        hh = c // 16
                        ev_kw[c] = p.run("v", lambda e, c=c, pi=pi: e.tensor_tensor(
                            Ksb[:, c * ODIM:(c + 1) * ODIM]
                            .rearrange("p (o q) -> p o q", q=arep_w),
                            arep[pi][:, c * arep_w:(c + 1) * arep_w]
                            .rearrange("p (o u) -> p o u", u=1)
                            .to_broadcast([128, ODIM // arep_w, arep_w]),
                            brep[pi][:, 0:arep_w]
                            .rearrange("p (u q) -> p u q", u=1)
                            .to_broadcast([128, ODIM // arep_w, arep_w]),
                            op=ALU.mult), deps=[aev[hh], ev_be, ev_K[c]])
                    ev_kwlast[pi] = ev_kw[NCHK - 1]
                    ev_yev = None
                    for g in range(NG):
                        for c in range(NCHK):
                            for oi in range(GOC):
                                oc = g * GOC + oi
                                dd = [ev_kw[c]]
                                if c == 0:
                                    dd += [ev_g[pi], ev_psY.get((g, oi))]
                                last_mm = p.run(
                                    "t", lambda e, c=c, oc=oc, oi=oi, pi=pi: e.matmul(
                                        PS[oi][:, 0:512],
                                        GTb[pi][:, c * 128:(c + 1) * 128],
                                        Ksb[:, c * ODIM + oc * 512:
                                            c * ODIM + (oc + 1) * 512],
                                        start=(c == 0), stop=(c == NCHK - 1)),
                                    deps=dd,
                                    ev=(c == NCHK - 1) or
                                       (g == NG - 1 and oi == GOC - 1))
                                if g == NG - 1 and oi == GOC - 1:
                                    ev_K[c] = last_mm
                                if c == NCHK - 1:
                                    ye = p.run("s", lambda e, oc=oc, oi=oi, pi=pi:
                                               e.activation(
                                        out=Yb[pi][:, oc * 512:(oc + 1) * 512],
                                        in_=PS[oi][:, 0:512], func=AF.Copy),
                                        deps=[last_mm] +
                                             ([ev_y[pi]] if oc == 0 else []))
                                    ev_psY[((g + 1) % NG, oi)] = ye
                                    ev_yev = ye
                    dsc = p.run("g", lambda e, pi=pi, out_hbm=out_hbm:
                                e.indirect_dma_start(
                        out=out_hbm, out_offset=IndirectOffsetOnAxis(SIK[pi][:, 0:1], 0),
                        in_=Yb[pi][:], in_offset=None),
                        deps=[ev_yev, i6, sb_last[t % 8]], dma=True,
                        slot="sb%d" % (t % 8))
                    sb_last[t % 8] = dsc
                    ev_y[pi] = dsc
                    ev_gt[pi] = dsc
                stt.close()
                return sb_last, ev_a1last[0]

            ffn0_accums = []
            st_r1 = ExitStack()
            OH = st_r1.enter_context(nc.sbuf_tensor("OHs", [64, 2 * TOK], BF16))
            SCA = st_r1.enter_context(nc.sbuf_tensor("SCAs", [64, TOK], F32))
            SCB = st_r1.enter_context(nc.sbuf_tensor("SCBs", [64, TOK], F32))
            abscr = st_r1.enter_context(nc.sbuf_tensor("abscr", [128, 2048], F32))
            cvt = ev_cf
            for (srcT, dstT, cc_) in [(ABupT_h, ABup16, 64), (ABdnT_h, ABdn16, 32)]:
                c1 = p.run("g", lambda e, srcT=srcT, cc_=cc_: e.dma_start(
                    out=abscr[:].rearrange("p (k c) -> p k c", c=cc_),
                    in_=srcT.rearrange("(k p) c -> p k c", p=128)),
                    deps=[cvt] + iter_dep, dma=True, slot="cvf")
                cvt = p.run("g", lambda e, dstT=dstT, cc_=cc_: e.dma_start(
                    out=dstT.rearrange("(k p) c -> p k c", p=128),
                    in_=abscr[:].rearrange("p (k c) -> p k c", c=cc_)),
                    deps=[c1], dma=True, slot="cwf")
            ev_ab16 = cvt

            oh_evs = []
            prob_evs = []
            ev_xn = [None, None]
            ev_coll = []
            ev_tr = None
            for tt in range(NT):
                pi = tt % 2
                t0 = tt * 128
                lxn = p.run("y", lambda e, t0=t0, pi=pi: e.dma_start(
                    out=xnti[pi][:], in_=xn_hbm[t0:t0 + 128, :]),
                    deps=[ev_xn[pi], xnh_last, ln_done[-1]] + iter_dep, dma=True,
                    slot="fxr%d" % pi)
                for c in range(NC):
                    tr = p.run("t", lambda e, c=c, pi=pi: e.transpose(
                        PST[c % 2][:, 0:128], xnti[pi][:, c * 128:(c + 1) * 128],
                        identb[:]), deps=[lxn, ev_tr] + iter_dep)
                    ev_tr = p.run("v", lambda e, c=c: e.tensor_copy(
                        hTt[:, c * 128:(c + 1) * 128], PST[c % 2][:, 0:128]), deps=[tr])
                ev_xn[pi] = ev_tr
                mml = None
                for c in range(NC):
                    mml = p.run("t", lambda e, c=c: e.matmul(
                        PS[4][0:128, 0:64], hTt[:, c * 128:(c + 1) * 128],
                        wupT[:, c * 64:(c + 1) * 64], start=(c == 0), stop=(c == NC - 1)),
                        deps=([ev_tr, ev_cf] + iter_dep + ev_coll if c == 0 else []),
                        ev=(c == NC - 1))
                lge = p.run("v", lambda e: e.tensor_copy(logitsb[:], PS[4][0:128, 0:64]),
                            deps=[mml])
                tevs = top2idx(lge, ev_coll)
                ev_coll = collect_tile(tt, OH, tevs, 0)
                oh_evs += [ev_coll[0], ev_coll[1]]
                prob_evs += [ev_coll[2], ev_coll[3]]

            sc_last, ev_et = sort_stage(OH, SCA, SCB, oh_evs, prob_evs, 0, 0, [])
            st_r1.close()

            up_sb, up_a1 = moe_tiles("u", NC, 4096, xn_hbm, h01_hbm, 0,
                                     ABup16, 64, selup_sb, repbu_sb, 32, 64,
                                     sc_last, ev_et, [xnh_last])

            st_g = ExitStack()
            OH2 = st_g.enter_context(nc.sbuf_tensor("OH2s", [64, 2 * TOK], BF16))
            SCA2 = st_g.enter_context(nc.sbuf_tensor("SCA2s", [64, TOK], F32))
            SCB2 = st_g.enter_context(nc.sbuf_tensor("SCB2s", [64, TOK], F32))
            h0t = [st_g.enter_context(nc.sbuf_tensor("h0t%d" % k, [128, 4096], BF16))
                   for k in range(2)]
            h1t = [st_g.enter_context(nc.sbuf_tensor("h1t%d" % k, [128, 4096], BF16))
                   for k in range(2)]
            hsum = [st_g.enter_context(nc.sbuf_tensor("hsum%d" % k, [128, 4096], F32))
                    for k in range(2)]
            ht16 = [st_g.enter_context(nc.sbuf_tensor("ht16%d" % k, [128, 4096], BF16))
                    for k in range(2)]
            oh_evs2 = []
            prob_evs2 = []
            ev_h = [None, None]
            ev_ht = [None, None]
            ev_coll = []
            hs_last = None
            for tt in range(NT):
                pi = tt % 2
                t0 = tt * 128
                d0 = p.run("y", lambda e, t0=t0, pi=pi: e.dma_start(
                    out=h0t[pi][:], in_=h01_hbm[t0:t0 + 128, :]),
                    deps=up_sb + [ev_h[pi]], dma=True, slot="h0%d" % pi)
                d1 = p.run("y", lambda e, t0=t0, pi=pi: e.dma_start(
                    out=h1t[pi][:], in_=h01_hbm[TOK + t0:TOK + t0 + 128, :]),
                    deps=up_sb + [ev_h[pi]], dma=True, slot="h1%d" % pi)
                s1 = p.run("v", lambda e, pi=pi: e.tensor_tensor(
                    hsum[pi][:], h0t[pi][:], h1t[pi][:], op=ALU.add), deps=[d0, d1])
                s2 = p.run("v", lambda e, pi=pi: e.tensor_scalar(
                    hsum[pi][:], hsum[pi][:], scup[:, 0:1], None, op0=ALU.mult),
                    deps=[s1, ev_sc])
                s3 = p.run("v", lambda e, pi=pi: e.tensor_tensor(
                    hsum[pi][:], hsum[pi][:], bupb[:], op=ALU.add), deps=[s2, ev_cf])
                g1 = p.run("s", lambda e, pi=pi: e.activation(
                    out=ht16[pi][:], in_=hsum[pi][:], func=AF.Gelu),
                    deps=[s3, ev_ht[pi]])
                dh = p.run("y", lambda e, t0=t0, pi=pi: e.dma_start(
                    out=h_hbm[t0:t0 + 128, :], in_=ht16[pi][:]), deps=[g1], dma=True,
                    slot="hs")
                hs_last = dh
                ev_ht[pi] = dh
                evtr = None
                for c in range(32):
                    cb = c % 2
                    tr = p.run("t", lambda e, c=c, pi=pi, cb=cb: e.transpose(
                        PST[cb][:, 0:128], ht16[pi][:, c * 128:(c + 1) * 128],
                        identb[:]), deps=[g1, evtr])
                    evtr = p.run("v", lambda e, c=c, cb=cb: e.tensor_copy(
                        hTt[:, c * 128:(c + 1) * 128], PST[cb][:, 0:128]), deps=[tr])
                ev_h[pi] = evtr
                mml = None
                for c in range(32):
                    mml = p.run("t", lambda e, c=c: e.matmul(
                        PS[4][0:128, 0:64], hTt[:, c * 128:(c + 1) * 128],
                        wdnT[:, c * 64:(c + 1) * 64], start=(c == 0), stop=(c == 31)),
                        deps=([evtr, ev_cf] + ev_coll if c == 0 else []), ev=(c == 31))
                lge = p.run("v", lambda e: e.tensor_copy(logitsb[:], PS[4][0:128, 0:64]),
                            deps=[mml])
                tevs = top2idx(lge, ev_coll)
                ev_coll = collect_tile(tt, OH2, tevs, 4)
                oh_evs2 += [ev_coll[0], ev_coll[1]]
                prob_evs2 += [ev_coll[2], ev_coll[3]]
            for eng in ("v", "s", "t", "g", "y"):
                p.barrier_all(eng)
            sc_last2, ev_et2 = sort_stage(OH2, SCA2, SCB2, oh_evs2, prob_evs2, 4, 16384,
                                          [up_a1])
            st_g.close()

            dn_sb, _ = moe_tiles("d", 32, 1024, h_hbm, o01_hbm, 16384,
                                 ABdn16, 128, seldn_sb, repbd_sb, 64, 32,
                                 sc_last2, ev_et2, [hs_last])

            st_f = ExitStack()
            o0t = [st_f.enter_context(nc.sbuf_tensor("o0t%d" % k, [128, 1024], BF16))
                   for k in range(2)]
            o1t = [st_f.enter_context(nc.sbuf_tensor("o1t%d" % k, [128, 1024], BF16))
                   for k in range(2)]
            foutb = [st_f.enter_context(nc.sbuf_tensor("foutb%d" % k, [128, 1024], F32))
                     for k in range(2)]
            ev_f = [None, None]
            for tt in range(NT):
                pi = tt % 2
                t0 = tt * 128
                d0 = p.run("y", lambda e, t0=t0, pi=pi: e.dma_start(
                    out=o0t[pi][:], in_=o01_hbm[t0:t0 + 128, :]),
                    deps=dn_sb + [ev_f[pi]], dma=True, slot="o0%d" % pi)
                d1 = p.run("y", lambda e, t0=t0, pi=pi: e.dma_start(
                    out=o1t[pi][:], in_=o01_hbm[TOK + t0:TOK + t0 + 128, :]),
                    deps=dn_sb + [ev_f[pi]], dma=True, slot="o1%d" % pi)
                f0 = p.run("v", lambda e, pi=pi: e.tensor_tensor(
                    foutb[pi][:], o0t[pi][:], o1t[pi][:], op=ALU.add), deps=[d0, d1])
                f1 = p.run("v", lambda e, pi=pi: e.tensor_scalar(
                    foutb[pi][:], foutb[pi][:], scup[:, 1:2], None, op0=ALU.mult),
                    deps=[f0, ev_sc])
                f2 = p.run("v", lambda e, pi=pi: e.tensor_tensor(
                    foutb[pi][:], foutb[pi][:], bdnb[:], op=ALU.add), deps=[f1, ev_cf])
                ac = p.run("g", lambda e, t0=t0, pi=pi: e.dma_start(
                    out=x_out[t0:t0 + 128, :], in_=foutb[pi][:], accum_op=ALU.add),
                    deps=[f2, attn_accums[-1]] + ev_init, dma=True, slot="f0a")
                ffn0_accums.append(ac)
                ev_f[pi] = ac
            st_f.close()
            stf.close()

        if fused is None:
            p.barrier_all("y")
            p.emit()
    return nc


def prep_block0(i, x_shards):
    H, HD = 16, 64
    wqkv = _f32(i["L0_attn_w_qkv"])
    wout = _f32(i["L0_attn_w_out"])
    bout = _f32(i["L0_attn_b_out"])
    wq = wqkv[0:1024]
    wk = wqkv[1024:2048]
    wv = wqkv[2048:3072]
    wqkT = np.zeros((H, 128, 1024), np.float32)
    for h in range(H):
        qT = wq[h * 64:(h + 1) * 64, :].T
        kT = wk[h * 64:(h + 1) * 64, :].T
        qk = np.concatenate([qT.reshape(8, 128, 64), kT.reshape(8, 128, 64)], axis=2)
        wqkT[h] = qk.transpose(1, 0, 2).reshape(128, 1024)
    wvT = np.ascontiguousarray(wv.T.reshape(8, 128, 1024).transpose(1, 0, 2).reshape(128, 8192))
    woutT = np.zeros((128, 8 * 1024), np.float32)
    for h in range(H):
        blockw = wout[:, h * 64:(h + 1) * 64].T
        r0 = (h % 2) * 64
        woutT[r0:r0 + 64, (h // 2) * 1024:(h // 2 + 1) * 1024] = blockw
    boutb = np.ascontiguousarray(np.broadcast_to(bout[None, :], (128, 1024)))
    pos = np.arange(2048, dtype=np.float32)
    inv = 1.0 / (10000.0 ** (np.arange(0, 64, 2, dtype=np.float32) / 64.0))
    fr = pos[None, :] * inv[:, None]
    cosT = np.concatenate([np.cos(fr), np.cos(fr)], axis=0).astype(np.float32)
    sinT = np.concatenate([np.sin(fr), np.sin(fr)], axis=0).astype(np.float32)
    identf = np.eye(128, dtype=np.float32)
    onesf = np.ones((128, 512), dtype=np.float32)
    wupT = np.ascontiguousarray(_f32(i["L0_ffn_W_up"]).T.reshape(8, 128, 64)
                                .transpose(1, 0, 2).reshape(128, 512))
    wdnT = np.ascontiguousarray(_f32(i["L0_ffn_W_down"]).T.reshape(32, 128, 64)
                                .transpose(1, 0, 2).reshape(128, 2048))
    iota64 = np.ascontiguousarray(np.broadcast_to(np.arange(64, dtype=np.float32)[None, :],
                                                  (128, 64)))
    scud = np.array([[float(_f32(i["L0_ffn_scale_up"])[0]),
                      float(_f32(i["L0_ffn_scale_down"])[0])]], dtype=np.float32)
    bup = np.ascontiguousarray(np.broadcast_to(_f32(i["L0_ffn_bias_up"])[None, :],
                                               (128, 4096)))
    bdn = np.ascontiguousarray(np.broadcast_to(_f32(i["L0_ffn_bias_down"])[None, :],
                                               (128, 1024)))
    Aup3 = _f32(i["L0_ffn_A_up"]).reshape(64, 64, 32)
    Bup3 = _f32(i["L0_ffn_B_up"]).reshape(64, 64, 32)
    Adn3 = _f32(i["L0_ffn_A_down"]).reshape(64, 32, 64)
    Bdn3 = _f32(i["L0_ffn_B_down"]).reshape(64, 32, 64)
    ABupT = np.ascontiguousarray(np.concatenate(
        [Aup3.transpose(0, 2, 1), Bup3.transpose(0, 2, 1)], axis=1).reshape(4096, 64))
    ABdnT = np.ascontiguousarray(np.concatenate(
        [Adn3.transpose(0, 2, 1), Bdn3.transpose(0, 2, 1)], axis=1).reshape(8192, 32))
    selup = np.zeros((32, 1024), np.float32)
    for c in range(8):
        for a in range(4):
            for j in range(32):
                selup[4 * c + a, c * 128 + 32 * a + j] = 1.0
    repbu = np.zeros((64, 128), np.float32)
    for a in range(4):
        for j in range(32):
            repbu[32 + j, 32 * a + j] = 1.0
    seldn = np.zeros((64, 4096), np.float32)
    for c in range(32):
        for b2 in range(2):
            for j in range(64):
                seldn[2 * c + b2, c * 128 + 64 * b2 + j] = 1.0
    repbd = np.zeros((128, 128), np.float32)
    for b2 in range(2):
        for j in range(64):
            repbd[64 + j, 64 * b2 + j] = 1.0
    triuf = np.triu(np.ones((64, 64), np.float32), 1)
    aenc = np.arange(8192, dtype=np.float32).reshape(128, 64)
    iotap = np.arange(128, dtype=np.float32).reshape(128, 1)
    tstart = (np.arange(128, dtype=np.float32) * 128).reshape(1, 128)
    sentp = np.zeros((128, 256), np.float32)
    pp = np.arange(128)[:, None]
    kk = np.arange(128)[None, :]
    sentp[:, 0::2] = 8192.0 + (pp % 8) * 128 + kk
    common = dict(wqkT=wqkT, wvT=wvT, woutT=woutT, boutb=boutb, cosT=cosT, sinT=sinT,
                  identf=identf, onesf=onesf, wupT=wupT, wdnT=wdnT, iota64=iota64,
                  scud=scud, bup=bup, bdn=bdn, ABupT=ABupT, ABdnT=ABdnT, selup=selup,
                  repbu=repbu, seldn=seldn, repbd=repbd, triuf=triuf, aenc=aenc,
                  iotap=iotap, tstart=tstart, sentp=sentp)
    return [dict(common, x_in=x_shards[c]) for c in range(NCORES)]


def run_block0(i, x, with_ffn=True):
    shards = _shard_x(x)
    nc = build_block0(with_ffn=with_ffn)
    maps = prep_block0(i, shards)
    res = run_bass_kernel_spmd(nc, maps, core_ids=list(range(NCORES)))
    return _gather_x([r["x_out"] for r in res.results])


def build_block1(fused=None):
    nc = (bass.Bass("TRN2", target_bir_lowering=False) if fused is None
          else _NCP(fused[0], fused[2]))
    C = 256
    NCH = S // C
    E1 = 4
    I1 = 4096

    if fused is None:
        x_in = nc.dram_tensor("x_in", [TOK, D], F32, kind="ExternalInput").ap()
        x_out = nc.dram_tensor("x_out", [TOK, D], F32, kind="ExternalOutput").ap()
    else:
        x_in, x_out = fused[3], fused[4]
    xn_hbm = nc.dram_tensor("xn_hbm", [TOK, D], BF16, kind="Internal").ap()
    intT_hbm = nc.dram_tensor("intT_hbm", [32, 128, TOK], BF16, kind="Internal").ap()

    wwT_h = nc.dram_tensor("wwT", [128, 8 * 1024], F32, kind="ExternalInput").ap()
    wrT_h = nc.dram_tensor("wrT", [128, 8 * 1024], F32, kind="ExternalInput").ap()
    dec_h = nc.dram_tensor("dec", [1, 2], F32, kind="ExternalInput").ap()
    w1_h = nc.dram_tensor("w1r", [32, 128, E1 * 8 * 128], F32, kind="ExternalInput").ap()
    b1_h = nc.dram_tensor("b1e", [E1, I1], F32, kind="ExternalInput").ap()
    wo_h = nc.dram_tensor("wor", [128, 32 * 1024], F32, kind="ExternalInput").ap()
    bo_h = nc.dram_tensor("bo1", [1, D], F32, kind="ExternalInput").ap()
    wrouter_h = nc.dram_tensor("wrouter", [128, 8 * 4], F32, kind="ExternalInput").ap()
    sel4_h = nc.dram_tensor("sel4", [4, 512], F32, kind="ExternalInput").ap()
    gln_h = nc.dram_tensor("gln", [128, D], F32, kind="ExternalInput").ap()
    bln_h = nc.dram_tensor("bln", [128, D], F32, kind="ExternalInput").ap()
    identf = nc.dram_tensor("identf", [128, 128], F32, kind="ExternalInput").ap()
    onesf = nc.dram_tensor("onesf", [128, 512], F32, kind="ExternalInput").ap()

    with ExitStack() as st:
        def sb(name, shape, dt=BF16):
            return st.enter_context(nc.sbuf_tensor(name, shape, dt))

        xnT = sb("xnT1", [128, NC * TOK])
        tiles = mk_tiles(st, nc, with_xpx=True)
        identb = sb("identb1", [128, 128])
        onesb_r = sb("onesb_r1", [1, 128])
        onesf_r = sb("onesf_r1", [1, 128], F32)

        PS = [st.enter_context(nc.psum_tensor("b1ps%d" % i, [128, 512], F32)) for i in range(6)]
        PST = [st.enter_context(nc.psum_tensor("b1pst%d" % i, [128, 1024], BF16)) for i in range(2)]

        p = Prog(nc) if fused is None else fused[1]
        tiles["eps_ev"] = p.run("v", lambda e: e.memset(tiles["eps"][:], 1e-5))

        st2 = ExitStack()

        def sb2(name, shape, dt=BF16):
            return st2.enter_context(nc.sbuf_tensor(name, shape, dt))
        wwT = sb2("wwT_sb", [128, 8 * 1024])
        wrT = sb2("wrT_sb", [128, 8 * 1024])
        WT = sb2("WT_sb", [128, 8 * 1024])
        v_sb = sb2("v_sb1", [128, 16 * 1024])
        vts = sb2("vts", [128, 1024])
        wk_sb = [sb2("wk_sb%d" % i, [128, 1024]) for i in range(2)]
        wk0 = sb2("wk0", [128, 8 * 128])
        rkg = sb2("rkg", [128, 8 * 256])
        STm = sb2("STm", [128, 2 * 256])
        MT = sb2("MT", [128, 2 * 256], F32)
        readsT = sb2("readsT", [128, 8 * 256])
        gam = sb2("gam", [1, 8], F32)
        lg_pp = sb2("lg_pp", [128, 1], F32)
        gC_pp = sb2("gC_pp", [128, 1], F32)
        al_pp = sb2("al_pp", [128, 1], F32)
        gw2 = sb2("gw2", [128, 2], F32)
        gpb = sb2("gpb", [128, 256])
        iosc = sb2("iosc", [128, 256], F32)
        at1 = [sb2("at1_%d" % i, [128, 512], F32) for i in range(2)]

        for ld_fn in [
            lambda e: e.dma_start(out=identb[:], in_=identf),
            lambda e: e.dma_start(out=onesb_r[:], in_=onesf[0:1, 0:128]),
            lambda e: e.dma_start(out=onesf_r[:], in_=onesf[0:1, 0:128]),
            lambda e: e.dma_start(out=wwT[:], in_=wwT_h),
            lambda e: e.dma_start(out=wrT[:], in_=wrT_h),
            lambda e: e.dma_start(out=gam[0:1, 0:2], in_=dec_h),
        ]:
            ev_const = p.run("g", ld_fn, dma=True, slot="const")

        g1 = p.run("s", lambda e: e.activation(out=gam[0:1, 2:3], in_=gam[0:1, 0:1],
                                               func=AF.Sigmoid), deps=[ev_const])
        g2 = p.run("s", lambda e: e.activation(out=gam[0:1, 3:4], in_=gam[0:1, 2:3],
                                               func=AF.Ln), deps=[g1])
        g3 = p.run("s", lambda e: e.activation(out=gam[0:1, 4:5], in_=gam[0:1, 3:4],
                                               func=AF.Exp, scale=float(C)), deps=[g2])
        g4 = p.run("s", lambda e: e.activation(out=gam[0:1, 5:6], in_=gam[0:1, 1:2],
                                               func=AF.Exp), deps=[g2])
        bc1 = p.run("t", lambda e: e.matmul(PS[0][:, 0:3], onesf_r[0:1, :], gam[0:1, 3:6],
                                            start=True, stop=True), deps=[g3, g4, ev_const])
        bc2 = p.run("v", lambda e: e.tensor_copy(lg_pp[:], PS[0][:, 0:1]), deps=[bc1])
        bc3 = p.run("v", lambda e: e.tensor_copy(gC_pp[:], PS[0][:, 1:2]), deps=[bc1])
        bc4 = p.run("v", lambda e: e.tensor_copy(al_pp[:], PS[0][:, 2:3]), deps=[bc1])
        i1 = p.run("g", lambda e: e.iota(iosc[:, 0:2], pattern=[[-128, 2]], base=C - 1,
                                         channel_multiplier=-1,
                                         allow_small_or_imprecise_dtypes=True), deps=[bc2])
        i2 = p.run("s", lambda e: e.activation(out=gw2[:], in_=iosc[:, 0:2], func=AF.Exp,
                                               scale=lg_pp[:, 0:1]), deps=[i1, bc2])
        i3 = p.run("g", lambda e: e.iota(iosc[:], pattern=[[1, 256]], base=0,
                                         channel_multiplier=0,
                                         allow_small_or_imprecise_dtypes=True), deps=[i2])
        i4 = p.run("s", lambda e: e.activation(out=gpb[:], in_=iosc[:], func=AF.Exp,
                                               scale=lg_pp[:, 0:1]), deps=[i3, bc2])
        ev_MT = None
        for kh in range(2):
            j1 = p.run("g", lambda e, kh=kh: e.iota(iosc[:], pattern=[[1, 256]],
                                                    base=-1 - kh * 128, channel_multiplier=-1,
                                                    allow_small_or_imprecise_dtypes=True),
                       deps=[i4, ev_MT])
            j2 = p.run("v", lambda e: e.tensor_scalar(iosc[:], iosc[:], 0.0, None, op0=ALU.max),
                       deps=[j1])
            j3 = p.run("s", lambda e, kh=kh: e.activation(out=MT[:, kh * 256:(kh + 1) * 256],
                                                          in_=iosc[:], func=AF.Exp,
                                                          scale=lg_pp[:, 0:1]), deps=[j2])
            j4 = p.run("g", lambda e, kh=kh: e.iota(iosc[:], pattern=[[1, 256]],
                                                    base=-1 - kh * 128, channel_multiplier=-1,
                                                    allow_small_or_imprecise_dtypes=True),
                       deps=[j3])
            ev_MT = p.run("g", lambda e, kh=kh: e.affine_select(
                MT[:, kh * 256:(kh + 1) * 256], MT[:, kh * 256:(kh + 1) * 256],
                pattern=[[1, 256]], compare_op=ALU.is_ge, fill=0.0,
                base=-1 - kh * 128, channel_multiplier=-1), deps=[j4])

        ln_done = emit_ln_prologue(p, tiles, x_in, xnT, identb, PST, dep0=[ev_const],
                                   x_out_plus_xn=x_out, xn_hbm=xn_hbm)
        ev_init = [ln_done[-1], ln_done[-2]]

        attn_accums = []
        prev_state = None
        prev_iter = [None]
        ev_ro = None
        for b in range(NB):
            zs = p.run("v", lambda e: e.memset(WT[:], 0.0), deps=[prev_state] + prev_iter)
            ev_v = None
            for kt in range(16):
                for ic2 in range(2):
                    mme = None
                    for c in range(NC):
                        mme = p.run("t", lambda e, kt=kt, ic2=ic2, c=c, b=b: e.matmul(
                            PS[4][:, 0:512],
                            xnT[:, c * TOK + b * S + kt * 128: c * TOK + b * S + (kt + 1) * 128],
                            wwT[:, c * 1024 + ic2 * 512: c * 1024 + (ic2 + 1) * 512],
                            start=(c == 0), stop=(c == NC - 1)),
                            deps=(ln_done + [ev_v, ev_const] + prev_iter if c == 0 else []),
                            ev=(c == NC - 1))
                    ev_v = p.run("v", lambda e, kt=kt, ic2=ic2: e.tensor_copy(
                        v_sb[:, kt * 1024 + ic2 * 512: kt * 1024 + (ic2 + 1) * 512],
                        PS[4][:, 0:512]), deps=[mme])
            ev_state = zs
            ev_reads = None
            ev_stm = None
            for ch in range(NCH):
                base = b * S + ch * C
                ev_wk = []
                for khalf in range(2):
                    r0 = base + khalf * 128 - 1
                    if r0 < b * S:
                        ldw = p.run("y", lambda e, khalf=khalf, r0=r0: e.dma_start(
                            out=wk_sb[khalf][1:128, :], in_=xn_hbm[r0 + 1:r0 + 128, :]),
                            deps=[ev_state, ln_done[-1]], dma=True, slot="wk%d" % khalf)
                        zw = p.run("v", lambda e, khalf=khalf: e.memset(wk_sb[khalf][0:1, :], 0.0),
                                   deps=[ldw])
                        ev_wk.append(zw)
                    else:
                        ldw = p.run("y", lambda e, khalf=khalf, r0=r0: e.dma_start(
                            out=wk_sb[khalf][:], in_=xn_hbm[r0:r0 + 128, :]),
                            deps=[ev_state, ln_done[-1]], dma=True, slot="wk%d" % khalf)
                        ev_wk.append(ldw)
                ev_rkg = None
                for c in range(NC):
                    ev_rkg = p.run("v", lambda e, c=c, base=base: e.tensor_tensor(
                        rkg[:, c * 256:(c + 1) * 256],
                        xnT[:, c * TOK + base: c * TOK + base + 256], gpb[:],
                        op=ALU.mult), deps=[i4, ln_done[c % 2], ev_reads, ev_ro])
                for kh in range(2):
                    mms = None
                    for c in range(NC):
                        if ch == 0 and kh == 0:
                            cpw = p.run("v", lambda e, c=c, base=base: e.tensor_copy(
                                wk0[:, c * 128 + 1:c * 128 + 128],
                                xnT[:, c * TOK + base: c * TOK + base + 127]),
                                deps=[ln_done[c % 2], ev_stm, ev_ro, prev_iter[0]])
                            cpz = p.run("v", lambda e, c=c: e.memset(wk0[:, c * 128:c * 128 + 1],
                                                                     0.0), deps=[cpw])
                            mms = p.run("t", lambda e, c=c, base=base: e.matmul(
                                PS[2][:, 0:256], wk0[:, c * 128:(c + 1) * 128],
                                xnT[:, c * TOK + base: c * TOK + base + 256],
                                start=(c == 0), stop=(c == NC - 1)),
                                deps=[cpz], ev=(c == NC - 1))
                        else:
                            mms = p.run("t", lambda e, c=c, kh=kh, base=base: e.matmul(
                                PS[2][:, 0:256],
                                xnT[:, c * TOK + base + kh * 128 - 1: c * TOK + base + kh * 128 + 127],
                                xnT[:, c * TOK + base: c * TOK + base + 256],
                                start=(c == 0), stop=(c == NC - 1)),
                                deps=([ev_stm, ev_reads, ev_ro, ev_state] if c == 0 else []),
                                ev=(c == NC - 1))
                    ev_stm = p.run("v", lambda e, kh=kh: e.tensor_tensor(
                        STm[:, kh * 256:(kh + 1) * 256], PS[2][:, 0:256],
                        MT[:, kh * 256:(kh + 1) * 256], op=ALU.mult), deps=[mms, ev_MT])
                ev_reads_new = None
                reads_free = [ev_reads, ev_reads]
                for ic in range(NC):
                    pb = ic % 2
                    for jc in range(NC):
                        p.run("t", lambda e, ic=ic, jc=jc, pb=pb: e.matmul(
                            PS[pb][:, 0:256],
                            WT[:, jc * 1024 + ic * 128: jc * 1024 + (ic + 1) * 128],
                            rkg[:, jc * 256:(jc + 1) * 256],
                            start=(jc == 0), stop=False),
                            deps=([ev_rkg, ev_state, reads_free[pb]] if jc == 0 else []), ev=False)
                    mmr = None
                    for kh in range(2):
                        mmr = p.run("t", lambda e, ic=ic, kh=kh, ch=ch, pb=pb: e.matmul(
                            PS[pb][:, 0:256],
                            v_sb[:, (2 * ch + kh) * 1024 + ic * 128: (2 * ch + kh) * 1024 + (ic + 1) * 128],
                            STm[:, kh * 256:(kh + 1) * 256],
                            start=False, stop=(kh == 1)),
                            deps=([ev_stm, ev_v] if kh == 0 else []), ev=(kh == 1))
                    ev_reads_new = p.run("v", lambda e, ic=ic, pb=pb: e.tensor_copy(
                        readsT[:, ic * 256:(ic + 1) * 256], PS[pb][:, 0:256]),
                        deps=[mmr])
                    reads_free[pb] = ev_reads_new
                ev_reads = ev_reads_new
                ev_state_new = None
                sps_free = {2: ev_stm, 3: None, 4: None, 5: None}
                for khalf in range(2):
                    ev_vts = p.run("v", lambda e, khalf=khalf, ch=ch: e.tensor_scalar(
                        vts[:], v_sb[:, (2 * ch + khalf) * 1024: (2 * ch + khalf + 1) * 1024],
                        gw2[:, khalf:khalf + 1], None, op0=ALU.mult),
                        deps=[ev_v, i2, ev_state, ev_state_new])
                    for jc in range(NC):
                        for icol in range(2):
                            pidx = 2 + 2 * (jc % 2) + icol
                            mmd = p.run("t", lambda e, jc=jc, icol=icol, khalf=khalf, pidx=pidx: e.matmul(
                                PS[pidx][:, 0:512],
                                wk_sb[khalf][:, jc * 128:(jc + 1) * 128],
                                vts[:, icol * 512:(icol + 1) * 512],
                                start=True, stop=True),
                                deps=[ev_wk[khalf], ev_vts, ev_state, sps_free[pidx],
                                      ev_reads])
                            if khalf == 0:
                                stte = p.run("v", lambda e, jc=jc, icol=icol, pidx=pidx: e.scalar_tensor_tensor(
                                    WT[:, jc * 1024 + icol * 512: jc * 1024 + (icol + 1) * 512],
                                    WT[:, jc * 1024 + icol * 512: jc * 1024 + (icol + 1) * 512],
                                    gC_pp[:, 0:1], PS[pidx][:, 0:512],
                                    op0=ALU.mult, op1=ALU.add), deps=[mmd, bc3, ev_reads])
                            else:
                                stte = p.run("v", lambda e, jc=jc, icol=icol, pidx=pidx: e.tensor_tensor(
                                    WT[:, jc * 1024 + icol * 512: jc * 1024 + (icol + 1) * 512],
                                    WT[:, jc * 1024 + icol * 512: jc * 1024 + (icol + 1) * 512],
                                    PS[pidx][:, 0:512], op=ALU.add),
                                    deps=[mmd, ev_reads])
                            sps_free[pidx] = stte
                            ev_state_new = stte
                ev_state = ev_state_new
                for ct in range(2):
                    for ecol in range(2):
                        ab = ct
                        mmo = None
                        for ic in range(NC):
                            mmo = p.run("t", lambda e, ic=ic, ct=ct, ecol=ecol: e.matmul(
                                PS[2 + ct][:, 0:512],
                                readsT[:, ic * 256 + ct * 128: ic * 256 + (ct + 1) * 128],
                                wrT[:, ic * 1024 + ecol * 512: ic * 1024 + (ecol + 1) * 512],
                                start=(ic == 0), stop=(ic == NC - 1)),
                                deps=([ev_reads, ev_ro, ev_const, ev_stm, ev_state]
                                      if ic == 0 else []),
                                ev=(ic == NC - 1))
                        evo = p.run("s", lambda e, ab=ab, ct=ct: e.activation(
                            out=at1[ab][:], in_=PS[2 + ct][:, 0:512], func=AF.Identity,
                            scale=al_pp[:, 0:1]), deps=[mmo, bc4])
                        row0 = base + ct * 128
                        ev_ro = p.run("g", lambda e, row0=row0, ecol=ecol, ab=ab: e.dma_start(
                            out=x_out[row0:row0 + 128, ecol * 512:(ecol + 1) * 512],
                            in_=at1[ab][:], accum_op=ALU.add),
                            deps=[evo] + (ev_init if b == 0 and ch == 0 and ct == 0 and ecol == 0
                                          else []),
                            dma=True, slot="ro")
                        attn_accums.append(ev_ro)
            prev_state = ev_state
            prev_iter = [ev_ro, ev_reads, ev_stm]

        st2.close()
        st3 = ExitStack()

        def sb3(name, shape, dt=BF16):
            return st3.enter_context(nc.sbuf_tensor(name, shape, dt))
        wrt_sb = sb3("wrt_sb", [128, 8 * 4])
        logits = sb3("logits", [128, 4], F32)
        lmax = sb3("lmax", [128, 1], F32)
        onehot = sb3("onehot", [128, 4], F32)
        oh4 = sb3("oh4", [4, TOK])
        sel4 = sb3("sel4_sb", [4, 512])
        b1_sb = sb3("b1_sb", [4, I1])
        mb = [sb3("mb%d" % e_, [128, 1024]) for e_ in range(E1)]
        xm = sb3("xm", [128, 4 * 8 * 1024])
        w1b = [sb3("w1b%d" % i, [128, E1 * 8 * 128]) for i in range(2)]
        zev = [sb3("zev%d" % i, [128, 512]) for i in range(2)]

        for ld_fn in [
            lambda e: e.dma_start(out=wrt_sb[:], in_=wrouter_h),
            lambda e: e.dma_start(out=b1_sb[:], in_=b1_h),
            lambda e: e.dma_start(out=sel4[:], in_=sel4_h),
        ]:
            ev_c2 = p.run("g", ld_fn, dma=True, slot="const2")

        ffn_pre = attn_accums[-1:] + list(prev_iter) + [prev_state, ev_c2]

        ev_oh = None
        for ttile in range(NT):
            mml = None
            for c in range(NC):
                mml = p.run("t", lambda e, ttile=ttile, c=c: e.matmul(
                    PS[0][0:128, 0:4], xnT[:, c * TOK + ttile * 128: c * TOK + (ttile + 1) * 128],
                    wrt_sb[:, c * 4:(c + 1) * 4], start=(c == 0), stop=(c == NC - 1)),
                    deps=((ffn_pre if ttile == 0 else []) + [ev_oh] if c == 0 else []),
                    ev=(c == NC - 1))
            cl = p.run("v", lambda e: e.tensor_copy(logits[:], PS[0][0:128, 0:4]), deps=[mml])
            rm = p.run("v", lambda e: e.reduce_max(lmax[:], logits[:], axis=AX.X), deps=[cl])
            oh = p.run("v", lambda e: e.tensor_scalar(onehot[:], logits[:], lmax[:, 0:1], None,
                                                      op0=ALU.is_ge), deps=[rm])
            ohb = p.run("v", lambda e: e.tensor_copy(tiles["xnb"][0][:, 0:4], onehot[:]),
                        deps=[oh, ev_oh])
            tr = p.run("t", lambda e: e.transpose(
                PST[0][0:4, 0:128], tiles["xnb"][0][:, 0:4], identb[:]), deps=[ohb, ev_oh])
            ev_oh = p.run("v", lambda e, ttile=ttile: e.tensor_copy(
                oh4[0:4, ttile * 128:(ttile + 1) * 128], PST[0][0:4, 0:128]), deps=[tr])
        ev_oh4 = ev_oh

        ev_xm = None
        ev_w1 = [None, None]
        ev_int = None
        intT_stores = None
        for tc in range(4):
            t0 = tc * 1024
            ev_mb = None
            for e_ in range(E1):
                mmm = p.run("t", lambda e, e_=e_, t0=t0: e.matmul(
                    PS[1][:, 0:512], sel4[0:4, e_ * 128:(e_ + 1) * 128], oh4[0:4, t0:t0 + 512],
                    start=True, stop=True), deps=[ev_oh, ev_mb, ev_c2])
                mmm2 = p.run("t", lambda e, e_=e_, t0=t0: e.matmul(
                    PS[3][:, 0:512], sel4[0:4, e_ * 128:(e_ + 1) * 128],
                    oh4[0:4, t0 + 512:t0 + 1024],
                    start=True, stop=True), deps=[])
                c1e = p.run("v", lambda e, e_=e_: e.tensor_copy(mb[e_][:, 0:512], PS[1][:, 0:512]),
                            deps=[mmm])
                ev_mb = p.run("v", lambda e, e_=e_: e.tensor_copy(mb[e_][:, 512:1024],
                                                                  PS[3][:, 0:512]),
                              deps=[mmm2, c1e])
            ev_xm_new = None
            for e_ in range(E1):
                for c in range(NC):
                    ev_xm_new = p.run("v", lambda e, e_=e_, c=c, t0=t0: e.tensor_tensor(
                        xm[:, (e_ * 8 + c) * 1024:(e_ * 8 + c + 1) * 1024],
                        xnT[:, c * TOK + t0: c * TOK + t0 + 1024], mb[e_][:],
                        op=ALU.mult), deps=[ev_mb, ev_xm])
            ev_xm = ev_xm_new
            for ic in range(32):
                wb = ic % 2
                lw = p.run("g", lambda e, ic=ic, wb=wb: e.dma_start(
                    out=w1b[wb][:], in_=w1_h[ic]), deps=[ev_w1[wb]], dma=True,
                    slot="w1%d" % wb)
                for tch in range(2):
                    for e_ in range(E1):
                        for c in range(NC):
                            p.run("t", lambda e, e_=e_, c=c, ic=ic, tch=tch, wb=wb: e.matmul(
                                PS[4 + tch][:, 0:512],
                                w1b[wb][:, (e_ * 8 + c) * 128:(e_ * 8 + c + 1) * 128],
                                xm[:, (e_ * 8 + c) * 1024 + tch * 512:(e_ * 8 + c) * 1024 + (tch + 1) * 512],
                                start=(e_ == 0 and c == 0), stop=False),
                                deps=([lw, ev_xm, ev_int] if e_ == 0 and c == 0 else []),
                                ev=False)
                    mmz = p.run("t", lambda e, ic=ic, tch=tch, t0=t0: e.matmul(
                        PS[4 + tch][:, 0:512], b1_sb[0:4, ic * 128:(ic + 1) * 128],
                        oh4[0:4, t0 + tch * 512:t0 + (tch + 1) * 512],
                        start=False, stop=True), deps=[ev_oh4, ev_c2])
                    gl = p.run("s", lambda e, tch=tch: e.activation(
                        out=zev[tch][:], in_=PS[4 + tch][:, 0:512], func=AF.Gelu),
                        deps=[mmz, ev_int])
                    ev_int = p.run("y", lambda e, ic=ic, tch=tch, t0=t0: e.dma_start(
                        out=intT_hbm[ic, :, t0 + tch * 512:t0 + (tch + 1) * 512],
                        in_=zev[tch][:]), deps=[gl], dma=True, slot="it%d" % tch)
                ev_w1[wb] = ev_int
        st3.close()

        st4 = ExitStack()

        def sb4(name, shape, dt=BF16):
            return st4.enter_context(nc.sbuf_tensor(name, shape, dt))
        wo_sb = sb4("wo_sb", [128, 32 * 1024])
        itb = [sb4("itb%d" % i, [128, 32 * 128]) for i in range(2)]
        glnb = sb4("glnb", [128, D], F32)
        blnb = sb4("blnb", [128, D], F32)
        bob = sb4("bob", [1, D])
        xnrow = sb4("xnrow", [128, D])
        yt = sb4("yt", [128, D], F32)
        ffn_accums = []
        ev_c3 = None
        for ld_fn in [
            lambda e: e.dma_start(out=wo_sb[:], in_=wo_h),
            lambda e: e.dma_start(out=glnb[:], in_=gln_h),
            lambda e: e.dma_start(out=blnb[:], in_=bln_h),
            lambda e: e.dma_start(out=bob[:], in_=bo_h),
        ]:
            ev_c3 = p.run("g", ld_fn, deps=[ev_int], dma=True, slot="const3")
        eps12 = p.run("v", lambda e: e.memset(tiles["eps"][:], 1e-12), deps=[ev_int])
        tiles["eps_ev"] = eps12

        ev_itb = [None, None]
        ev_y = None
        prev_lnf = None
        prev_acc = None
        for ttile in range(NT):
            ib = ttile % 2
            li = p.run("y", lambda e, ttile=ttile, ib=ib: e.dma_start(
                out=itb[ib][:].rearrange("p (a t) -> p a t", a=32),
                in_=intT_hbm[:, :, ttile * 128:(ttile + 1) * 128].rearrange("a p t -> p a t")),
                deps=[ev_itb[ib], ev_int], dma=True, slot="itb%d" % ib)
            lx = p.run("y", lambda e, ttile=ttile: e.dma_start(
                out=xnrow[:], in_=xn_hbm[ttile * 128:(ttile + 1) * 128, :]),
                deps=[ev_y, ln_done[-1]], dma=True, slot="xnr")
            for ecol in range(2):
                mmy = None
                for ic in range(32):
                    mmy = p.run("t", lambda e, ic=ic, ecol=ecol, ib=ib: e.matmul(
                        PS[ecol][:, 0:512], itb[ib][:, ic * 128:(ic + 1) * 128],
                        wo_sb[:, ic * 1024 + ecol * 512: ic * 1024 + (ecol + 1) * 512],
                        start=(ic == 0), stop=False),
                        deps=([li, ev_c3, ev_y, prev_acc] if ic == 0 else []), ev=False)
                mmy = p.run("t", lambda e, ecol=ecol: e.matmul(
                    PS[ecol][:, 0:512], onesb_r[0:1, :], bob[0:1, ecol * 512:(ecol + 1) * 512],
                    start=False, stop=True), deps=[ev_c3])
                ev_y = p.run("v", lambda e, ecol=ecol: e.scalar_tensor_tensor(
                    yt[:, ecol * 512:(ecol + 1) * 512], xnrow[:, ecol * 512:(ecol + 1) * 512],
                    1.0, PS[ecol][:, 0:512], op0=ALU.mult, op1=ALU.add), deps=[mmy, lx])
            rsf = emit_ln_stats(p, tiles, ev_y, yt, prev_lnf)
            n1 = p.run("v", lambda e: e.tensor_scalar(
                tiles["scr0"][:], yt[:], tiles["mean"][:, 0:1], tiles["rstd"][:, 0:1],
                op0=ALU.subtract, op1=ALU.mult), deps=[rsf])
            n2 = p.run("v", lambda e: e.tensor_tensor(tiles["scr0"][:], tiles["scr0"][:],
                                                      glnb[:], op=ALU.mult), deps=[n1, ev_c3])
            n3 = p.run("v", lambda e: e.tensor_tensor(tiles["xpx"][0][:], tiles["scr0"][:],
                                                      blnb[:], op=ALU.add),
                       deps=[n2, prev_acc])
            prev_lnf = n3
            ac = p.run("g", lambda e, ttile=ttile: e.dma_start(
                out=x_out[ttile * 128:(ttile + 1) * 128, :], in_=tiles["xpx"][0][:],
                accum_op=ALU.add),
                deps=[n3] + (attn_accums[-1:] + ev_init if ttile == 0 else []),
                dma=True, slot="fa")
            prev_acc = ac
            ffn_accums.append(ac)
            ev_itb[ib] = ev_y

        if fused is None:
            p.barrier_all("y")
            p.emit()
        st4.close()
    return nc


def prep_block1(i, x_shards):
    E1, I1 = 4, 4096
    wwT = np.ascontiguousarray(
        _f32(i["L1_attn_W_write"]).T.reshape(8, 128, 1024).transpose(1, 0, 2).reshape(128, 8192))
    wrT = np.ascontiguousarray(
        _f32(i["L1_attn_W_read"]).T.reshape(8, 128, 1024).transpose(1, 0, 2).reshape(128, 8192))
    dec = np.array([[float(np.asarray(i["L1_attn_decay"])),
                     float(np.asarray(i["L1_attn_log_alpha"]))]], dtype=np.float32)
    W1 = _f32(i["L1_ffn_W1"])
    w1r = np.ascontiguousarray(
        W1.reshape(E1, 8, 128, 32, 128).transpose(3, 2, 0, 1, 4).reshape(32, 128, E1 * 8 * 128))
    b1 = _f32(i["L1_ffn_b1"])
    Wo = _f32(i["L1_ffn_Wo"])
    wor = np.ascontiguousarray(Wo.reshape(32, 128, 1024).transpose(1, 0, 2).reshape(128, 32 * 1024))
    bo = _f32(i["L1_ffn_bo"]).reshape(1, D)
    wrouter = np.ascontiguousarray(
        _f32(i["L1_ffn_W_router"]).reshape(8, 128, 4).transpose(1, 0, 2).reshape(128, 32))
    gln = np.ascontiguousarray(np.broadcast_to(_f32(i["L1_ffn_ln_gamma"])[None, :], (128, D)))
    bln = np.ascontiguousarray(np.broadcast_to(_f32(i["L1_ffn_ln_beta"])[None, :], (128, D)))
    identf = np.eye(128, dtype=np.float32)
    onesf = np.ones((128, 512), dtype=np.float32)
    sel4 = np.zeros((4, 512), np.float32)
    for e_ in range(4):
        sel4[e_, e_ * 128:(e_ + 1) * 128] = 1.0
    common = dict(wwT=wwT, wrT=wrT, dec=dec, w1r=w1r, b1e=b1, wor=wor, bo1=bo,
                  wrouter=wrouter, sel4=sel4, gln=gln, bln=bln, identf=identf, onesf=onesf)
    return [dict(common, x_in=x_shards[c]) for c in range(NCORES)]


def run_block1(i, x):
    shards = _shard_x(x)
    nc = build_block1()
    maps = prep_block1(i, shards)
    res = run_bass_kernel_spmd(nc, maps, core_ids=list(range(NCORES)))
    return _gather_x([r["x_out"] for r in res.results])


def build_block2(fused=None):
    nc = (bass.Bass("TRN2", target_bir_lowering=False) if fused is None
          else _NCP(fused[0], fused[2]))
    DS = 64
    CC = 16
    NCHU = 128

    if fused is None:
        x_in = nc.dram_tensor("x_in", [TOK, D], F32, kind="ExternalInput").ap()
        x_out = nc.dram_tensor("x_out", [TOK, D], F32, kind="ExternalOutput").ap()
    else:
        x_in, x_out = fused[3], fused[4]
    xn_hbm = nc.dram_tensor("xn_hbm", [TOK, D], BF16, kind="Internal").ap()
    xmid_hbm = nc.dram_tensor("xmid_hbm", [TOK, D], F32, kind="Internal").ap()
    dwn16 = nc.dram_tensor("dwn16", [16384, 1024], BF16, kind="Internal").ap()
    up16 = nc.dram_tensor("up16", [16384, 1024], BF16, kind="Internal").ap()
    guT_hbm = nc.dram_tensor("guT_hbm", [32, 128, TOK], BF16, kind="Internal").ap()

    w1T_h = nc.dram_tensor("w1T2", [128, 8 * 1024], F32, kind="ExternalInput").ap()
    v1T_h = nc.dram_tensor("v1T2", [128, 8 * 1024], F32, kind="ExternalInput").ap()
    w2T_h = nc.dram_tensor("w2T2", [128, 8 * 1024], F32, kind="ExternalInput").ap()
    wb12_h = nc.dram_tensor("wb12", [128, 16], F32, kind="ExternalInput").ap()
    w2br_h = nc.dram_tensor("w2br", [1, D], F32, kind="ExternalInput").ap()
    convw_h = nc.dram_tensor("convw", [128, 24], F32, kind="ExternalInput").ap()
    A_h = nc.dram_tensor("Am", [64, 64], F32, kind="ExternalInput").ap()
    BmT_h = nc.dram_tensor("BmT", [128, 8 * 64], F32, kind="ExternalInput").ap()
    CmT_h = nc.dram_tensor("CmT", [64, 1024], F32, kind="ExternalInput").ap()
    ln1g_h = nc.dram_tensor("ln1g2", [128, D], F32, kind="ExternalInput").ap()
    ln1b_h = nc.dram_tensor("ln1b2", [128, D], F32, kind="ExternalInput").ap()
    ln2g_h = nc.dram_tensor("ln2g2", [128, D], F32, kind="ExternalInput").ap()
    ln2b_h = nc.dram_tensor("ln2b2", [128, D], F32, kind="ExternalInput").ap()
    ff1T_h = nc.dram_tensor("ff1T2", [128, 8 * 2048], F32, kind="ExternalInput").ap()
    ff1b_h = nc.dram_tensor("ff1b2", [128, 16], F32, kind="ExternalInput").ap()
    ff2T_h = nc.dram_tensor("ff2T2", [128, 16 * 1024], F32, kind="ExternalInput").ap()
    ff2br_h = nc.dram_tensor("ff2br", [1, D], F32, kind="ExternalInput").ap()
    identf = nc.dram_tensor("identf", [128, 128], F32, kind="ExternalInput").ap()
    onesf = nc.dram_tensor("onesf", [128, 512], F32, kind="ExternalInput").ap()
    wqT_h = nc.dram_tensor("wqT2", [128, 8 * 512], F32, kind="ExternalInput").ap()
    keysT_h = nc.dram_tensor("keysT2", [128, 8 * 128], F32, kind="ExternalInput").ap()
    dn_h = nc.dram_tensor("dn_f32", [16384, 1024], F32, kind="ExternalInput").ap()
    upe_h = nc.dram_tensor("up_f32", [16384, 1024], F32, kind="ExternalInput").ap()
    wg_h = nc.dram_tensor("wg_r", [32, 128, 8 * 128], F32, kind="ExternalInput").ap()
    wu_h = nc.dram_tensor("wu_r", [32, 128, 8 * 128], F32, kind="ExternalInput").ap()
    wd_h = nc.dram_tensor("wd_r", [32, 128, 1024], F32, kind="ExternalInput").ap()

    with ExitStack() as st:
        def sb(name, shape, dt=BF16):
            return st.enter_context(nc.sbuf_tensor(name, shape, dt))

        tiles = mk_tiles(st, nc, with_xpx=False)
        identb = sb("identb2", [128, 128])
        onesb_r = sb("onesb_r2", [1, 128])

        PS = [st.enter_context(nc.psum_tensor("b2ps%d" % i, [128, 512], F32)) for i in range(6)]
        PST = [st.enter_context(nc.psum_tensor("b2pst%d" % i, [128, 1024], BF16)) for i in range(2)]

        p = Prog(nc) if fused is None else fused[1]
        tiles["eps_ev"] = p.run("v", lambda e: e.memset(tiles["eps"][:], 1e-5))

        for ld_fn in [
            lambda e: e.dma_start(out=identb[:], in_=identf),
            lambda e: e.dma_start(out=onesb_r[:], in_=onesf[0:1, 0:128]),
        ]:
            ev_const = p.run("g", ld_fn, dma=True, slot="const")

        ev_init = p.run("y", lambda e: e.dma_start(out=x_out, in_=x_in), dma=True, slot="init")
        ln_done = emit_ln_prologue(p, tiles, x_in, None, identb, PST, dep0=[ev_const],
                                   xn_hbm=xn_hbm)

        cvt_ev = None
        cvt_b = [None, None]
        for (srcT, dstT) in [(dn_h, dwn16), (upe_h, up16)]:
            for blk in range(32):
                bsel = blk % 2
                lde = p.run("g", lambda e, srcT=srcT, blk=blk: e.dma_start(
                    out=tiles["xnb"][blk % 2][:, 0:1024].rearrange("p (a b) -> p a b", a=1)
                    if False else tiles["xnb"][blk % 2][:],
                    in_=srcT[blk * 512:blk * 512 + 128, :]), deps=[cvt_b[bsel], ln_done[-1]],
                    dma=True, slot="cv%d" % bsel)
                st_ev = lde
                for sub in range(4):
                    if sub > 0:
                        lde = p.run("g", lambda e, srcT=srcT, blk=blk, sub=sub, bsel=bsel: e.dma_start(
                            out=tiles["xnb"][bsel][:],
                            in_=srcT[blk * 512 + sub * 128: blk * 512 + (sub + 1) * 128, :]),
                            deps=[st_ev], dma=True, slot="cv%d" % bsel)
                    st_ev = p.run("y", lambda e, dstT=dstT, blk=blk, sub=sub, bsel=bsel: e.dma_start(
                        out=dstT[blk * 512 + sub * 128: blk * 512 + (sub + 1) * 128, :],
                        in_=tiles["xnb"][bsel][:]), deps=[lde], dma=True, slot="cw%d" % bsel)
                cvt_b[bsel] = st_ev
                cvt_ev = st_ev

        st2 = ExitStack()

        def sb2(name, shape, dt=BF16):
            return st2.enter_context(nc.sbuf_tensor(name, shape, dt))
        w1T = sb2("w1T_sb2", [128, 8 * 1024])
        v1T = sb2("v1T_sb2", [128, 8 * 1024])
        w2T = sb2("w2T_sb2", [128, 8 * 1024])
        wb12 = sb2("wb12_sb", [128, 16], F32)
        w2br = sb2("w2br_sb", [1, D])
        convw = sb2("convw_sb", [128, 24], F32)
        A_sb = sb2("A_sb", [64, 64])
        BmT = sb2("BmT_sb", [128, 8 * 64])
        CmT = sb2("CmT_sb", [64, 1024])
        ln1g = sb2("ln1g_sb", [128, D])
        ln1b = sb2("ln1b_sb", [128, D])
        hT = sb2("hT2", [128, 8 * S])
        x1T = sb2("x1T", [128, 8 * S])
        cnv = sb2("cnv", [128, S])
        APT = sb2("APT", [64, 17 * 64])
        SQT = sb2("SQT", [64, 7 * 64])
        SQN = sb2("SQN", [64, 7 * 64])
        TT2 = sb2("TT2_sb", [128, 8 * 1024])
        uarr = sb2("uarr", [128, 8 * 128])
        uT = sb2("uT", [64, S])
        intraT = sb2("intraT", [64, S])
        Dcum = sb2("Dcum", [64, 128])
        Hin = sb2("Hin", [64, 128])
        at3 = [sb2("at3_%d" % i, [128, 512], F32) for i in range(2)]
        xnrow2 = sb2("xnrow2", [128, D])

        ev_c2 = None
        for ld_fn in [
            lambda e: e.dma_start(out=w1T[:], in_=w1T_h),
            lambda e: e.dma_start(out=v1T[:], in_=v1T_h),
            lambda e: e.dma_start(out=w2T[:], in_=w2T_h),
            lambda e: e.dma_start(out=wb12[:], in_=wb12_h),
            lambda e: e.dma_start(out=w2br[:], in_=w2br_h),
            lambda e: e.dma_start(out=convw[:], in_=convw_h),
            lambda e: e.dma_start(out=A_sb[:], in_=A_h),
            lambda e: e.dma_start(out=BmT[:], in_=BmT_h),
            lambda e: e.dma_start(out=CmT[:], in_=CmT_h),
            lambda e: e.dma_start(out=ln1g[:], in_=ln1g_h),
            lambda e: e.dma_start(out=ln1b[:], in_=ln1b_h),
        ]:
            ev_c2 = p.run("g", ld_fn, dma=True, slot="const2")

        ev_ap = p.run("v", lambda e: e.tensor_copy(APT[0:64, 0:64], identb[0:64, 0:64]),
                      deps=[ev_const])
        ap_evs = [ev_ap]
        for d_ in range(16):
            mm = p.run("t", lambda e, d_=d_: e.matmul(
                PS[0][0:64, 0:64], A_sb[:], APT[0:64, d_ * 64:(d_ + 1) * 64],
                start=True, stop=True), deps=[ev_c2, ev_ap])
            ev_ap = p.run("v", lambda e, d_=d_: e.tensor_copy(
                APT[0:64, (d_ + 1) * 64:(d_ + 2) * 64], PS[0][0:64, 0:64]), deps=[mm])
            ap_evs.append(ev_ap)
        sqt_e = p.run("v", lambda e: e.tensor_copy(SQT[0:64, 0:64], APT[0:64, 16 * 64:]),
                      deps=[ev_ap])
        mm = p.run("t", lambda e: e.matmul(PS[1][0:64, 0:64], APT[0:64, 15 * 64:16 * 64],
                                           A_sb[:], start=True, stop=True), deps=[ev_ap])
        sqn_e = p.run("v", lambda e: e.tensor_copy(SQN[0:64, 0:64], PS[1][0:64, 0:64]), deps=[mm])
        for l_ in range(6):
            mm1 = p.run("t", lambda e, l_=l_: e.matmul(
                PS[0][0:64, 0:64], SQN[0:64, l_ * 64:(l_ + 1) * 64],
                SQT[0:64, l_ * 64:(l_ + 1) * 64], start=True, stop=True),
                deps=[sqt_e, sqn_e])
            sqt_e = p.run("v", lambda e, l_=l_: e.tensor_copy(
                SQT[0:64, (l_ + 1) * 64:(l_ + 2) * 64], PS[0][0:64, 0:64]), deps=[mm1])
            mm2 = p.run("t", lambda e, l_=l_: e.matmul(
                PS[1][0:64, 0:64], SQT[0:64, l_ * 64:(l_ + 1) * 64],
                SQN[0:64, l_ * 64:(l_ + 1) * 64], start=True, stop=True),
                deps=[sqt_e])
            sqn_e = p.run("v", lambda e, l_=l_: e.tensor_copy(
                SQN[0:64, (l_ + 1) * 64:(l_ + 2) * 64], PS[1][0:64, 0:64]), deps=[mm2])
        ztt = p.run("v", lambda e: e.memset(TT2[:], 0.0), deps=[ln_done[-1]])
        ev_tt = ztt
        for j_ in range(16):
            for c_ in range(j_, 16):
                ev_tt = p.run("v", lambda e, j_=j_, c_=c_: e.tensor_copy(
                    TT2[(j_ % 2) * 64:(j_ % 2) * 64 + 64,
                        (j_ // 2) * 1024 + c_ * 64:(j_ // 2) * 1024 + (c_ + 1) * 64],
                    APT[0:64, (c_ - j_) * 64:(c_ - j_ + 1) * 64]), deps=[ztt, ev_ap])

        attn_accums = []
        ev_prev = [None]
        ev_x1 = None
        ev_w2acc = None
        for b in range(NB):
            ev_h = None
            prev_ln1 = None
            for tl in range(16):
                row0 = b * S + tl * 128
                lx = p.run("y", lambda e, row0=row0: e.dma_start(
                    out=xnrow2[:], in_=xn_hbm[row0:row0 + 128, :]),
                    deps=[ln_done[-1], prev_ln1] + ev_prev, dma=True, slot="lxr")
                cpf = p.run("v", lambda e: e.tensor_copy(tiles["xt"][0][:], xnrow2[:]), deps=[lx])
                rsx = emit_ln_stats(p, tiles, cpf, tiles["xt"][0], prev_ln1)
                n1 = p.run("v", lambda e: e.tensor_scalar(
                    tiles["scr0"][:], tiles["xt"][0][:], tiles["mean"][:, 0:1],
                    tiles["rstd"][:, 0:1], op0=ALU.subtract, op1=ALU.mult), deps=[rsx])
                n2 = p.run("v", lambda e: e.tensor_tensor(tiles["scr0"][:], tiles["scr0"][:],
                                                          ln1g[:], op=ALU.mult), deps=[n1, ev_c2])
                n3 = p.run("v", lambda e: e.tensor_tensor(tiles["xnb"][0][:], tiles["scr0"][:],
                                                          ln1b[:], op=ALU.add), deps=[n2])
                prev_ln1 = n3
                for c in range(NC):
                    pi = c % 2
                    tr = p.run("t", lambda e, c=c, pi=pi: e.transpose(
                        PST[pi][:, 0:128], tiles["xnb"][0][:, c * 128:(c + 1) * 128], identb[:]),
                        deps=[n3, ev_h])
                    ev_h = p.run("v", lambda e, c=c, tl=tl, pi=pi: e.tensor_copy(
                        hT[:, c * S + tl * 128: c * S + (tl + 1) * 128], PST[pi][:, 0:128]),
                        deps=[tr])
            ev_x1 = None
            for oc in range(NC):
                for qc in range(4):
                    mm = None
                    for c in range(NC):
                        mm = p.run("t", lambda e, oc=oc, qc=qc, c=c: e.matmul(
                            PS[oc % 2][:, 0:512], w1T[:, c * 1024 + oc * 128: c * 1024 + (oc + 1) * 128],
                            hT[:, c * S + qc * 512: c * S + (qc + 1) * 512],
                            start=(c == 0), stop=(c == NC - 1)),
                            deps=([ev_h, ev_c2, ev_x1] if c == 0 else []), ev=(c == NC - 1))
                    ev_x1 = p.run("s", lambda e, oc=oc, qc=qc: e.activation(
                        out=x1T[:, oc * S + qc * 512: oc * S + (qc + 1) * 512],
                        in_=PS[oc % 2][:, 0:512], func=AF.Identity, bias=wb12[:, oc:oc + 1]),
                        deps=[mm])
            ev_cv = None
            for oc in range(NC):
                c1 = p.run("v", lambda e, oc=oc: e.tensor_scalar(
                    cnv[:], x1T[:, oc * S:(oc + 1) * S], convw[:, 8 + oc:8 + oc + 1], None,
                    op0=ALU.mult), deps=[ev_x1, ev_c2, ev_cv])
                c2_ = p.run("v", lambda e, oc=oc: e.scalar_tensor_tensor(
                    cnv[:, 1:S], x1T[:, oc * S: oc * S + S - 1], convw[:, oc:oc + 1],
                    cnv[:, 1:S], op0=ALU.mult, op1=ALU.add), deps=[c1])
                c3_ = p.run("v", lambda e, oc=oc: e.scalar_tensor_tensor(
                    cnv[:, 0:S - 1], x1T[:, oc * S + 1: oc * S + S], convw[:, 16 + oc:16 + oc + 1],
                    cnv[:, 0:S - 1], op0=ALU.mult, op1=ALU.add), deps=[c2_])
                ev_cv = p.run("s", lambda e, oc=oc: e.activation(
                    out=x1T[:, oc * S:(oc + 1) * S], in_=cnv[:], func=AF.Silu), deps=[c3_])
            ev_u = None
            for qc in range(4):
                mm = None
                for c in range(NC):
                    mm = p.run("t", lambda e, qc=qc, c=c: e.matmul(
                        PS[2][0:64, 0:512], BmT[:, c * 64:(c + 1) * 64],
                        x1T[:, c * S + qc * 512: c * S + (qc + 1) * 512],
                        start=(c == 0), stop=(c == NC - 1)),
                        deps=([ev_cv, ev_c2, ev_u] if c == 0 else []), ev=(c == NC - 1))
                ev_u = p.run("v", lambda e, qc=qc: e.tensor_copy(
                    uT[0:64, qc * 512:(qc + 1) * 512], PS[2][0:64, 0:512]), deps=[mm])
            ev_ua = None
            for cpos in range(16):
                ev_ua = p.run("v", lambda e, cpos=cpos: e.tensor_copy(
                    uarr[(cpos % 2) * 64:(cpos % 2) * 64 + 64,
                         (cpos // 2) * 128:(cpos // 2) * 128 + 128],
                    uT[0:64, :].rearrange("p (a b) -> p a b", b=16)[:, :, cpos]),
                    deps=[ev_u, ev_ua] + ([ev_w2acc] if cpos == 0 else []))
            ev_ia = None
            for oc2 in range(8):
                mm = None
                for pc in range(8):
                    mm = p.run("t", lambda e, oc2=oc2, pc=pc: e.matmul(
                        PS[3][:, 0:128], TT2[:, pc * 1024 + oc2 * 128: pc * 1024 + (oc2 + 1) * 128]
                        if False else TT2[:, pc * 1024 + oc2 * 128: pc * 1024 + (oc2 + 1) * 128],
                        uarr[:, pc * 128:(pc + 1) * 128],
                        start=(pc == 0), stop=(pc == 7)),
                        deps=([ev_ua, ev_tt, ev_ia] if pc == 0 else []), ev=(pc == 7))
                for half in range(2):
                    ev_ia = p.run("v", lambda e, oc2=oc2, half=half: e.tensor_copy(
                        intraT[0:64, :].rearrange("p (a b) -> p a b", b=16)[:, :, oc2 * 2 + half],
                        PS[3][half * 64:half * 64 + 64, 0:128]), deps=[mm])
            dc = p.run("v", lambda e: e.tensor_copy(
                Dcum[0:64, :], intraT[0:64, :].rearrange("p (a b) -> p a b", b=16)[:, :, 15]),
                deps=[ev_ia])
            for l_ in range(7):
                sh = 1 << l_
                mm = p.run("t", lambda e, l_=l_, sh=sh: e.matmul(
                    PS[0][0:64, 0:128 - sh], SQT[0:64, l_ * 64:(l_ + 1) * 64],
                    Dcum[0:64, 0:128 - sh], start=True, stop=True), deps=[dc, sqt_e])
                dc = p.run("v", lambda e, l_=l_, sh=sh: e.tensor_tensor(
                    Dcum[0:64, sh:128], Dcum[0:64, sh:128], PS[0][0:64, 0:128 - sh],
                    op=ALU.add), deps=[mm])
            hi1 = p.run("v", lambda e: e.tensor_copy(Hin[0:64, 1:128], Dcum[0:64, 0:127]),
                        deps=[dc])
            hi2 = p.run("v", lambda e: e.memset(Hin[0:64, 0:1], 0.0), deps=[hi1])
            ev_it = hi2
            for cpos in range(16):
                mm = p.run("t", lambda e, cpos=cpos: e.matmul(
                    PS[1][:, 0:64], Hin[0:64, :], APT[0:64, (cpos + 1) * 64:(cpos + 2) * 64],
                    start=True, stop=True), deps=[hi2, ev_it])
                cpb = p.run("v", lambda e: e.tensor_copy(tiles["xnb"][1][:, 0:64], PS[1][:, 0:64]),
                            deps=[mm])
                tr = p.run("t", lambda e: e.transpose(PST[0][0:64, 0:128],
                                                      tiles["xnb"][1][:, 0:64], identb[:]),
                           deps=[cpb])
                ev_it = p.run("v", lambda e, cpos=cpos: e.tensor_tensor(
                    intraT[0:64, :].rearrange("p (a b) -> p a b", b=16)[:, :, cpos],
                    intraT[0:64, :].rearrange("p (a b) -> p a b", b=16)[:, :, cpos],
                    PST[0][0:64, 0:128], op=ALU.add), deps=[tr])
            hcv = ev_it
            ev_vb = None
            for oc in range(NC):
                for qc in range(4):
                    mm = None
                    for c in range(NC):
                        mm = p.run("t", lambda e, oc=oc, qc=qc, c=c: e.matmul(
                            PS[4][:, 0:512], v1T[:, c * 1024 + oc * 128: c * 1024 + (oc + 1) * 128],
                            hT[:, c * S + qc * 512: c * S + (qc + 1) * 512],
                            start=(c == 0), stop=(c == NC - 1)),
                            deps=([ev_h, ev_c2, ev_vb, ev_u, ev_ua] if c == 0 else []),
                            ev=(c == NC - 1))
                    ev_vb = p.run("s", lambda e, oc=oc, qc=qc: e.activation(
                        out=x1T[:, oc * S + qc * 512: oc * S + (qc + 1) * 512],
                        in_=PS[4][:, 0:512], func=AF.Silu, bias=wb12[:, 8 + oc:8 + oc + 1]),
                        deps=[mm])
            ev_g = None
            for oc in range(NC):
                for qc in range(4):
                    mm = p.run("t", lambda e, oc=oc, qc=qc: e.matmul(
                        PS[5][:, 0:512], CmT[:, oc * 128:(oc + 1) * 128],
                        intraT[0:64, qc * 512:(qc + 1) * 512], start=True, stop=True),
                        deps=[hcv, ev_c2, ev_g, ev_vb])
                    ev_g = p.run("v", lambda e, oc=oc, qc=qc: e.tensor_tensor(
                        hT[:, oc * S + qc * 512: oc * S + (qc + 1) * 512],
                        x1T[:, oc * S + qc * 512: oc * S + (qc + 1) * 512],
                        PS[5][:, 0:512], op=ALU.mult), deps=[mm, ev_vb])
            sm1_prev = None
            ev_w2acc = None
            for tl in range(16):
                row0 = b * S + tl * 128
                lx = p.run("y", lambda e, row0=row0: e.dma_start(
                    out=xnrow2[:], in_=xn_hbm[row0:row0 + 128, :]),
                    deps=[ev_w2acc, ln_done[-1], prev_ln1, sm1_prev], dma=True, slot="lxr")
                for ecol in range(2):
                    mm = None
                    for c in range(NC):
                        mm = p.run("t", lambda e, tl=tl, ecol=ecol, c=c: e.matmul(
                            PS[ecol][:, 0:512], hT[:, c * S + tl * 128: c * S + (tl + 1) * 128],
                            w2T[:, c * 1024 + ecol * 512: c * 1024 + (ecol + 1) * 512],
                            start=(c == 0), stop=False),
                            deps=([ev_g, ev_w2acc] if c == 0 else []), ev=False)
                    mm = p.run("t", lambda e, ecol=ecol: e.matmul(
                        PS[ecol][:, 0:512], onesb_r[0:1, :], w2br[0:1, ecol * 512:(ecol + 1) * 512],
                        start=False, stop=True), deps=[ev_c2])
                    sm1 = p.run("v", lambda e, ecol=ecol, tl=tl: e.tensor_tensor(
                        at3[ecol][:], xnrow2[:, ecol * 512:(ecol + 1) * 512], PS[ecol][:, 0:512],
                        op=ALU.add), deps=[mm, lx])
                    sm1_prev = sm1
                    sm2 = p.run("y", lambda e, row0=row0, ecol=ecol: e.dma_start(
                        out=xmid_hbm[row0:row0 + 128, ecol * 512:(ecol + 1) * 512],
                        in_=at3[ecol][:]), deps=[sm1], dma=True, slot="xm%d" % ecol)
                    ev_w2acc = p.run("g", lambda e, row0=row0, ecol=ecol: e.dma_start(
                        out=x_out[row0:row0 + 128, ecol * 512:(ecol + 1) * 512],
                        in_=at3[ecol][:], accum_op=ALU.add),
                        deps=[sm2, ev_init], dma=True, slot="am%d" % ecol)
                    attn_accums.append(ev_w2acc)
            ev_prev = [ev_w2acc, ev_g, ev_it]

        ff1T = w1T
        ev_f0 = None
        for ld_fn in [
            lambda e: e.dma_start(out=w1T[:], in_=ff1T_h[:, 0:8192]),
            lambda e: e.dma_start(out=v1T[:], in_=ff1T_h[:, 8192:16384]),
            lambda e: e.dma_start(out=wb12[:], in_=ff1b_h),
            lambda e: e.dma_start(out=w2br[:], in_=ff2br_h),
            lambda e: e.dma_start(out=ln1g[:], in_=ln2g_h),
            lambda e: e.dma_start(out=ln1b[:], in_=ln2b_h),
            lambda e: e.dma_start(out=w2T[:], in_=ff2T_h[:, 0:8192]),
        ]:
            ev_f0 = p.run("g", ld_fn, deps=list(ev_prev) + [ev_c2], dma=True, slot="const3")
        ev_f1 = p.run("g", lambda e: e.dma_start(out=hT[:, 8192:16384], in_=ff2T_h[:, 8192:16384]),
                      deps=list(ev_prev) + [ev_f0], dma=True, slot="const3")

        ffn_accums = []
        prev_t = None
        prev_ac = None
        for b in range(NB):
            for th in range(2):
                ev_l2T = None
                prev_lnx = None
                for tl in range(8):
                    row0 = b * S + th * 1024 + tl * 128
                    lx = p.run("y", lambda e, row0=row0: e.dma_start(
                        out=tiles["xt"][0][:], in_=xmid_hbm[row0:row0 + 128, :]),
                        deps=[prev_lnx, ev_f1, prev_t], dma=True, slot="lx2")
                    rsx = emit_ln_stats(p, tiles, lx, tiles["xt"][0], prev_lnx)
                    n1 = p.run("v", lambda e: e.tensor_scalar(
                        tiles["scr0"][:], tiles["xt"][0][:], tiles["mean"][:, 0:1],
                        tiles["rstd"][:, 0:1], op0=ALU.subtract, op1=ALU.mult), deps=[rsx])
                    n2 = p.run("v", lambda e: e.tensor_tensor(
                        tiles["scr0"][:], tiles["scr0"][:], ln1g[:], op=ALU.mult),
                        deps=[n1, ev_f0])
                    n3 = p.run("v", lambda e: e.tensor_tensor(
                        tiles["xnb"][0][:], tiles["scr0"][:], ln1b[:], op=ALU.add), deps=[n2])
                    prev_lnx = n3
                    for c in range(NC):
                        pi = c % 2
                        tr = p.run("t", lambda e, c=c, pi=pi: e.transpose(
                            PST[pi][:, 0:128], tiles["xnb"][0][:, c * 128:(c + 1) * 128],
                            identb[:]), deps=[n3, ev_l2T])
                        ev_l2T = p.run("v", lambda e, c=c, tl=tl, pi=pi: e.tensor_copy(
                            hT[:, c * 1024 + tl * 128: c * 1024 + (tl + 1) * 128],
                            PST[pi][:, 0:128]), deps=[tr])
                ev_ff1 = None
                for oc in range(16):
                    for qc in range(2):
                        mm = None
                        for c in range(NC):
                            mm = p.run("t", lambda e, oc=oc, qc=qc, c=c: e.matmul(
                                PS[oc % 2][:, 0:512],
                                (w1T if oc < 8 else v1T)[:, ((oc % 8) * 8 + c) * 128:
                                                         ((oc % 8) * 8 + c) * 128 + 128],
                                hT[:, c * 1024 + qc * 512: c * 1024 + (qc + 1) * 512],
                                start=(c == 0), stop=(c == NC - 1)),
                                deps=([ev_l2T, ev_f0, ev_ff1, prev_t] if c == 0 else []),
                                ev=(c == NC - 1))
                        ev_ff1 = p.run("s", lambda e, oc=oc, qc=qc: e.activation(
                            out=x1T[:, oc * 1024 + qc * 512: oc * 1024 + (qc + 1) * 512],
                            in_=PS[oc % 2][:, 0:512], func=AF.Gelu, bias=wb12[:, oc:oc + 1]),
                            deps=[mm])
                for tl in range(8):
                    row0 = b * S + th * 1024 + tl * 128
                    for ecol in range(2):
                        mm = None
                        for hc in range(16):
                            off2 = 0 if hc < 8 else 8192
                            w2src = w2T if hc < 8 else hT
                            mm = p.run("t", lambda e, tl=tl, ecol=ecol, hc=hc, w2src=w2src,
                                       off2=off2: e.matmul(
                                PS[2 + ecol][:, 0:512],
                                x1T[:, hc * 1024 + tl * 128: hc * 1024 + (tl + 1) * 128],
                                w2src[:, off2 + (hc % 8) * 1024 + ecol * 512:
                                      off2 + (hc % 8) * 1024 + (ecol + 1) * 512],
                                start=(hc == 0), stop=False),
                                deps=([ev_ff1, prev_ac, ev_f1] if hc == 0 else []), ev=False)
                        mm = p.run("t", lambda e, ecol=ecol: e.matmul(
                            PS[2 + ecol][:, 0:512], onesb_r[0:1, :],
                            w2br[0:1, ecol * 512:(ecol + 1) * 512],
                            start=False, stop=True), deps=[ev_f0])
                        cpe = p.run("v", lambda e, ecol=ecol: e.tensor_copy(
                            at3[ecol][:], PS[2 + ecol][:, 0:512]), deps=[mm, prev_ac])
                        prev_ac = p.run("g", lambda e, row0=row0, ecol=ecol: e.dma_start(
                            out=x_out[row0:row0 + 128, ecol * 512:(ecol + 1) * 512],
                            in_=at3[ecol][:], accum_op=ALU.add),
                            deps=[cpe, attn_accums[-1], ev_init], dma=True,
                            slot="fa%d" % ecol)
                        ffn_accums.append(prev_ac)
                prev_t = prev_ac


        st2.close()
        stp = ExitStack()

        def sbp(name, shape, dt=BF16):
            return stp.enter_context(nc.sbuf_tensor(name, shape, dt))
        xnT = sbp("xnT2", [128, NC * TOK])
        qT = sbp("qT2", [128, 4 * TOK])
        keysT = sbp("keysT_sb", [128, 8 * 128])
        wqT = sbp("wqT_sb", [128, 8 * 512])
        simev = sbp("simev", [128, 128])
        simtok = sbp("simtok", [128, 128])
        sx = [sbp("sx%d" % i, [128, 8], F32) for i in range(2)]
        ixv = [sbp("ix%d" % i, [128, 8], mybir.dt.uint32) for i in range(2)]
        ixf = [sbp("ixf%d" % i, [128, 8], F32) for i in range(2)]
        ix0m = sbp("ix0m", [128, 8], F32)
        alls = sbp("alls", [128, 64], F32)
        alli = sbp("alli", [128, 64], F32)
        eqs = sbp("eqs", [128, 64], F32)
        sc8 = sbp("sc8", [128, 8], F32)
        nm8 = sbp("nm8", [128, 1], F32)
        sm8 = sbp("sm8", [128, 8], F32)
        w8 = sbp("w8", [128, 8], F32)
        idxf = sbp("idxf", [128, 8], F32)
        idxi = sbp("idxi", [128, 8], I32)
        Dg = [sbp("Dg%d" % i, [128, 1024]) for i in range(2)]
        Ug8 = sbp("Ug8", [128, 8 * 1024])
        prodscr = sbp("prodscr", [128, 1024])
        accp = [sbp("accp%d" % i, [128, 1024], F32) for i in range(2)]
        xnrow3 = sbp("xnrow3", [128, 1024])
        wgb = [sbp("wgb%d" % i, [128, 1024]) for i in range(2)]
        wub = [sbp("wub%d" % i, [128, 1024]) for i in range(2)]
        gscr = sbp("gscr", [128, 512])
        gub = sbp("gub", [128, 512])

        ev_cp = None
        for ld_fn in [
            lambda e: e.dma_start(out=keysT[:], in_=keysT_h),
            lambda e: e.dma_start(out=wqT[:], in_=wqT_h),
        ]:
            ev_cp = p.run("g", ld_fn, deps=[prev_ac, prev_t], dma=True, slot="constp")

        ev_xnT = None
        ev_xl = None
        for tt in range(NT):
            lx = p.run("y", lambda e, tt=tt: e.dma_start(
                out=xnrow3[:], in_=xn_hbm[tt * 128:(tt + 1) * 128, :]),
                deps=[ev_xl, ln_done[-1], prev_ac, prev_t], dma=True, slot="pxr")
            ev_xl = lx
            for c in range(NC):
                tr = p.run("t", lambda e, c=c: e.transpose(
                    PST[c % 2][:, 0:128], xnrow3[:, c * 128:(c + 1) * 128], identb[:]),
                    deps=[lx, ev_xnT])
                ev_xnT = p.run("v", lambda e, tt=tt, c=c: e.tensor_copy(
                    xnT[:, c * TOK + tt * 128: c * TOK + (tt + 1) * 128],
                    PST[c % 2][:, 0:128]), deps=[tr])
            ev_xl = ev_xnT

        ev_q = None
        for rc_ in range(4):
            for qc in range(8):
                mm = None
                for c in range(NC):
                    mm = p.run("t", lambda e, rc_=rc_, qc=qc, c=c: e.matmul(
                        PS[0][:, 0:512], wqT[:, c * 512 + rc_ * 128: c * 512 + (rc_ + 1) * 128],
                        xnT[:, c * TOK + qc * 512: c * TOK + (qc + 1) * 512],
                        start=(c == 0), stop=(c == NC - 1)),
                        deps=([ev_cp, ev_q, prev_ac, ev_xnT] if c == 0 else []),
                        ev=(c == NC - 1))
                ev_q = p.run("v", lambda e, rc_=rc_, qc=qc: e.tensor_copy(
                    qT[:, rc_ * TOK + qc * 512: rc_ * TOK + (qc + 1) * 512], PS[0][:, 0:512]),
                    deps=[mm])

        peer_accums = []
        ev_idx = None
        ev_dg = [None, None]
        ev_ug = None
        ev_acc = None
        ev_xr = ev_xnT
        prev_pa = ffn_accums[-1]
        for ttile in range(NT):
            t0 = ttile * 128
            lxr = p.run("y", lambda e, t0=t0: e.dma_start(
                out=xnrow3[:], in_=xn_hbm[t0:t0 + 128, :]), deps=[ev_xr, ln_done[-1]],
                dma=True, slot="pxr")
            za = p.run("v", lambda e: e.memset(accp[0][:], 0.0), deps=[ev_acc, prev_pa])
            acc_par = 0
            ev_accl = za
            for h in range(4):
                hb = (h % 2) * 64
                for pp in range(2):
                    rc_ = 2 * pp + h // 2
                    mm = p.run("t", lambda e, h=h, pp=pp, rc_=rc_, hb=hb, t0=t0: e.matmul(
                        PS[1][:, 0:128], keysT[hb:hb + 64, (2 * h + pp) * 128:(2 * h + pp + 1) * 128],
                        qT[hb:hb + 64, rc_ * TOK + t0: rc_ * TOK + t0 + 128],
                        start=True, stop=True), deps=[ev_q, ev_cp, ev_idx])
                    cpe = p.run("v", lambda e: e.tensor_copy(simev[:], PS[1][:, 0:128]), deps=[mm])
                    tr = p.run("t", lambda e: e.transpose(PST[0][:, 0:128], simev[:], identb[:]),
                               deps=[cpe])
                    cpe2 = p.run("v", lambda e: e.tensor_copy(simtok[:], PST[0][:, 0:128]),
                                 deps=[tr])
                    mx = p.run("v", lambda e, pp=pp: e.max(out=sx[pp][:], in_=simtok[:]),
                               deps=[cpe2])
                    mi = p.run("v", lambda e, pp=pp: e.max_index(
                        out=ixv[pp][:], in_max=sx[pp][:], in_values=simtok[:]), deps=[mx])
                    ev_idx = p.run("v", lambda e, pp=pp: e.tensor_copy(ixf[pp][:], ixv[pp][:]),
                                   deps=[mi])
                im = p.run("v", lambda e: e.tensor_scalar(ix0m[:], ixf[0][:], 128.0, None,
                                                          op0=ALU.mult), deps=[ev_idx])
                ev_cb = None
                for j0 in range(8):
                    c1_ = p.run("v", lambda e, j0=j0: e.tensor_scalar(
                        alls[:, j0 * 8:(j0 + 1) * 8], sx[1][:], sx[0][:, j0:j0 + 1], None,
                        op0=ALU.add), deps=[ev_idx, ev_cb])
                    ev_cb = p.run("v", lambda e, j0=j0: e.tensor_scalar(
                        alli[:, j0 * 8:(j0 + 1) * 8], ixf[1][:], ix0m[:, j0:j0 + 1], None,
                        op0=ALU.add), deps=[im, c1_])
                mx8 = p.run("v", lambda e: e.max(out=sc8[:], in_=alls[:]), deps=[ev_cb])
                ev_sel = None
                for j in range(8):
                    e1 = p.run("v", lambda e, j=j: e.tensor_scalar(
                        eqs[:], alls[:], sc8[:, j:j + 1], None, op0=ALU.is_ge),
                        deps=[mx8, ev_sel])
                    e2 = p.run("v", lambda e: e.tensor_tensor(eqs[:], eqs[:], alli[:],
                                                              op=ALU.mult), deps=[e1])
                    ev_sel = p.run("v", lambda e, j=j: e.reduce_max(
                        idxf[:, j:j + 1], eqs[:], axis=AX.X), deps=[e2])
                cvi = p.run("v", lambda e: e.tensor_copy(idxi[:], idxf[:]), deps=[ev_sel])
                rm8 = p.run("v", lambda e: e.reduce_max(nm8[:], sc8[:], axis=AX.X), deps=[mx8])
                ng8 = p.run("v", lambda e: e.tensor_scalar(nm8[:], nm8[:], -1.0, None,
                                                           op0=ALU.mult), deps=[rm8])
                ex8 = p.run("s", lambda e: e.activation(out=sm8[:], in_=sc8[:], func=AF.Exp,
                                                        bias=nm8[:, 0:1]), deps=[ng8])
                su8 = p.run("v", lambda e: e.reduce_sum(nm8[:], sm8[:], axis=AX.X), deps=[ex8])
                rc8 = p.run("v", lambda e: e.reciprocal(nm8[:], nm8[:]), deps=[su8])
                smn = p.run("v", lambda e: e.tensor_scalar(sm8[:], sm8[:], nm8[:, 0:1], None,
                                                           op0=ALU.mult), deps=[rc8])
                ev_w = None
                for j in range(8):
                    jb = j % 2
                    gd = p.run("g", lambda e, j=j, jb=jb: e.indirect_dma_start(
                        out=Dg[jb][:], out_offset=None, in_=dwn16,
                        in_offset=IndirectOffsetOnAxis(idxi[:, j:j + 1], 0)),
                        deps=[cvi, ev_dg[jb], cvt_ev], dma=True, slot="gd%d" % jb)
                    gu_ = p.run("g", lambda e, j=j: e.indirect_dma_start(
                        out=Ug8[:, j * 1024:(j + 1) * 1024], out_offset=None, in_=up16,
                        in_offset=IndirectOffsetOnAxis(idxi[:, j:j + 1], 0)),
                        deps=[cvi, (ev_accl if j == 0 else None), ev_ug, cvt_ev],
                        dma=True, slot="gu")
                    ev_dg[jb] = p.run("v", lambda e, j=j, jb=jb: e.scalar_tensor_tensor(
                        prodscr[:], Dg[jb][:], 1.0, xnrow3[:], op0=ALU.mult, op1=ALU.mult,
                        accum_out=w8[:, j:j + 1]), deps=[gd, lxr, ev_w])
                    ev_w = ev_dg[jb]
                ev_ug = gu_
                wsl = p.run("s", lambda e: e.activation(out=w8[:], in_=w8[:], func=AF.Silu),
                            deps=[ev_w])
                wti = p.run("v", lambda e: e.tensor_tensor(w8[:], w8[:], sm8[:], op=ALU.mult),
                            deps=[wsl, smn])
                for j in range(8):
                    nx = 1 - acc_par
                    ev_accl = p.run("v", lambda e, j=j, acc_par=acc_par, nx=nx: e.scalar_tensor_tensor(
                        accp[nx][:], Ug8[:, j * 1024:(j + 1) * 1024], w8[:, j:j + 1],
                        accp[acc_par][:], op0=ALU.mult, op1=ALU.add),
                        deps=[wti, ev_ug, ev_accl])
                    acc_par = nx
            ev_acc = ev_accl
            pa = p.run("g", lambda e, t0=t0, acc_par=acc_par: e.dma_start(
                out=x_out[t0:t0 + 128, :], in_=accp[acc_par][:], accum_op=ALU.add),
                deps=[ev_accl, prev_pa, ev_init], dma=True, slot="pa")
            prev_pa = pa
            peer_accums.append(pa)
            ev_xr = ev_w

        ev_wg = [None, None]
        ev_wu = [None, None]
        ev_gu = None
        for tc8 in range(8):
            t0 = tc8 * 512
            for ib in range(32):
                wb = ib % 2
                lg_ = p.run("g", lambda e, ib=ib, wb=wb: e.dma_start(
                    out=wgb[wb][:], in_=wg_h[ib]), deps=[ev_wg[wb], prev_pa], dma=True,
                    slot="wg%d" % wb)
                lu_ = p.run("g", lambda e, ib=ib, wb=wb: e.dma_start(
                    out=wub[wb][:], in_=wu_h[ib]), deps=[ev_wu[wb]], dma=True,
                    slot="wu%d" % wb)
                mmg = None
                for c in range(NC):
                    mmg = p.run("t", lambda e, ib=ib, wb=wb, c=c, t0=t0: e.matmul(
                        PS[4][:, 0:512], wgb[wb][:, c * 128:(c + 1) * 128],
                        xnT[:, c * TOK + t0: c * TOK + t0 + 512],
                        start=(c == 0), stop=(c == NC - 1)),
                        deps=([lg_, ev_gu] if c == 0 else []), ev=(c == NC - 1))
                sg = p.run("s", lambda e: e.activation(out=gscr[:], in_=PS[4][:, 0:512],
                                                       func=AF.Silu), deps=[mmg])
                mmu = None
                for c in range(NC):
                    mmu = p.run("t", lambda e, ib=ib, wb=wb, c=c, t0=t0: e.matmul(
                        PS[5][:, 0:512], wub[wb][:, c * 128:(c + 1) * 128],
                        xnT[:, c * TOK + t0: c * TOK + t0 + 512],
                        start=(c == 0), stop=(c == NC - 1)),
                        deps=([lu_, ev_gu] if c == 0 else []), ev=(c == NC - 1))
                gm = p.run("v", lambda e: e.tensor_tensor(gub[:], gscr[:], PS[5][:, 0:512],
                                                          op=ALU.mult), deps=[mmu, sg])
                ev_gu = p.run("y", lambda e, ib=ib, t0=t0: e.dma_start(
                    out=guT_hbm[ib, :, t0:t0 + 512], in_=gub[:]), deps=[gm], dma=True,
                    slot="gw")
                ev_wg[wb] = mmg
                ev_wu[wb] = mmu
        stp.close()

        stw = ExitStack()

        def sbw(name, shape, dt=BF16):
            return stw.enter_context(nc.sbuf_tensor(name, shape, dt))
        wd_sb = sbw("wd_sb", [128, 32 * 1024])
        itb2 = [sbw("itb2_%d" % i, [128, 32 * 128]) for i in range(2)]
        aw = [sbw("aw%d" % i, [128, 512], F32) for i in range(2)]
        ev_wd = None
        for ib in range(32):
            ev_wd = p.run("g", lambda e, ib=ib: e.dma_start(
                out=wd_sb[:, ib * 1024:(ib + 1) * 1024], in_=wd_h[ib]),
                deps=[ev_gu, prev_pa] if ib == 0 else [], dma=True, slot="wd")
        ev_it2 = [None, None]
        prev_ga = None
        for ttile in range(NT):
            ib2 = ttile % 2
            li = p.run("y", lambda e, ttile=ttile, ib2=ib2: e.dma_start(
                out=itb2[ib2][:].rearrange("p (a t) -> p a t", a=32),
                in_=guT_hbm[:, :, ttile * 128:(ttile + 1) * 128].rearrange("a p t -> p a t")),
                deps=[ev_it2[ib2], ev_gu], dma=True, slot="gi%d" % ib2)
            for ecol in range(2):
                mm = None
                for ic in range(32):
                    mm = p.run("t", lambda e, ic=ic, ecol=ecol, ib2=ib2: e.matmul(
                        PS[ecol][:, 0:512], itb2[ib2][:, ic * 128:(ic + 1) * 128],
                        wd_sb[:, ic * 1024 + ecol * 512: ic * 1024 + (ecol + 1) * 512],
                        start=(ic == 0), stop=(ic == 31)),
                        deps=([li, ev_wd, prev_ga] if ic == 0 else []), ev=(ic == 31))
                cpe = p.run("v", lambda e, ecol=ecol: e.tensor_copy(aw[ecol][:],
                                                                    PS[ecol][:, 0:512]),
                            deps=[mm, prev_ga])
                prev_ga = p.run("g", lambda e, ttile=ttile, ecol=ecol: e.dma_start(
                    out=x_out[ttile * 128:(ttile + 1) * 128, ecol * 512:(ecol + 1) * 512],
                    in_=aw[ecol][:], accum_op=ALU.add),
                    deps=[cpe, prev_pa, ev_init], dma=True, slot="ga")
            ev_it2[ib2] = prev_ga
        stw.close()
        if fused is None:
            p.barrier_all("y")
            p.emit()
    return nc


def prep_block2(i, x_shards):
    def chunkT(W):
        WT = _f32(W).T
        din, dout = WT.shape
        return np.ascontiguousarray(WT.reshape(din // 128, 128, dout).transpose(1, 0, 2)
                                    .reshape(128, (din // 128) * dout))
    w1T2 = chunkT(i["L2_attn_w1_w"])
    v1T2 = chunkT(i["L2_attn_v1_w"])
    w2T2 = chunkT(i["L2_attn_w2_w"])
    wb12 = np.zeros((128, 16), np.float32)
    wb12[:, 0:8] = _f32(i["L2_attn_w1_b"]).reshape(8, 128).T
    wb12[:, 8:16] = _f32(i["L2_attn_v1_b"]).reshape(8, 128).T
    w2br = _f32(i["L2_attn_w2_b"]).reshape(1, D)
    cw = _f32(i["L2_attn_conv_w"])
    convw = np.zeros((128, 24), np.float32)
    for tap in range(3):
        convw[:, tap * 8:(tap + 1) * 8] = cw[:, 0, tap].reshape(8, 128).T
    Am = _f32(i["L2_attn_A"])
    BmT = chunkT(i["L2_attn_Bm"])
    CmT = np.ascontiguousarray(_f32(i["L2_attn_Cm"]).T)
    ln1g = np.ascontiguousarray(np.broadcast_to(_f32(i["L2_attn_ln1_g"])[None, :], (128, D)))
    ln1b = np.ascontiguousarray(np.broadcast_to(_f32(i["L2_attn_ln1_b"])[None, :], (128, D)))
    ln2g = np.ascontiguousarray(np.broadcast_to(_f32(i["L2_attn_ln2_g"])[None, :], (128, D)))
    ln2b = np.ascontiguousarray(np.broadcast_to(_f32(i["L2_attn_ln2_b"])[None, :], (128, D)))
    ff1w = _f32(i["L2_attn_ff1_w"])
    ff1T = np.ascontiguousarray(
        ff1w.reshape(16, 128, 8, 128).transpose(3, 0, 2, 1).reshape(128, 16 * 8 * 128))
    ff1b = np.ascontiguousarray(_f32(i["L2_attn_ff1_b"]).reshape(16, 128).T)
    ff2T = chunkT(i["L2_attn_ff2_w"])
    ff2br = _f32(i["L2_attn_ff2_b"]).reshape(1, D)
    wqT2 = chunkT(i["L2_ffn_Wq"])
    keys = _f32(i["L2_ffn_keys"])
    keysT2 = np.zeros((128, 8 * 128), np.float32)
    for h in range(4):
        for pp in range(2):
            base = (h % 2) * 64
            keysT2[base:base + 64, (2 * h + pp) * 128:(2 * h + pp + 1) * 128] = \
                keys[h, :, pp, :].T
    dn = _f32(i["L2_ffn_down_embed"])
    up = _f32(i["L2_ffn_up_embed"])
    Wg = _f32(i["L2_ffn_Wg"])
    wg_r = np.ascontiguousarray(
        Wg.reshape(32, 128, 8, 128).transpose(1, 3, 2, 0)[0] if False else
        Wg.reshape(32, 128, 8, 128).transpose(0, 3, 2, 1).reshape(32, 128, 1024))
    Wu = _f32(i["L2_ffn_Wu"])
    wu_r = np.ascontiguousarray(Wu.reshape(32, 128, 8, 128).transpose(0, 3, 2, 1)
                                .reshape(32, 128, 1024))
    Wd = _f32(i["L2_ffn_Wd"])
    wd_r = np.ascontiguousarray(Wd.T.reshape(32, 128, 1024))
    identf = np.eye(128, dtype=np.float32)
    onesf = np.ones((128, 512), dtype=np.float32)
    common = dict(w1T2=w1T2, v1T2=v1T2, w2T2=w2T2, wb12=wb12, w2br=w2br, convw=convw,
                  Am=Am, BmT=BmT, CmT=CmT, ln1g2=ln1g, ln1b2=ln1b, ln2g2=ln2g, ln2b2=ln2b,
                  ff1T2=ff1T, ff1b2=ff1b, ff2T2=ff2T, ff2br=ff2br, identf=identf,
                  onesf=onesf, wqT2=wqT2, keysT2=keysT2, dn_f32=dn, up_f32=up,
                  wg_r=wg_r, wu_r=wu_r, wd_r=wd_r)
    return [dict(common, x_in=x_shards[c]) for c in range(NCORES)]


def run_block2(i, x):
    shards = _shard_x(x)
    nc = build_block2()
    maps = prep_block2(i, shards)
    res = run_bass_kernel_spmd(nc, maps, core_ids=list(range(NCORES)))
    return _gather_x([r["x_out"] for r in res.results])


def build_fused():
    nc = bass.Bass("TRN2", target_bir_lowering=False)
    x_in = nc.dram_tensor("x_in", [TOK, D], F32, kind="ExternalInput").ap()
    x1h = nc.dram_tensor("x1h", [TOK, D], F32, kind="Internal").ap()
    x2h = nc.dram_tensor("x2h", [TOK, D], F32, kind="Internal").ap()
    x3h = nc.dram_tensor("x3h", [TOK, D], F32, kind="Internal").ap()
    x_out = nc.dram_tensor("x_out", [TOK, D], F32, kind="ExternalOutput").ap()
    p = Prog(nc)
    build_block0(with_ffn=True, fused=(nc, p, "b0_", x_in, x1h))
    p.block_boundary()
    build_block1(fused=(nc, p, "b1_", x1h, x2h))
    p.block_boundary()
    build_block2(fused=(nc, p, "b2_", x2h, x3h))
    p.block_boundary()
    build_block3(fused=(nc, p, "b3_", x3h, x_out))
    p.barrier_all("y")
    p.emit()
    return nc


def prep_fused(i, x_shards):
    zshards = [np.zeros((2, 2), np.float32)] * NCORES
    parts = [("b0_", prep_block0(i, zshards)), ("b1_", prep_block1(i, zshards)),
             ("b2_", prep_block2(i, zshards)), ("b3_", prep_block3(i, zshards))]
    maps = []
    for c in range(NCORES):
        d = {"x_in": x_shards[c]}
        for pfx, ms in parts:
            for k, v in ms[c].items():
                if k == "x_in":
                    continue
                d[pfx + k] = v
        maps.append(d)
    return maps


_NC_CACHE = {}


def kernel(**inputs):
    x = np.ascontiguousarray(np.asarray(inputs["L0_attn_x"], dtype=np.float32))
    if "fused" not in _NC_CACHE:
        _NC_CACHE["fused"] = build_fused()
    shards = _shard_x(x)
    maps = prep_fused(inputs, shards)
    res = run_bass_kernel_spmd(_NC_CACHE["fused"], maps, core_ids=list(range(NCORES)))
    x = _gather_x([r["x_out"] for r in res.results])
    return np.ascontiguousarray(x.reshape(16, 2048, 1024), dtype=np.float32)


# version 05
# speedup vs baseline: 1.8590x; 1.1224x over previous
import numpy as np
from contextlib import ExitStack

import concourse.bass as bass
import concourse.mybir as mybir
from concourse.bass_utils import run_bass_kernel_spmd
from concourse.bass_core_rust import IndirectOffsetOnAxis

F32 = mybir.dt.float32
BF16 = mybir.dt.bfloat16
I32 = mybir.dt.int32
AF = mybir.ActivationFunctionType
ALU = mybir.AluOpType
AX = mybir.AxisListType

ENG_ATTR = {"t": "tensor", "v": "vector", "s": "scalar", "g": "gpsimd", "y": "sync"}

NCORES = 8
TOK = 4096
NB = 2
S = 2048
D = 1024
NT = TOK // 128
NC = D // 128


class _NCP:

    def __init__(self, nc, pfx):
        self._nc = nc
        self._pfx = pfx

    def dram_tensor(self, name, *a, **k):
        return self._nc.dram_tensor(self._pfx + name, *a, **k)

    def sbuf_tensor(self, name, *a, **k):
        return self._nc.sbuf_tensor(self._pfx + name, *a, **k)

    def psum_tensor(self, name, *a, **k):
        return self._nc.psum_tensor(self._pfx + name, *a, **k)

    def __getattr__(self, n):
        return getattr(self._nc, n)


class Prog:
    AUTO_CHAIN = ("v", "s", "g", "y")

    def __init__(self, nc):
        self.nc = nc
        self.ops = {e: [] for e in ENG_ATTR}
        self.cnt = {}
        self.seen = {e: {} for e in ENG_ATTR}
        self.last_ev = {e: None for e in ENG_ATTR}
        self.slot_alias = {}

    def block_boundary(self):
        for e in ENG_ATTR:
            self.barrier_all(e)
        self.slot_alias = {}

    def run(self, eng, fn, deps=(), dma=False, slot=None, ev=True):
        if dma:
            key = (eng, slot if slot is not None else eng)
            neng = sum(1 for k in self.slot_alias if k[0] == eng)
            canon = self.slot_alias.setdefault(key, "%s%d" % (eng, neng))
            sem = "D_" + canon
        else:
            sem = eng
        if eng in self.AUTO_CHAIN:
            deps = list(deps) + [self.last_ev[eng]]
        waits = {}
        for d in deps:
            if d is None:
                continue
            ds, dv = d
            if dv > self.seen[eng].get(ds, 0):
                waits[ds] = max(waits.get(ds, 0), dv)
        for ds, dv in waits.items():
            self.seen[eng][ds] = dv
        myev = None
        amt = 0
        if ev:
            amt = 16 if dma else 1
            self.cnt[sem] = self.cnt.get(sem, 0) + amt
            myev = (sem, self.cnt[sem])
        self.ops[eng].append((fn, sorted(waits.items()), sem if ev else None, amt))
        if myev is not None:
            self.last_ev[eng] = myev
        return myev

    def barrier(self, eng, deps):
        return self.run(eng, lambda e: e.nop(), deps=deps, ev=True)

    def barrier_all(self, eng):
        deps = [(s, c) for s, c in self.cnt.items()]
        return self.run(eng, lambda e: e.nop(), deps=deps, ev=True)

    def emit(self):
        nc = self.nc
        sems = sorted(self.cnt.keys())
        with ExitStack() as st:
            h = {sn: st.enter_context(nc.semaphore(sn)) for sn in sems}
            with nc.Block() as block:
                for e, attr in ENG_ATTR.items():
                    ops = self.ops[e]
                    if not ops:
                        continue

                    def mkbody(ops):
                        def body(engobj):
                            for fn, waits, sem, amt in ops:
                                for ds, dv in waits:
                                    engobj.wait_ge(h[ds], dv)
                                inst = fn(engobj)
                                if sem is not None:
                                    inst.then_inc(h[sem], amt)
                        return body

                    getattr(block, attr)(mkbody(ops))


def _shard_x(x):
    x = np.ascontiguousarray(np.asarray(x), dtype=np.float32).reshape(16, 2048, 1024)
    return [np.ascontiguousarray(x[2 * c:2 * c + 2].reshape(TOK, D)) for c in range(NCORES)]


def _gather_x(shards):
    return np.concatenate([np.asarray(s).reshape(2, 2048, 1024) for s in shards], axis=0)


def _f32(a):
    return np.ascontiguousarray(np.asarray(a), dtype=np.float32)


def emit_ln_stats(p, tiles, src_ev, src, prev_ln):
    sm = p.run("v", lambda e: e.reduce_sum(tiles["sums"][:], src[:], axis=AX.X), deps=[src_ev, prev_ln])
    sq = p.run("s", lambda e: e.activation(out=tiles["scr0"][:], in_=src[:], func=AF.Square,
                                           accum_out=tiles["sqs"][:]), deps=[src_ev, prev_ln])
    mn = p.run("v", lambda e: e.tensor_scalar(tiles["mean"][:], tiles["sums"][:], 1.0 / D, None,
                                              op0=ALU.mult), deps=[sm])
    m2 = p.run("v", lambda e: e.tensor_tensor(tiles["m2"][:], tiles["mean"][:], tiles["mean"][:],
                                              op=ALU.mult), deps=[mn])
    vr = p.run("v", lambda e: e.scalar_tensor_tensor(tiles["var"][:], tiles["sqs"][:], 1.0 / D,
                                                     tiles["m2"][:], op0=ALU.mult, op1=ALU.subtract),
               deps=[sq, m2])
    sd = p.run("s", lambda e: e.activation(out=tiles["std"][:], in_=tiles["var"][:], func=AF.Sqrt,
                                           bias=tiles["eps"][:, 0:1]), deps=[vr, tiles.get("eps_ev")])
    rs = p.run("v", lambda e: e.reciprocal(tiles["rstd"][:], tiles["std"][:]), deps=[sd])
    return rs


def emit_ln_prologue(p, tiles, x_in, xnT_sb, identb, pst, dep0=(),
                     x_out_plus_xn=None, xn_hbm=None):
    xev = [None, None]
    prev_xpx = [None, None]
    prev_ln = None
    tr_done = [None] * len(pst)
    last_store = None
    last_xn = None
    for tt in range(NT):
        b = tt % 2
        xt = tiles["xt"][b]
        ld = p.run("y", lambda e, xt=xt, tt=tt: e.dma_start(
            out=xt[:], in_=x_in[tt * 128:(tt + 1) * 128, :]),
            deps=[xev[b]] + list(dep0 if tt < 2 else ()), dma=True, slot="x%d" % b)
        rs = emit_ln_stats(p, tiles, ld, xt, prev_ln)
        xnf = tiles["xnf"][b]
        lnev = p.run("v", lambda e, xt=xt, xnf=xnf: e.tensor_scalar(
            xnf[:], xt[:], tiles["mean"][:, 0:1], tiles["rstd"][:, 0:1],
            op0=ALU.subtract, op1=ALU.mult), deps=[rs])
        prev_ln = lnev
        xnb = tiles["xnb"][b]
        cb = p.run("v", lambda e, xnf=xnf, xnb=xnb: e.tensor_copy(xnb[:], xnf[:]), deps=[lnev])
        if xn_hbm is not None:
            last_xn = p.run("y", lambda e, xnb=xnb, tt=tt: e.dma_start(
                out=xn_hbm[tt * 128:(tt + 1) * 128, :], in_=xnb[:]), deps=[cb], dma=True,
                slot="xnh")
        if x_out_plus_xn is not None:
            so = p.run("v", lambda e, xt=xt, xnf=xnf, b=b: e.tensor_add(
                tiles["xpx"][b][:], xt[:], xnf[:]), deps=[lnev, prev_xpx[b]])
            last_store = p.run("y", lambda e, tt=tt, b=b: e.dma_start(
                out=x_out_plus_xn[tt * 128:(tt + 1) * 128, :], in_=tiles["xpx"][b][:]),
                deps=[so], dma=True, slot="xo%d" % b)
            prev_xpx[b] = last_store
            xev[b] = so
        else:
            xev[b] = lnev
        if xnT_sb is not None:
            for c in range(NC):
                pi = (tt * NC + c) % len(pst)
                tr = p.run("t", lambda e, xnb=xnb, c=c, pi=pi: e.transpose(
                    pst[pi][:, 0:128], xnb[:, c * 128:(c + 1) * 128], identb[:]),
                    deps=[cb, tr_done[pi]] + list(dep0 if tt == 0 else ()))
                tr_done[pi] = p.run("v", lambda e, tt=tt, c=c, pi=pi: e.tensor_copy(
                    xnT_sb[:, c * TOK + tt * 128: c * TOK + (tt + 1) * 128], pst[pi][:, 0:128]),
                    deps=[tr])
        else:
            tr_done = [cb, cb]
    done = list(tr_done) + [prev_ln]
    if last_store is not None:
        done.append(last_store)
    if last_xn is not None:
        done.append(last_xn)
    return done


def mk_tiles(st, nc, with_xpx=False):
    def sb(name, shape, dt=BF16):
        return st.enter_context(nc.sbuf_tensor(name, shape, dt))
    return dict(
        xt=[sb("xt0", [128, D], F32), sb("xt1", [128, D], F32)],
        xnf=[sb("xnf0", [128, D], F32), sb("xnf1", [128, D], F32)],
        xnb=[sb("xnb0", [128, D]), sb("xnb1", [128, D])],
        xpx=[sb("xpx0", [128, D], F32), sb("xpx1", [128, D], F32)] if with_xpx else None,
        scr0=sb("lnscr0", [128, D], F32),
        sums=sb("sums", [128, 1], F32), sqs=sb("sqs", [128, 1], F32),
        mean=sb("mean", [128, 1], F32), m2=sb("m2s", [128, 1], F32),
        var=sb("var", [128, 1], F32), std=sb("std", [128, 1], F32),
        rstd=sb("rstd", [128, 1], F32), eps=sb("epsc", [128, 1], F32),
    )


def build_block3(phase=2, sub=9, fused=None):
    nc = (bass.Bass("TRN2", target_bir_lowering=False) if fused is None
          else _NCP(fused[0], fused[2]))
    H, DH, R = 8, 128, 32
    SCALE = 1.0 / np.sqrt(DH)
    FF = 4096
    TCH = 256
    NTC = TOK // TCH

    if fused is None:
        x_in = nc.dram_tensor("x_in", [TOK, D], F32, kind="ExternalInput").ap()
        x_out = nc.dram_tensor("x_out", [TOK, D], F32, kind="ExternalOutput").ap()
    else:
        x_in, x_out = fused[3], fused[4]

    vqkvT = nc.dram_tensor("vqkvT", [128, H * 96], F32, kind="ExternalInput").ap()
    uqkvT = nc.dram_tensor("uqkvT", [96, H * 128], F32, kind="ExternalInput").ap()
    bqkT = nc.dram_tensor("bqkT", [128, 2 * H], F32, kind="ExternalInput").ap()
    bvfl = nc.dram_tensor("bvfl", [1, D], F32, kind="ExternalInput").ap()
    identf = nc.dram_tensor("identf", [128, 128], F32, kind="ExternalInput").ap()
    onesf = nc.dram_tensor("onesf", [128, 512], F32, kind="ExternalInput").ap()
    g1bc = nc.dram_tensor("g1bc", [128, D], F32, kind="ExternalInput").ap()
    b1bc = nc.dram_tensor("b1bc", [128, D], F32, kind="ExternalInput").ap()
    w1T_h = nc.dram_tensor("w1T", [D, FF], F32, kind="ExternalInput").ap()
    b1pp_h = nc.dram_tensor("b1pp", [128, FF // 128], F32, kind="ExternalInput").ap()
    ln2gb_h = nc.dram_tensor("ln2gb", [2, FF], F32, kind="ExternalInput").ap()
    w2T_h = nc.dram_tensor("w2T", [FF, D], F32, kind="ExternalInput").ap()
    b2row_h = nc.dram_tensor("b2row", [1, D], F32, kind="ExternalInput").ap()

    with ExitStack() as st:
        def sb(name, shape, dt=BF16):
            return st.enter_context(nc.sbuf_tensor(name, shape, dt))

        bufA = sb("bufA", [128, NC * TOK])
        tiles = mk_tiles(st, nc)
        identb = sb("identb", [128, 128])
        onesb_r = sb("onesb_r", [1, 128])
        onesb_c = sb("onesb_c", [128, 1])
        onesf_c = sb("onesf_c", [128, 1], F32)
        onesf_r = sb("onesf_r", [1, 128], F32)
        st2 = ExitStack()

        def sb2(name, shape, dt=BF16):
            return st2.enter_context(nc.sbuf_tensor(name, shape, dt))
        vqkv_sb = sb2("vqkv_sb", [128, H * 96])
        uqkv_sb = sb2("uqkv_sb", [96, H * 128])
        bqk_sb = sb2("bqk_sb", [128, 2 * H], F32)
        bv_sb = sb2("bv_sb", [1, D])
        tqkv = sb2("tqkv", [96, S])
        qT = sb2("qT", [128, S])
        kT = sb2("kT", [128, S])
        V = sb2("V", [128, 16 * 129])
        PT = [sb2("PT0", [128, 512]), sb2("PT1", [128, 512])]
        sqscr = sb2("sqscr", [128, S])
        normrow = sb2("normrow", [1, S], F32)
        sc1 = sb2("sc1", [1, 1], F32)
        sc2 = sb2("sc2", [1, 1], F32)
        negb = sb2("negb", [128, 1], F32)
        rz = sb2("rz", [128, 1], F32)
        at = [sb2("at%d" % i, [128, 128], F32) for i in range(4)]
        bvb = sb2("bvb", [128, 128])

        PS = [st.enter_context(nc.psum_tensor("ps%d" % i, [128, 512], F32)) for i in range(6)]
        PST = [st.enter_context(nc.psum_tensor("pstt%d" % i, [128, 1024], BF16)) for i in range(2)]

        p = Prog(nc) if fused is None else fused[1]
        tiles["eps_ev"] = p.run("v", lambda e: e.memset(tiles["eps"][:], 1e-5))

        ev_init = p.run("y", lambda e: e.dma_start(out=x_out, in_=x_in), dma=True, slot="init")

        for ld_fn in [
            lambda e: e.dma_start(out=identb[:], in_=identf),
            lambda e: e.dma_start(out=onesb_r[:], in_=onesf[0:1, 0:128]),
            lambda e: e.dma_start(out=onesb_c[:], in_=onesf[:, 0:1], allow_slow_non_contiguous=True),
            lambda e: e.dma_start(out=onesf_c[:], in_=onesf[:, 0:1], allow_slow_non_contiguous=True),
            lambda e: e.dma_start(out=onesf_r[:], in_=onesf[0:1, 0:128]),
            lambda e: e.dma_start(out=vqkv_sb[:], in_=vqkvT),
            lambda e: e.dma_start(out=uqkv_sb[:], in_=uqkvT),
            lambda e: e.dma_start(out=bqk_sb[:], in_=bqkT),
            lambda e: e.dma_start(out=bv_sb[:], in_=bvfl),
            lambda e: e.dma_start(out=g1b_sb[:], in_=g1bc),
            lambda e: e.dma_start(out=b1b_sb[:], in_=b1bc),
            lambda e: e.dma_start(out=b1pp[:], in_=b1pp_h),
            lambda e: e.dma_start(out=ln2gb[:], in_=ln2gb_h),
            lambda e: e.dma_start(out=b2row[:], in_=b2row_h),
        ]:
            ev_const = p.run("g", ld_fn, dma=True, slot="const")
        ev_idb = ev_or = ev_oc = ev_ofc = ev_ofr = ev_const
        ev_w1 = ev_w2 = ev_w3 = ev_w4 = ev_const
        ev_g1 = ev_b1 = ev_b1pp = ev_l2 = ev_b2r = ev_const

        ln_done = emit_ln_prologue(p, tiles, x_in, bufA, identb, PST, dep0=[ev_idb])

        ev_vones = None
        for kt in range(16):
            ev_vones = p.run("v", lambda e, kt=kt: e.memset(V[:, kt * 129 + 128:kt * 129 + 129], 1.0))

        attn_accums = []
        xnh_evs = []
        iter_dep = [None]
        at_free = [None] * 4
        for b in (range(NB) if phase >= 1 else []):
            for h in range(H):
                deps0 = ln_done + [ev_w1] + iter_dep
                ev_tq = []
                prev = None
                for cqc in range(4):
                    mm = p.run("t", lambda e, cqc=cqc, h=h, b=b: e.matmul(
                        PS[4][0:96, 0:512], vqkv_sb[:, h * 96:(h + 1) * 96],
                        bufA[:, h * TOK + b * S + cqc * 512: h * TOK + b * S + (cqc + 1) * 512],
                        start=True, stop=True),
                        deps=(deps0 if cqc == 0 else []) + [prev])
                    cv = p.run("v", lambda e, cqc=cqc: e.tensor_copy(
                        tqkv[:, cqc * 512:(cqc + 1) * 512], PS[4][0:96, 0:512]), deps=[mm])
                    ev_tq.append(cv)
                    prev = cv
                ev_q = None
                ev_k = None
                for cqc in range(4):
                    mq = p.run("t", lambda e, cqc=cqc, h=h: e.matmul(
                        PS[0][:, 0:512], uqkv_sb[0:32, h * 128:(h + 1) * 128],
                        tqkv[0:32, cqc * 512:(cqc + 1) * 512], start=True, stop=True),
                        deps=[ev_tq[cqc], ev_w2, ev_q])
                    ev_q = p.run("s", lambda e, cqc=cqc, h=h: e.activation(
                        out=qT[:, cqc * 512:(cqc + 1) * 512], in_=PS[0][:, 0:512], func=AF.Identity,
                        bias=bqk_sb[:, h:h + 1]), deps=[mq, ev_w3])
                    mk = p.run("t", lambda e, cqc=cqc, h=h: e.matmul(
                        PS[1][:, 0:512], uqkv_sb[32:64, h * 128:(h + 1) * 128],
                        tqkv[32:64, cqc * 512:(cqc + 1) * 512], start=True, stop=True),
                        deps=[ev_tq[cqc], ev_k])
                    ev_k = p.run("s", lambda e, cqc=cqc, h=h: e.activation(
                        out=kT[:, cqc * 512:(cqc + 1) * 512], in_=PS[1][:, 0:512], func=AF.Identity,
                        bias=bqk_sb[:, H + h:H + h + 1]), deps=[mk, ev_w3])
                mvb = p.run("t", lambda e, h=h: e.matmul(
                    PS[3][:, 0:128], onesb_r[0:1, :], bv_sb[0:1, h * 128:(h + 1) * 128],
                    start=True, stop=True), deps=[ev_w4, ev_or] + deps0)
                ev_bvb = p.run("v", lambda e: e.tensor_copy(bvb[:], PS[3][:, 0:128]), deps=[mvb])
                ev_v = None
                for kt in range(16):
                    cqc, off = kt // 4, (kt % 4) * 128
                    mv2 = p.run("t", lambda e, cqc=cqc, off=off, h=h: e.matmul(
                        PS[2][:, 0:128], tqkv[64:96, cqc * 512 + off: cqc * 512 + off + 128],
                        uqkv_sb[64:96, h * 128:(h + 1) * 128], start=True, stop=True),
                        deps=[ev_tq[cqc], ev_v] + (deps0 if kt == 0 else []))
                    ev_v = p.run("v", lambda e, kt=kt: e.tensor_tensor(
                        V[:, kt * 129:kt * 129 + 128], PS[2][:, 0:128], bvb[:], op=ALU.add),
                        deps=[mv2, ev_bvb])
                if sub < 3:
                    iter_dep = [ev_q, ev_k, ev_v]
                    continue
                sq_q = p.run("s", lambda e: e.activation(out=sqscr[:], in_=qT[:], func=AF.Square),
                             deps=[ev_q])
                ev_nq = ev_bvb
                for cqc in range(4):
                    mm = p.run("t", lambda e, cqc=cqc: e.matmul(
                        PS[3][0:1, 0:512], onesb_c[:, 0:1], sqscr[:, cqc * 512:(cqc + 1) * 512],
                        start=True, stop=True), deps=[sq_q, ev_oc, ev_nq])
                    ev_nq = p.run("v", lambda e, cqc=cqc: e.tensor_copy(
                        normrow[0:1, cqc * 512:(cqc + 1) * 512], PS[3][0:1, 0:512]), deps=[mm])
                mq_ev = p.run("v", lambda e: e.reduce_max(sc1[0:1, 0:1], normrow[0:1, :], axis=AX.X),
                              deps=[ev_nq])
                sq_k = p.run("s", lambda e: e.activation(out=sqscr[:], in_=kT[:], func=AF.Square),
                             deps=[ev_k, ev_nq])
                ev_nk = None
                for cqc in range(4):
                    mm = p.run("t", lambda e, cqc=cqc: e.matmul(
                        PS[3][0:1, 0:512], onesb_c[:, 0:1], sqscr[:, cqc * 512:(cqc + 1) * 512],
                        start=True, stop=True), deps=[sq_k, ev_nk, mq_ev])
                    ev_nk = p.run("v", lambda e, cqc=cqc: e.tensor_copy(
                        normrow[0:1, cqc * 512:(cqc + 1) * 512], PS[3][0:1, 0:512]), deps=[mm])
                mk_ev = p.run("v", lambda e: e.reduce_max(sc2[0:1, 0:1], normrow[0:1, :], axis=AX.X),
                              deps=[ev_nk])
                bb = p.run("v", lambda e: e.tensor_tensor(sc1[0:1, 0:1], sc1[0:1, 0:1], sc2[0:1, 0:1],
                                                          op=ALU.mult), deps=[mq_ev, mk_ev])
                s1 = p.run("s", lambda e: e.activation(out=sc1[0:1, 0:1], in_=sc1[0:1, 0:1],
                                                       func=AF.Sqrt), deps=[bb])
                bb2 = p.run("v", lambda e: e.tensor_scalar(sc1[0:1, 0:1], sc1[0:1, 0:1], -SCALE, None,
                                                           op0=ALU.mult), deps=[s1])
                mmb = p.run("t", lambda e: e.matmul(PS[3][:, 0:1], onesf_r[0:1, :], sc1[0:1, 0:1],
                                                    start=True, stop=True), deps=[bb2, ev_ofr, ev_nk])
                ev_negb = p.run("v", lambda e: e.tensor_copy(negb[:], PS[3][:, 0:1]), deps=[mmb])

                if sub < 4:
                    iter_dep = [ev_negb, ev_v]
                    continue
                av_psum_free = [None, ev_negb, ev_tq[3], None]
                s_free = [ev_negb, None]
                pt_free = [None, None]
                last_av = {}
                for qc in range(4):
                    for kt in range(16):
                        pi = kt % 2
                        ms = p.run("t", lambda e, qc=qc, kt=kt, pi=pi: e.matmul(
                            PS[pi][:, 0:512], kT[:, kt * 128:(kt + 1) * 128],
                            qT[:, qc * 512:(qc + 1) * 512], start=True, stop=True),
                            deps=[ev_q, ev_k, s_free[pi]])
                        pe = p.run("s", lambda e, pi=pi: e.activation(
                            out=PT[pi][:], in_=PS[pi][:, 0:512], func=AF.Exp, scale=SCALE,
                            bias=negb[:, 0:1]), deps=[ms, pt_free[pi], ev_negb])
                        s_free[pi] = pe
                        for qt in (range(4) if sub >= 5 else []):
                            last_av[qt] = p.run("t", lambda e, kt=kt, qt=qt, pi=pi: e.matmul(
                                PS[2 + qt][:, 0:129], PT[pi][:, qt * 128:(qt + 1) * 128],
                                V[:, kt * 129:(kt + 1) * 129],
                                start=(kt == 0), stop=(kt == 15)),
                                deps=([pe, av_psum_free[qt], ev_v, ev_vones] if kt == 0 else [pe]))
                        pt_free[pi] = last_av[3] if sub >= 5 else pe
                    for qt in (range(4) if sub >= 6 else []):
                        rc = p.run("v", lambda e, qt=qt: e.reciprocal(
                            rz[:], PS[2 + qt][:, 128:129]), deps=[last_av[qt]])
                        sc = p.run("v", lambda e, qt=qt: e.tensor_scalar(
                            at[qt][:], PS[2 + qt][:, 0:128], rz[:, 0:1], None, op0=ALU.mult),
                            deps=[rc, at_free[qt]])
                        av_psum_free[qt] = sc
                        row0 = b * S + qc * 512 + qt * 128
                        ac = p.run("g", lambda e, row0=row0, h=h, qt=qt: e.dma_start(
                            out=x_out[row0:row0 + 128, h * 128:(h + 1) * 128], in_=at[qt][:],
                            accum_op=ALU.add), deps=[sc, ev_init], dma=True, slot="a%d" % qt)
                        at_free[qt] = ac
                        attn_accums.append(ac)
                iter_dep = [s_free[0], s_free[1]] + [x for x in av_psum_free if x]

        attn_done = (attn_accums[-4:] if attn_accums else [ev_vones]) + iter_dep + ln_done

        st2.close()
        QCH = 256
        bufB = sb("bufB", [128, 32 * D])
        wf32 = [sb("wf0", [128, QCH], F32), sb("wf1", [128, QCH], F32)]
        qscr1 = sb("qscr1", [128, QCH], F32)
        qscr2 = sb("qscr2", [128, QCH], F32)
        asum = sb("asum", [128, 1], F32)
        acc1 = sb("acc1", [128, 1], F32)
        mrow = sb("mrow", [1, 1], F32)
        m_pp = sb("m_pp", [128, 1], F32)
        s_pp = sb("s_pp", [128, 1], F32)
        m2_pp = sb("m2_pp", [128, 1], F32)
        s2_pp = sb("s2_pp", [128, 1], F32)
        g1b_sb = sb("g1b", [128, D], F32)
        b1b_sb = sb("b1b", [128, D], F32)
        b1pp = sb("b1pp_sb", [128, FF // 128], F32)
        ln2gb = sb("ln2gb_sb", [2, FF])
        b2row = sb("b2row_sb", [1, D])
        ln1T = sb("ln1T", [128, NC * TCH])
        hg = sb("hg", [128, 32 * TCH])
        hn = hg
        hsc = sb("hsc", [128, TCH])
        s1row = sb("s1row", [1, TCH], F32)
        mu2 = sb("mu2", [1, TCH], F32)
        var2 = sb("var2", [1, TCH], F32)
        s2row = sb("s2row", [1, TCH], F32)
        rstd2 = sb("rstd2", [1, TCH], F32)
        rstd2m = sb("rstd2m", [1, TCH])
        cv2 = sb("cv2", [2, TCH])
        cvU = sb("cvU", [2, TCH], F32)
        sqb = sb("sqb", [128, TCH])
        fout = [sb("fout%d" % i, [128, 512], F32) for i in range(2)]

        def quantize(wT_ap, nchunk, dst, dst_cols, m_ppt, s_ppt, pre_deps):
            nq = dst_cols // QCH
            ev_ld = [None, None]
            ev_acc = None
            z = p.run("v", lambda e: e.memset(acc1[:], 0.0), deps=pre_deps)
            first = True
            for c in range(nchunk):
                for q in range(nq):
                    bsel = (c * nq + q) % 2
                    ld = p.run("y", lambda e, c=c, q=q, bsel=bsel: e.dma_start(
                        out=wf32[bsel][:], in_=wT_ap[c * 128:(c + 1) * 128, q * QCH:(q + 1) * QCH]),
                        deps=[ev_ld[bsel]] + list(pre_deps if first else ()), dma=True,
                        slot="w%d" % bsel)
                    first = False
                    sm = p.run("v", lambda e, bsel=bsel: e.reduce_sum(
                        asum[:], wf32[bsel][:], axis=AX.X, apply_absolute_value=True),
                        deps=[ld, ev_acc])
                    ev_acc = p.run("v", lambda e: e.tensor_add(acc1[:], acc1[:], asum[:]),
                                   deps=[sm, z])
                    ev_ld[bsel] = sm
            mm = p.run("t", lambda e: e.matmul(PS[5][0:1, 0:1], onesf_c[:, 0:1], acc1[:, 0:1],
                                               start=True, stop=True),
                       deps=[ev_acc, ev_ofc] + pre_deps)
            e1a = p.run("v", lambda e, n=nchunk: e.tensor_scalar(
                mrow[0:1, 0:1], PS[5][0:1, 0:1], 1.0 / (n * 128 * dst_cols), None,
                op0=ALU.mult), deps=[mm])
            e1 = p.run("v", lambda e: e.tensor_scalar(
                mrow[0:1, 0:1], mrow[0:1, 0:1], 1e-5, None, op0=ALU.max), deps=[e1a])
            mb = p.run("t", lambda e: e.matmul(PS[5][:, 0:1], onesf_r[0:1, :], mrow[0:1, 0:1],
                                               start=True, stop=True), deps=[e1, ev_ofr])
            e2 = p.run("v", lambda e, m_ppt=m_ppt: e.tensor_copy(m_ppt[:], PS[5][:, 0:1]), deps=[mb])
            e3 = p.run("v", lambda e, m_ppt=m_ppt, s_ppt=s_ppt: e.reciprocal(s_ppt[:], m_ppt[:]),
                       deps=[e2])
            ev_q = None
            ev_ld = [None, None]
            for c in range(nchunk):
                for q in range(nq):
                    bsel = (c * nq + q) % 2
                    ld = p.run("y", lambda e, c=c, q=q, bsel=bsel: e.dma_start(
                        out=wf32[bsel][:], in_=wT_ap[c * 128:(c + 1) * 128, q * QCH:(q + 1) * QCH]),
                        deps=[ev_ld[bsel], e3], dma=True, slot="w%d" % bsel)
                    a1 = p.run("s", lambda e, bsel=bsel: e.activation(
                        out=qscr1[:], in_=wf32[bsel][:], func=AF.Abs), deps=[ld, ev_q])
                    a2a = p.run("v", lambda e, s_ppt=s_ppt: e.tensor_scalar(
                        qscr1[:], qscr1[:], s_ppt[:, 0:1], None, op0=ALU.mult), deps=[a1])
                    a2 = p.run("v", lambda e: e.tensor_scalar(
                        qscr1[:], qscr1[:], 0.5, None, op0=ALU.is_gt), deps=[a2a])
                    a3 = p.run("s", lambda e, bsel=bsel: e.sign(
                        qscr2[:], wf32[bsel][:]), deps=[ld, a1])
                    ev_q = p.run("v", lambda e, c=c, q=q, dst=dst: e.tensor_tensor(
                        dst[:, c * dst_cols + q * QCH: c * dst_cols + (q + 1) * QCH], qscr1[:],
                        qscr2[:], op=ALU.mult), deps=[a2, a3])
                    ev_ld[bsel] = ev_q
            return ev_q

        if phase < 2:
            p.barrier_all("y")
            p.emit()
            return nc
        evq1 = quantize(w1T_h, NC, bufA, FF, m_pp, s_pp,
                        attn_done + [ev_b1pp, ev_l2, ev_b2r, ev_g1, ev_b1])
        evq2 = quantize(w2T_h, 32, bufB, D, m2_pp, s2_pp, [evq1])

        ev_ot = p.run("g", lambda e: e.dma_start(out=cvU[1:2, :], in_=onesf[0:1, 0:TCH]),
                      deps=[evq2], dma=True, slot="ot")

        ffn_accums = []
        ev_ln1T = None
        ev_hg = {0: None, 1: None}
        fout_free = {(0, 0): None, (0, 1): None, (1, 0): None, (1, 1): None}
        ff2_ps_free = {}
        ev_stats = None
        prev_AB = None
        pst_free = [None, None]
        prev_ln = None
        for tc in range(NTC):
            trdep = ev_ln1T
            for tt in range(2):
                row0 = tc * TCH + tt * 128
                xt = tiles["xt"][tt]
                ld = p.run("y", lambda e, row0=row0, xt=xt: e.dma_start(
                    out=xt[:], in_=x_in[row0:row0 + 128, :]),
                    deps=[trdep, evq2, prev_ln, prev_AB], dma=True, slot="x%d" % tt)
                rs = emit_ln_stats(p, tiles, ld, xt, prev_ln)
                xnf = tiles["xnf"][tt]
                e_xn = p.run("v", lambda e, xt=xt, xnf=xnf: e.tensor_scalar(
                    xnf[:], xt[:], tiles["mean"][:, 0:1], tiles["rstd"][:, 0:1],
                    op0=ALU.subtract, op1=ALU.mult), deps=[rs])
                rs1 = emit_ln_stats(p, tiles, e_xn, xnf, None)
                e_l1 = p.run("v", lambda e, xnf=xnf: e.tensor_scalar(
                    tiles["scr0"][:], xnf[:], tiles["mean"][:, 0:1], tiles["rstd"][:, 0:1],
                    op0=ALU.subtract, op1=ALU.mult), deps=[rs1])
                e_g = p.run("v", lambda e: e.tensor_tensor(tiles["scr0"][:], tiles["scr0"][:],
                                                           g1b_sb[:], op=ALU.mult),
                            deps=[e_l1, ev_g1])
                ln1tok = tiles["xnb"][tt]
                e_bt = p.run("v", lambda e, ln1tok=ln1tok: e.tensor_tensor(
                    ln1tok[:], tiles["scr0"][:], b1b_sb[:], op=ALU.add), deps=[e_g, ev_b1])
                prev_ln = e_bt
                for c in range(NC):
                    pi = c % 2
                    tr = p.run("t", lambda e, ln1tok=ln1tok, c=c, pi=pi: e.transpose(
                        PST[pi][:, 0:128], ln1tok[:, c * 128:(c + 1) * 128], identb[:]),
                        deps=[e_bt, pst_free[pi]])
                    ev_ln1T = p.run("v", lambda e, c=c, tt=tt, pi=pi: e.tensor_copy(
                        ln1T[:, c * TCH + tt * 128: c * TCH + (tt + 1) * 128], PST[pi][:, 0:128]),
                        deps=[tr])
                    pst_free[pi] = ev_ln1T
            for oc in range(32):
                mme = None
                for c in range(NC):
                    mme = p.run("t", lambda e, oc=oc, c=c: e.matmul(
                        PS[oc % 2][:, 0:TCH], bufA[:, c * FF + oc * 128: c * FF + (oc + 1) * 128],
                        ln1T[:, c * TCH:(c + 1) * TCH], start=(c == 0), stop=(c == NC - 1)),
                        deps=([ev_ln1T, ev_hg[oc % 2], evq1] if c == 0 else []),
                        ev=(c == NC - 1))
                ev_hg[oc % 2] = p.run("s", lambda e, oc=oc: e.activation(
                    out=hg[:, oc * TCH:(oc + 1) * TCH], in_=PS[oc % 2][:, 0:TCH], func=AF.Gelu,
                    scale=m_pp[:, 0:1], bias=b1pp[:, oc:oc + 1]), deps=[mme])
            hg_done = [ev_hg[0], ev_hg[1]]
            ev_s1 = None
            for oc in range(32):
                ev_s1 = p.run("t", lambda e, oc=oc: e.matmul(
                    PS[2][0:1, 0:TCH], onesb_c[:, 0:1], hg[:, oc * TCH:(oc + 1) * TCH],
                    start=(oc == 0), stop=(oc == 31)),
                    deps=(hg_done + [ev_stats, prev_AB, ev_oc] if oc == 0 else []),
                    ev=(oc == 31))
            ev_s2 = None
            sq_prev = None
            for oc in range(32):
                sqe = p.run("s", lambda e, oc=oc: e.activation(
                    out=sqb[:], in_=hg[:, oc * TCH:(oc + 1) * TCH], func=AF.Square),
                    deps=(hg_done + [ev_stats, prev_AB] if oc == 0 else [ev_s2]))
                ev_s2 = p.run("t", lambda e, oc=oc: e.matmul(
                    PS[3][0:1, 0:TCH], onesb_c[:, 0:1], sqb[:],
                    start=(oc == 0), stop=(oc == 31)), deps=[sqe])
            c1 = p.run("v", lambda e: e.tensor_scalar(mu2[0:1, :], PS[2][0:1, 0:TCH], 1.0 / FF,
                                                      None, op0=ALU.mult), deps=[ev_s1])
            c2 = p.run("v", lambda e: e.tensor_scalar(var2[0:1, :], PS[3][0:1, 0:TCH], 1.0 / FF,
                                                      None, op0=ALU.mult), deps=[ev_s2])
            c3 = p.run("v", lambda e: e.tensor_tensor(s1row[0:1, :], mu2[0:1, :], mu2[0:1, :],
                                                      op=ALU.mult), deps=[c1])
            c4 = p.run("v", lambda e: e.tensor_tensor(var2[0:1, :], var2[0:1, :], s1row[0:1, :],
                                                      op=ALU.subtract), deps=[c2, c3])
            c5 = p.run("s", lambda e: e.activation(out=s2row[0:1, :], in_=var2[0:1, :],
                                                   func=AF.Sqrt, bias=tiles["eps"][0:1, 0:1]),
                       deps=[c4, tiles.get("eps_ev")])
            c6 = p.run("v", lambda e: e.reciprocal(rstd2[0:1, :], s2row[0:1, :]), deps=[c5])
            c7 = p.run("v", lambda e: e.tensor_scalar(rstd2m[0:1, :], rstd2[0:1, :],
                                                      m2_pp[0:1, 0:1], None, op0=ALU.mult),
                       deps=[c6, evq2])
            c8 = p.run("v", lambda e: e.tensor_tensor(s1row[0:1, :], mu2[0:1, :], rstd2[0:1, :],
                                                      op=ALU.mult), deps=[c6, c3])
            c9 = p.run("v", lambda e: e.tensor_scalar(cvU[0:1, :], s1row[0:1, :], -1.0, None,
                                                      op0=ALU.mult), deps=[c8, ev_ot])
            c10 = p.run("v", lambda e: e.tensor_scalar(cv2[0:2, :], cvU[0:2, :],
                                                       m2_pp[0:2, 0:1], None, op0=ALU.mult),
                        deps=[c9])
            ev_stats = c10
            ev_AB = None
            prevA = None
            prevB = None
            for oc in range(32):
                mA = p.run("t", lambda e, oc=oc: e.matmul(
                    PS[2][:, 0:TCH], ln2gb[0:1, oc * 128:(oc + 1) * 128], rstd2m[0:1, 0:TCH],
                    start=True, stop=True), deps=[c9, c10, ev_l2, prevA])
                mB = p.run("t", lambda e, oc=oc: e.matmul(
                    PS[3][:, 0:TCH], ln2gb[0:2, oc * 128:(oc + 1) * 128], cv2[0:2, 0:TCH],
                    start=True, stop=True), deps=[prevB, c10])
                h1 = p.run("v", lambda e, oc=oc: e.tensor_tensor(
                    hsc[:], hg[:, oc * TCH:(oc + 1) * TCH], PS[2][:, 0:TCH], op=ALU.mult),
                    deps=[mA, ev_AB])
                prevA = h1
                ev_AB = p.run("v", lambda e, oc=oc: e.tensor_tensor(
                    hn[:, oc * TCH:(oc + 1) * TCH], hsc[:], PS[3][:, 0:TCH], op=ALU.add),
                    deps=[h1, mB])
                prevB = ev_AB
            ev_hn = ev_AB
            prev_AB = ev_AB
            for tt in range(2):
                for ocol in range(2):
                    fb = tt
                    for hc in range(32):
                        p.run("t", lambda e, tt=tt, ocol=ocol, hc=hc: e.matmul(
                            PS[4 + tt][:, 0:512],
                            hn[:, hc * TCH + tt * 128: hc * TCH + (tt + 1) * 128],
                            bufB[:, hc * D + ocol * 512: hc * D + (ocol + 1) * 512],
                            start=(hc == 0), stop=False),
                            deps=([ev_hn, fout_free[(tt, ocol)], evq2,
                                   ff2_ps_free.get(tt)] if hc == 0 else []),
                            ev=False)
                    mme = p.run("t", lambda e, ocol=ocol, tt=tt: e.matmul(
                        PS[4 + tt][:, 0:512], onesb_r[0:1, :],
                        b2row[0:1, ocol * 512:(ocol + 1) * 512],
                        start=False, stop=True), deps=[ev_b2r, ev_or])
                    ev_f = p.run("v", lambda e, fb=fb, tt=tt: e.tensor_copy(
                        fout[fb][:], PS[4 + tt][:, 0:512]), deps=[mme, fout_free[(tt, 1 - ocol)]])
                    ff2_ps_free[tt] = ev_f
                    row0 = tc * TCH + tt * 128
                    ac = p.run("g", lambda e, row0=row0, ocol=ocol, fb=fb: e.dma_start(
                        out=x_out[row0:row0 + 128, ocol * 512:(ocol + 1) * 512], in_=fout[fb][:],
                        accum_op=ALU.add),
                        deps=[ev_f, ev_init] + (attn_accums[-4:] if tc == 0 else []), dma=True,
                        slot="f%d" % fb)
                    fout_free[(tt, ocol)] = ac
                    ffn_accums.append(ac)

        if fused is None:
            p.barrier_all("y")
            p.emit()
    return nc


def prep_block3(i, x_shards):
    H = 8
    Vq, Vk, Vv = _f32(i["L3_attn_Vq"]), _f32(i["L3_attn_Vk"]), _f32(i["L3_attn_Vv"])
    Uq, Uk, Uv = _f32(i["L3_attn_Uq"]), _f32(i["L3_attn_Uk"]), _f32(i["L3_attn_Uv"])
    bq, bk, bv = _f32(i["L3_attn_bq"]), _f32(i["L3_attn_bk"]), _f32(i["L3_attn_bv"])
    cat = np.concatenate([Vq.transpose(0, 2, 1), Vk.transpose(0, 2, 1), Vv.transpose(0, 2, 1)],
                         axis=2)
    vqkvT = np.ascontiguousarray(cat.transpose(1, 0, 2).reshape(128, H * 96))
    def _urows(U):
        return U.transpose(0, 2, 1).transpose(1, 0, 2).reshape(32, H * 128)
    uqkvT = np.ascontiguousarray(np.concatenate([_urows(Uq), _urows(Uk), _urows(Uv)], axis=0))
    bqkT = np.ascontiguousarray(np.concatenate([bq.T, bk.T], axis=1))
    bvfl = np.ascontiguousarray(bv.reshape(1, 1024))
    identf = np.eye(128, dtype=np.float32)
    onesf = np.ones((128, 512), dtype=np.float32)
    g1bc = np.ascontiguousarray(np.broadcast_to(_f32(i["L3_ffn_ln1_g"])[None, :], (128, 1024)))
    b1bc = np.ascontiguousarray(np.broadcast_to(_f32(i["L3_ffn_ln1_b"])[None, :], (128, 1024)))
    w1T = np.ascontiguousarray(_f32(i["L3_ffn_w1"]).T)
    b1pp = np.ascontiguousarray(_f32(i["L3_ffn_b1"]).reshape(32, 128).T)
    ln2gb = np.ascontiguousarray(np.stack([_f32(i["L3_ffn_ln2_g"]), _f32(i["L3_ffn_ln2_b"])],
                                          axis=0))
    w2T = np.ascontiguousarray(_f32(i["L3_ffn_w2"]).T)
    b2row = np.ascontiguousarray(_f32(i["L3_ffn_b2"]).reshape(1, 1024))
    common = dict(vqkvT=vqkvT, uqkvT=uqkvT, bqkT=bqkT, bvfl=bvfl, identf=identf, onesf=onesf,
                  g1bc=g1bc, b1bc=b1bc, w1T=w1T, b1pp=b1pp, ln2gb=ln2gb, w2T=w2T, b2row=b2row)
    return [dict(common, x_in=x_shards[c]) for c in range(NCORES)]


def run_block3(i, x):
    shards = _shard_x(x)
    nc = build_block3()
    maps = prep_block3(i, shards)
    res = run_bass_kernel_spmd(nc, maps, core_ids=list(range(NCORES)))
    return _gather_x([r["x_out"] for r in res.results])




def build_block0(with_ffn=True, fused=None):
    nc = (bass.Bass("TRN2", target_bir_lowering=False) if fused is None
          else _NCP(fused[0], fused[2]))
    H, HD = 16, 64
    SCALE = 1.0 / np.sqrt(HD)

    if fused is None:
        x_in = nc.dram_tensor("x_in", [TOK, D], F32, kind="ExternalInput").ap()
        x_out = nc.dram_tensor("x_out", [TOK, D], F32, kind="ExternalOutput").ap()
    else:
        x_in, x_out = fused[3], fused[4]
    xn_hbm = nc.dram_tensor("xn_hbm", [TOK, D], BF16, kind="Internal").ap()

    wqkT_h = nc.dram_tensor("wqkT", [H, 128, 1024], F32, kind="ExternalInput").ap()
    wvT_h = nc.dram_tensor("wvT", [128, 8 * 1024], F32, kind="ExternalInput").ap()
    woutT_h = nc.dram_tensor("woutT", [128, 8 * 1024], F32, kind="ExternalInput").ap()
    boutb_h = nc.dram_tensor("boutb", [128, D], F32, kind="ExternalInput").ap()
    cosT_h = nc.dram_tensor("cosT", [64, S], F32, kind="ExternalInput").ap()
    sinT_h = nc.dram_tensor("sinT", [64, S], F32, kind="ExternalInput").ap()
    identf = nc.dram_tensor("identf", [128, 128], F32, kind="ExternalInput").ap()
    onesf = nc.dram_tensor("onesf", [128, 512], F32, kind="ExternalInput").ap()
    wupT_h = nc.dram_tensor("wupT", [128, 8 * 64], F32, kind="ExternalInput").ap()
    wdnT_h = nc.dram_tensor("wdnT", [128, 32 * 64], F32, kind="ExternalInput").ap()
    iota64_h = nc.dram_tensor("iota64", [128, 64], F32, kind="ExternalInput").ap()
    scud_h = nc.dram_tensor("scud", [1, 2], F32, kind="ExternalInput").ap()
    bup_h = nc.dram_tensor("bup", [128, 4096], F32, kind="ExternalInput").ap()
    bdn_h = nc.dram_tensor("bdn", [128, 1024], F32, kind="ExternalInput").ap()
    ABupT_h = nc.dram_tensor("ABupT", [4096, 64], F32, kind="ExternalInput").ap()
    ABdnT_h = nc.dram_tensor("ABdnT", [8192, 32], F32, kind="ExternalInput").ap()
    selup_h = nc.dram_tensor("selup", [32, 1024], F32, kind="ExternalInput").ap()
    repbu_h = nc.dram_tensor("repbu", [64, 128], F32, kind="ExternalInput").ap()
    seldn_h = nc.dram_tensor("seldn", [64, 4096], F32, kind="ExternalInput").ap()
    repbd_h = nc.dram_tensor("repbd", [128, 128], F32, kind="ExternalInput").ap()
    triu_h = nc.dram_tensor("triuf", [64, 64], F32, kind="ExternalInput").ap()
    aenc_h = nc.dram_tensor("aenc", [128, 64], F32, kind="ExternalInput").ap()
    iotap_h = nc.dram_tensor("iotap", [128, 1], F32, kind="ExternalInput").ap()
    tstart_h = nc.dram_tensor("tstart", [1, 128], F32, kind="ExternalInput").ap()
    sentp_h = nc.dram_tensor("sentp", [128, 256], F32, kind="ExternalInput").ap()
    ABup16 = nc.dram_tensor("ABup16", [4096, 64], BF16, kind="Internal").ap()
    ABdn16 = nc.dram_tensor("ABdn16", [8192, 32], BF16, kind="Internal").ap()
    h_hbm = nc.dram_tensor("h_hbm", [TOK, 4096], BF16, kind="Internal").ap()
    h01_hbm = nc.dram_tensor("h01_hbm", [2 * TOK + 1024, 4096], BF16, kind="Internal").ap()
    o01_hbm = nc.dram_tensor("o01_hbm", [2 * TOK + 1024, 1024], BF16, kind="Internal").ap()
    srt_hbm = nc.dram_tensor("srt_hbm", [8, 2 * TOK], F32, kind="Internal").ap()
    scp_hbm = nc.dram_tensor("scp_hbm", [32768, 2], F32, kind="Internal").ap()

    with ExitStack() as st:
        def sb(name, shape, dt=BF16):
            return st.enter_context(nc.sbuf_tensor(name, shape, dt))

        onesb_r = sb("onesb_r", [1, 128])
        onesb_c = sb("onesb_c", [128, 1])
        onesf_r = sb("onesf_r", [1, 128], F32)
        boutb = sb("boutb_sb", [128, D], F32)
        identb = sb("identb", [128, 128])
        st2 = ExitStack()

        def sb2(name, shape, dt=BF16):
            return st2.enter_context(nc.sbuf_tensor(name, shape, dt))
        xnT = sb2("xnT", [128, NC * S])
        tiles = mk_tiles(st2, nc, with_xpx=True)
        wqk_sb = [sb2("wqk0", [128, 1024]), sb2("wqk1", [128, 1024])]
        wv_sb = sb2("wv_sb", [128, 8 * 1024])
        wout_sb = sb2("wout_sb", [128, 8 * 1024])
        V = sb2("V0", [128, 16 * 1040])
        cosT = sb2("cosT_sb", [64, S])
        sinT = sb2("sinT_sb", [64, S])
        qf = sb2("qf", [64, S])
        kf = qf
        qrot = sb2("qrot", [64, S])
        krot = sb2("krot", [64, S])
        rs1 = sb2("rs1", [32, S // 2])
        rs2 = sb2("rs2", [32, S // 2])
        sqscr0 = sb2("sqscr0", [64, S])
        normrow = sb2("normrow0", [1, 512], F32)
        scm = sb2("scm_0", [1, 1], F32)
        sc1 = sb2("sc1_0", [1, 1], F32)
        sc2 = sb2("sc2_0", [1, 1], F32)
        negb = sb2("negb0", [128, 1], F32)
        PT = [sb2("PT0_0", [128, 512]), sb2("PT1_0", [128, 512])]
        avsb = sb2("avsb", [64, 512])
        rzrow = sb2("rzrow", [1, 512], F32)
        AVn = sb2("AVn", [128, 8 * S])
        at2 = [tiles["xpx"][0], tiles["xpx"][1]]

        PS = [st.enter_context(nc.psum_tensor("b0ps%d" % i, [128, 512], F32)) for i in range(6)]
        PST = [st.enter_context(nc.psum_tensor("b0pst%d" % i, [128, 1024], BF16)) for i in range(2)]

        p = Prog(nc) if fused is None else fused[1]
        tiles["eps_ev"] = p.run("v", lambda e: e.memset(tiles["eps"][:], 1e-5))

        for ld_fn in [
            lambda e: e.dma_start(out=boutb[:], in_=boutb_h),
            lambda e: e.dma_start(out=identb[:], in_=identf),
            lambda e: e.dma_start(out=onesb_r[:], in_=onesf[0:1, 0:128]),
            lambda e: e.dma_start(out=onesb_c[:], in_=onesf[:, 0:1], allow_slow_non_contiguous=True),
            lambda e: e.dma_start(out=onesf_r[:], in_=onesf[0:1, 0:128]),
            lambda e: e.dma_start(out=wv_sb[:], in_=wvT_h),
            lambda e: e.dma_start(out=wout_sb[:], in_=woutT_h),
            lambda e: e.dma_start(out=cosT[:], in_=cosT_h),
            lambda e: e.dma_start(out=sinT[:], in_=sinT_h),
        ]:
            ev_const = p.run("g", ld_fn, dma=True, slot="const")

        Vap = V[:, :].rearrange("p (a s) -> p a s", s=65)
        ev_vones = p.run("v", lambda e: e.memset(Vap[:, :, 64:65], 1.0))

        xev = [None, None]
        prev_lnp = [None]
        tr_done = [None, None]
        init_stores = []

        def prologue_batch(b, dep_extra):
            for tloc in range(16):
                tt = b * 16 + tloc
                bb = tt % 2
                xt = tiles["xt"][bb]
                ld = p.run("y", lambda e, xt=xt, tt=tt: e.dma_start(
                    out=xt[:], in_=x_in[tt * 128:(tt + 1) * 128, :]),
                    deps=[xev[bb]] + dep_extra, dma=True, slot="x%d" % bb)
                rsx = emit_ln_stats(p, tiles, ld, xt, prev_lnp[0])
                xnf = tiles["xnf"][bb]
                lnev = p.run("v", lambda e, xt=xt, xnf=xnf: e.tensor_scalar(
                    xnf[:], xt[:], tiles["mean"][:, 0:1], tiles["rstd"][:, 0:1],
                    op0=ALU.subtract, op1=ALU.mult), deps=[rsx])
                prev_lnp[0] = lnev
                xnb = tiles["xnb"][bb]
                cb = p.run("v", lambda e, xnf=xnf, xnb=xnb: e.tensor_copy(xnb[:], xnf[:]), deps=[lnev])
                xnh_evs.append(p.run("y", lambda e, xnb=xnb, tt=tt: e.dma_start(
                    out=xn_hbm[tt * 128:(tt + 1) * 128, :], in_=xnb[:]), deps=[cb], dma=True,
                    slot="xnh"))
                so = p.run("v", lambda e, xt=xt, bb=bb: e.tensor_add(
                    tiles["xpx"][bb][:], xt[:], boutb[:]), deps=[lnev, ev_const, xev[bb]])
                stv = p.run("y", lambda e, tt=tt, bb=bb: e.dma_start(
                    out=x_out[tt * 128:(tt + 1) * 128, :], in_=tiles["xpx"][bb][:]),
                    deps=[so], dma=True, slot="xo%d" % bb)
                init_stores.append(stv)
                xev[bb] = so
                for c in range(NC):
                    pi = (tloc * NC + c) % 2
                    tr = p.run("t", lambda e, xnb=xnb, c=c, pi=pi: e.transpose(
                        PST[pi][:, 0:128], xnb[:, c * 128:(c + 1) * 128], identb[:]),
                        deps=[cb, tr_done[pi], ev_const])
                    tr_done[pi] = p.run("v", lambda e, tloc=tloc, c=c, pi=pi: e.tensor_copy(
                        xnT[:, c * S + tloc * 128: c * S + (tloc + 1) * 128], PST[pi][:, 0:128]),
                        deps=[tr])
            return tr_done + [prev_lnp[0]]

        attn_accums = []
        xnh_evs = []
        iter_dep = [None]
        at_free = [None, None]
        wqk_ev = [None, None]
        for b in range(NB):
            ln_done = prologue_batch(b, iter_dep)
            if b == 0:
                ev_init = [init_stores[-1], init_stores[-2]]
            at_free = [init_stores[-1], init_stores[-2]]
            ev_v = None
            for kt in range(16):
                for vc in range(2):
                    mme = None
                    for c in range(NC):
                        mme = p.run("t", lambda e, kt=kt, vc=vc, c=c: e.matmul(
                            PS[4][:, 0:512], xnT[:, c * S + kt * 128: c * S + (kt + 1) * 128],
                            wv_sb[:, c * 1024 + vc * 512: c * 1024 + (vc + 1) * 512],
                            start=(c == 0), stop=(c == NC - 1)),
                            deps=(ln_done + [ev_v, ev_const] + iter_dep if c == 0 else []),
                            ev=(c == NC - 1))
                    ev_v = p.run("v", lambda e, kt=kt, vc=vc: e.tensor_copy(
                        Vap[:, kt * 16 + vc * 8:kt * 16 + (vc + 1) * 8, 0:64],
                        PS[4][:, 0:512].rearrange("p (a s) -> p a s", s=64)),
                        deps=[mme, ev_vones])
            for h in range(H):
                hb = h % 2
                wl = p.run("g", lambda e, h=h, hb=hb: e.dma_start(
                    out=wqk_sb[hb][:], in_=wqkT_h[h]), deps=[wqk_ev[hb]], dma=True,
                    slot="wqk%d" % hb)
                ev_q = None
                ev_k = None
                for qc in range(4):
                    mmq = None
                    for c in range(NC):
                        mmq = p.run("t", lambda e, qc=qc, c=c, hb=hb: e.matmul(
                            PS[0][0:64, 0:512], wqk_sb[hb][:, c * 128: c * 128 + 64],
                            xnT[:, c * S + qc * 512: c * S + (qc + 1) * 512],
                            start=(c == 0), stop=(c == NC - 1)),
                            deps=([wl, ev_q] + iter_dep if c == 0 else []), ev=(c == NC - 1))
                    ev_q = p.run("v", lambda e, qc=qc: e.tensor_copy(
                        qf[:, qc * 512:(qc + 1) * 512], PS[0][0:64, 0:512]), deps=[mmq])

                def rope(srcb, dstb, dep):
                    HS = S // 2
                    e6 = dep
                    for hc in range(2):
                        sl = slice(hc * HS, (hc + 1) * HS)
                        e1 = p.run("v", lambda e, sl=sl: e.tensor_tensor(
                            rs1[:], srcb[0:32, sl], cosT[0:32, sl], op=ALU.mult), deps=[e6])
                        e2 = p.run("v", lambda e, sl=sl: e.tensor_tensor(
                            rs2[:], srcb[32:64, sl], sinT[32:64, sl], op=ALU.mult), deps=[e6])
                        e3 = p.run("v", lambda e, sl=sl: e.tensor_tensor(
                            dstb[0:32, sl], rs1[:], rs2[:], op=ALU.subtract), deps=[e1, e2])
                        e4 = p.run("v", lambda e, sl=sl: e.tensor_tensor(
                            rs1[:], srcb[0:32, sl], sinT[0:32, sl], op=ALU.mult), deps=[e3])
                        e5 = p.run("v", lambda e, sl=sl: e.tensor_tensor(
                            rs2[:], srcb[32:64, sl], cosT[32:64, sl], op=ALU.mult), deps=[e3])
                        e6 = p.run("v", lambda e, sl=sl: e.tensor_tensor(
                            dstb[32:64, sl], rs1[:], rs2[:], op=ALU.add), deps=[e4, e5])
                    return e6
                ev_qr = rope(qf, qrot, ev_q)
                for qc in range(4):
                    mmk = None
                    for c in range(NC):
                        mmk = p.run("t", lambda e, qc=qc, c=c, hb=hb: e.matmul(
                            PS[1][0:64, 0:512], wqk_sb[hb][:, c * 128 + 64: c * 128 + 128],
                            xnT[:, c * S + qc * 512: c * S + (qc + 1) * 512],
                            start=(c == 0), stop=(c == NC - 1)),
                            deps=([wl, ev_k] if c == 0 else []), ev=(c == NC - 1))
                    ev_k = p.run("v", lambda e, qc=qc: e.tensor_copy(
                        kf[:, qc * 512:(qc + 1) * 512], PS[1][0:64, 0:512]),
                        deps=[mmk, ev_qr if qc == 0 else None])
                wqk_ev[hb] = ev_k
                ev_kr = rope(kf, krot, ev_k)
                def maxnorm(srcb, dst_sc, dep0, extra):
                    z = p.run("v", lambda e, dst_sc=dst_sc: e.memset(dst_sc[0:1, 0:1], 0.0),
                              deps=[extra])
                    sq = p.run("s", lambda e, srcb=srcb: e.activation(
                        out=sqscr0[:], in_=srcb[:], func=AF.Square), deps=[dep0, extra])
                    evn = extra
                    for qc in range(4):
                        mm = p.run("t", lambda e, qc=qc: e.matmul(
                            PS[3][0:1, 0:512], onesb_c[0:64, 0:1],
                            sqscr0[:, qc * 512:(qc + 1) * 512],
                            start=True, stop=True), deps=[sq, ev_const, evn])
                        cpn = p.run("v", lambda e: e.tensor_copy(normrow[0:1, :], PS[3][0:1, 0:512]),
                                    deps=[mm, z])
                        rm = p.run("v", lambda e: e.reduce_max(scm[0:1, 0:1], normrow[0:1, :],
                                                               axis=AX.X), deps=[cpn])
                        evn = p.run("v", lambda e, dst_sc=dst_sc: e.tensor_tensor(
                            dst_sc[0:1, 0:1], dst_sc[0:1, 0:1], scm[0:1, 0:1], op=ALU.max),
                            deps=[rm])
                    return evn
                mq_ev = maxnorm(qrot, sc1, ev_qr, iter_dep[0])
                mk_ev = maxnorm(krot, sc2, ev_kr, mq_ev)
                ev_nk = mk_ev
                bb2e = p.run("v", lambda e: e.tensor_tensor(sc1[0:1, 0:1], sc1[0:1, 0:1],
                                                            sc2[0:1, 0:1], op=ALU.mult),
                             deps=[mq_ev, mk_ev])
                s1e = p.run("s", lambda e: e.activation(out=sc1[0:1, 0:1], in_=sc1[0:1, 0:1],
                                                        func=AF.Sqrt), deps=[bb2e])
                bb3 = p.run("v", lambda e: e.tensor_scalar(sc1[0:1, 0:1], sc1[0:1, 0:1], -SCALE,
                                                           None, op0=ALU.mult), deps=[s1e])
                mmb = p.run("t", lambda e: e.matmul(PS[3][:, 0:1], onesf_r[0:1, :], sc1[0:1, 0:1],
                                                    start=True, stop=True), deps=[bb3, ev_nk])
                ev_negb = p.run("v", lambda e: e.tensor_copy(negb[:], PS[3][:, 0:1]), deps=[mmb])

                s_free = [ev_negb, None]
                pt_free = [None, None]
                av_free = ev_negb
                for qc in range(4):
                    last_av = None
                    for kt in range(16):
                        pi = kt % 2
                        ms = p.run("t", lambda e, qc=qc, kt=kt, pi=pi: e.matmul(
                            PS[pi][:, 0:512], krot[:, kt * 128:(kt + 1) * 128],
                            qrot[:, qc * 512:(qc + 1) * 512], start=True, stop=True),
                            deps=[ev_qr, ev_kr, s_free[pi]])
                        pe = p.run("s", lambda e, pi=pi: e.activation(
                            out=PT[pi][:], in_=PS[pi][:, 0:512], func=AF.Exp, scale=SCALE,
                            bias=negb[:, 0:1]), deps=[ms, pt_free[pi], ev_negb])
                        s_free[pi] = pe
                        last_av = p.run("t", lambda e, kt=kt, pi=pi, h=h: e.matmul(
                            PS[2][0:65, 0:512],
                            V[:, kt * 1040 + h * 65: kt * 1040 + (h + 1) * 65],
                            PT[pi][:], start=(kt == 0), stop=(kt == 15)),
                            deps=([pe, av_free, ev_v] if kt == 0 else [pe]))
                        pt_free[pi] = last_av
                    rce = p.run("v", lambda e: e.reciprocal(
                        rzrow[0:1, 0:512], PS[2][64:65, 0:512]), deps=[last_av, av_free])
                    cpe = p.run("v", lambda e: e.tensor_copy(
                        avsb[:, 0:512], PS[2][0:64, 0:512]), deps=[rce])
                    mbc = p.run("t", lambda e: e.matmul(
                        PS[3][0:64, 0:512], onesf_r[0:1, 0:64], rzrow[0:1, 0:512],
                        start=True, stop=True), deps=[cpe, ev_negb, av_free])
                    av_free = p.run("v", lambda e, qc=qc, h=h: e.tensor_tensor(
                        AVn[(h % 2) * 64:(h % 2) * 64 + 64,
                            (h // 2) * S + qc * 512:(h // 2) * S + (qc + 1) * 512],
                        avsb[:, 0:512], PS[3][0:64, 0:512], op=ALU.mult),
                        deps=[mbc])
                iter_dep = [av_free, s_free[0], s_free[1]]
            for qt in range(16):
                for oc in range(2):
                    m4 = None
                    for h2 in range(8):
                        m4 = p.run("t", lambda e, qt=qt, oc=oc, h2=h2: e.matmul(
                            PS[4][:, 0:512],
                            AVn[0:64, h2 * S + qt * 128: h2 * S + (qt + 1) * 128],
                            wout_sb[0:64, h2 * 1024 + oc * 512: h2 * 1024 + (oc + 1) * 512],
                            start=(h2 == 0), stop=(h2 == 7)),
                            deps=(iter_dep + [at_free[0]] if h2 == 0 else []), ev=(h2 == 7))
                    m5 = None
                    for h2 in range(8):
                        m5 = p.run("t", lambda e, qt=qt, oc=oc, h2=h2: e.matmul(
                            PS[5][:, 0:512],
                            AVn[64:128, h2 * S + qt * 128: h2 * S + (qt + 1) * 128],
                            wout_sb[64:128, h2 * 1024 + oc * 512: h2 * 1024 + (oc + 1) * 512],
                            start=(h2 == 0), stop=(h2 == 7)),
                            deps=([at_free[1]] if h2 == 0 else []), ev=(h2 == 7))
                    c4 = p.run("v", lambda e: e.tensor_copy(at2[0][:, 0:512], PS[4][:, 0:512]),
                               deps=[m4])
                    c5 = p.run("v", lambda e: e.tensor_tensor(at2[1][:, 0:512], at2[0][:, 0:512],
                                                              PS[5][:, 0:512], op=ALU.add),
                               deps=[m5, c4])
                    at_free[0] = c5
                    row0 = b * S + qt * 128
                    ac = p.run("g", lambda e, row0=row0, oc=oc: e.dma_start(
                        out=x_out[row0:row0 + 128, oc * 512:(oc + 1) * 512], in_=at2[1][:, 0:512],
                        accum_op=ALU.add), deps=[c5] + ev_init, dma=True, slot="ao")
                    at_free[1] = ac
                    attn_accums.append(ac)
            iter_dep = iter_dep + [at_free[1]]

        st2.close()
        if with_ffn:
            NTL = 128
            SENT = 12000.0
            stf = ExitStack()

            def sbf(name, shape, dt=BF16):
                return stf.enter_context(nc.sbuf_tensor(name, shape, dt))
            wupT = sbf("wupT_sb", [128, 8 * 64])
            wdnT = sbf("wdnT_sb", [128, 32 * 64])
            logitsb = sbf("logitsb", [128, 64], F32)
            m8k = sbf("m8k", [128, 8], F32)
            eqs0 = sbf("eqs0", [128, 64], F32)
            eqs1 = sbf("eqs1", [128, 64], F32)
            prb = sbf("prb", [128, 4], F32)
            idxf0 = sbf("idxf0", [128, 2], F32)
            iota64 = sbf("iota64_sb", [128, 64], F32)
            scup = sbf("scupk", [128, 2], F32)
            bupb = sbf("bupbk", [128, 4096], F32)
            bdnb = sbf("bdnbk", [128, 1024], F32)
            identfb = sbf("identfb", [128, 128], F32)
            selup_sb = sbf("selup_sb", [32, 1024])
            repbu_sb = sbf("repbu_sb", [64, 128])
            seldn_sb = sbf("seldn_sb", [64, 4096])
            repbd_sb = sbf("repbd_sb", [128, 128])
            triu_sb = sbf("triu_sb", [64, 64])
            aenc_sb = sbf("aenc_sb", [128, 64], F32)
            iotap_sb = sbf("iotap_sb", [128, 1], F32)
            tstart_sb = sbf("tstart_sb", [1, 128], F32)
            sentp_sb = sbf("sentp_sb", [128, 256], F32)
            onescf = sbf("onescf", [64, 1], F32)
            ET = sbf("ET_sb", [128, 128], F32)
            xnti = [sbf("xntik%d" % k, [128, 1024]) for k in range(2)]
            hTt = sbf("hTtk", [128, 32 * 128])
            tmpr = sbf("tmpr", [1, 128], F32)
            rowA = sbf("rowA", [1, 512], F32)
            rowB = sbf("rowB", [1, 512], F32)
            PL = [sbf("PLp%d" % k, [128, 2], F32) for k in range(2)]
            TIK = [stf.enter_context(nc.sbuf_tensor("TIKp%d" % k, [128, 1], I32))
                   for k in range(2)]
            SIK = [stf.enter_context(nc.sbuf_tensor("SIKp%d" % k, [128, 1], I32))
                   for k in range(2)]
            ABI = [stf.enter_context(nc.sbuf_tensor("ABIp%d" % k, [128, 1], I32))
                   for k in range(2)]
            idsc = [sbf("idscp%d" % k, [128, 4], F32) for k in range(2)]
            tot1 = sbf("tot1", [64, 1], F32)
            totg = sbf("totg", [64, 1], F32)
            pbase = sbf("pbase", [64, 1], F32)
            pendb = sbf("pendb", [64, 1], F32)
            pc128 = sbf("pc128", [64, 1], F32)
            pci = stf.enter_context(nc.sbuf_tensor("pci", [64, 1], I32))
            pc16 = sbf("pc16", [64, 1])
            scrE = sbf("scrE", [64, 128], F32)
            scrE2 = sbf("scrE2", [64, 128], F32)
            scrC = sbf("scrC", [64, 512], F32)
            PAY = sbf("PAYs", [128, 128], F32)
            CIDF = sbf("CIDF", [128, 64], F32)
            CIDX = stf.enter_context(nc.sbuf_tensor("CIDX", [128, 64], I32))
            PRL = sbf("PRLs", [128, 64], F32)

            ev_cf = None
            for ld_fn in [
                lambda e: e.dma_start(out=wupT[:], in_=wupT_h),
                lambda e: e.dma_start(out=wdnT[:], in_=wdnT_h),
                lambda e: e.dma_start(out=iota64[:], in_=iota64_h),
                lambda e: e.dma_start(out=scup[0:1, 0:2], in_=scud_h),
                lambda e: e.dma_start(out=bupb[:], in_=bup_h),
                lambda e: e.dma_start(out=bdnb[:], in_=bdn_h),
                lambda e: e.dma_start(out=identfb[:], in_=identf),
                lambda e: e.dma_start(out=selup_sb[:], in_=selup_h),
                lambda e: e.dma_start(out=repbu_sb[:], in_=repbu_h),
                lambda e: e.dma_start(out=seldn_sb[:], in_=seldn_h),
                lambda e: e.dma_start(out=repbd_sb[:], in_=repbd_h),
                lambda e: e.dma_start(out=triu_sb[:], in_=triu_h),
                lambda e: e.dma_start(out=aenc_sb[:], in_=aenc_h),
                lambda e: e.dma_start(out=iotap_sb[:], in_=iotap_h),
                lambda e: e.dma_start(out=tstart_sb[:], in_=tstart_h),
                lambda e: e.dma_start(out=sentp_sb[:], in_=sentp_h),
                lambda e: e.dma_start(out=onescf[:], in_=onesf[0:64, 0:1],
                                      allow_slow_non_contiguous=True),
            ]:
                ev_cf = p.run("g", ld_fn, deps=iter_dep, dma=True, slot="constf")
            mmsc = p.run("t", lambda e: e.matmul(PS[3][:, 0:2], onesf_r[0:1, :],
                                                 scup[0:1, 0:2], start=True, stop=True),
                         deps=[ev_cf] + iter_dep)
            ev_sc = p.run("v", lambda e: e.tensor_copy(scup[:, 0:2], PS[3][:, 0:2]),
                          deps=[mmsc])
            xnh_last = xnh_evs[-1]

            def top2idx(logit_ev, dep_extra):
                mx = p.run("v", lambda e: e.max(out=m8k[:], in_=logitsb[:]),
                           deps=[logit_ev] + dep_extra)
                d1 = p.run("v", lambda e: e.tensor_tensor(prb[:, 2:3], m8k[:, 1:2],
                                                          m8k[:, 0:1], op=ALU.subtract),
                           deps=[mx])
                d2 = p.run("s", lambda e: e.activation(out=prb[:, 3:4], in_=prb[:, 2:3],
                                                       func=AF.Exp), deps=[d1])
                d3 = p.run("v", lambda e: e.tensor_scalar(prb[:, 2:3], prb[:, 3:4], 1.0,
                                                          None, op0=ALU.add), deps=[d2])
                d4 = p.run("v", lambda e: e.reciprocal(prb[:, 0:1], prb[:, 2:3]), deps=[d3])
                d5 = p.run("v", lambda e: e.tensor_tensor(prb[:, 1:2], prb[:, 3:4],
                                                          prb[:, 0:1], op=ALU.mult),
                           deps=[d4])
                e0 = p.run("v", lambda e: e.tensor_scalar(eqs0[:], logitsb[:], m8k[:, 0:1],
                                                          None, op0=ALU.is_ge), deps=[mx])
                e0b = p.run("v", lambda e: e.tensor_tensor(eqs1[:], eqs0[:], iota64[:],
                                                           op=ALU.mult), deps=[e0, ev_cf])
                e0c = p.run("v", lambda e: e.reduce_max(idxf0[:, 0:1], eqs1[:], axis=AX.X),
                            deps=[e0b])
                e1 = p.run("v", lambda e: e.tensor_scalar(eqs1[:], logitsb[:], m8k[:, 1:2],
                                                          None, op0=ALU.is_ge), deps=[e0b])
                e1b = p.run("v", lambda e: e.tensor_tensor(eqs1[:], eqs1[:], eqs0[:],
                                                           op=ALU.subtract), deps=[e1])
                e1c = p.run("v", lambda e: e.tensor_tensor(eqs1[:], eqs1[:], iota64[:],
                                                           op=ALU.mult), deps=[e1b])
                e1d = p.run("v", lambda e: e.reduce_max(idxf0[:, 1:2], eqs1[:], axis=AX.X),
                            deps=[e1c])
                return [e0c, e1d, d5]

            def collect_tile(tt, OH, top_evs, rb):
                evs = []
                prev = top_evs
                for kk, (srcb, col, aoff) in enumerate([
                        (idxf0, 0, 0), (idxf0, 1, TOK), (prb, 0, 0), (prb, 1, TOK)]):
                    tr = p.run("t", lambda e, srcb=srcb, col=col: e.transpose(
                        PS[5][0:1, 0:128], srcb[:, col:col + 1], identfb[:]),
                        deps=prev + [ev_cf])
                    cp = p.run("v", lambda e: e.tensor_copy(tmpr[:], PS[5][0:1, 0:128]),
                               deps=[tr])
                    if kk < 2:
                        mm = p.run("t", lambda e: e.matmul(
                            PS[4][0:64, 0:128], onesf_r[0:1, 0:64], tmpr[:],
                            start=True, stop=True), deps=[cp])
                        oh = p.run("v", lambda e, aoff=aoff, tt=tt: e.tensor_tensor(
                            OH[:, aoff + tt * 128: aoff + (tt + 1) * 128],
                            PS[4][0:64, 0:128],
                            iotap_sb[0:64, 0:1].to_broadcast([64, 128]),
                            op=ALU.is_equal), deps=[mm])
                        evs.append(oh)
                        prev = [oh, mm]
                    else:
                        dw = p.run("y", lambda e, aoff=aoff, tt=tt, rb=rb: e.dma_start(
                            out=srt_hbm[rb + 2:rb + 3,
                                        aoff + tt * 128: aoff + (tt + 1) * 128],
                            in_=tmpr[:]), deps=[cp], dma=True, slot="pw")
                        evs.append(dw)
                        prev = [dw]
                return evs

            def sort_stage(OH, SCA, SCB, oh_evs, prob_evs, rb, so, et_free):
                HL = TOK
                ev_cnt = []
                ev_t1 = None
                for half in range(2):
                    sl = slice(half * HL, (half + 1) * HL)
                    ev = p.run("v", lambda e, sl=sl: e.tensor_copy(SCA[:], OH[:, sl]),
                               deps=oh_evs + [ev_t1] +
                               ([ev_cnt[-1]] if ev_cnt else []))
                    cur, nxt = SCA, SCB
                    for lv in range(12):
                        s_ = 1 << lv
                        ev = p.run("v", lambda e, cur=cur, nxt=nxt, s_=s_: e.tensor_tensor(
                            nxt[:, s_:HL], cur[:, s_:HL], cur[:, 0:HL - s_], op=ALU.add),
                            deps=[ev])
                        ev = p.run("v", lambda e, cur=cur, nxt=nxt, s_=s_: e.tensor_copy(
                            nxt[:, 0:s_], cur[:, 0:s_]), deps=[ev])
                        cur, nxt = nxt, cur
                    if half == 0:
                        ev_t1 = p.run("v", lambda e: e.tensor_copy(
                            tot1[:], SCA[:, HL - 1:HL]), deps=[ev])
                    else:
                        ev = p.run("v", lambda e: e.tensor_scalar(
                            SCA[:], SCA[:], tot1[:, 0:1], None, op0=ALU.add),
                            deps=[ev, ev_t1])
                        ev_t1 = p.run("v", lambda e: e.tensor_copy(
                            totg[:], SCA[:, HL - 1:HL]), deps=[ev])
                    evp = p.run("v", lambda e, sl=sl: e.tensor_copy(SCB[:], OH[:, sl]),
                                deps=[ev, ev_t1])
                    evp = p.run("v", lambda e: e.tensor_tensor(SCB[:], SCB[:], SCA[:],
                                                               op=ALU.mult), deps=[evp])
                    dw = None
                    for ch in range(8):
                        mm = p.run("t", lambda e, ch=ch: e.matmul(
                            PS[5][0:1, 0:512], onescf[:], SCB[:, ch * 512:(ch + 1) * 512],
                            start=True, stop=True), deps=[evp, dw, ev_cf])
                        cp = p.run("v", lambda e: e.tensor_copy(rowA[:], PS[5][0:1, 0:512]),
                                   deps=[mm])
                        dw = p.run("y", lambda e, half=half, ch=ch, rb=rb: e.dma_start(
                            out=srt_hbm[rb:rb + 1,
                                        half * HL + ch * 512: half * HL + (ch + 1) * 512],
                            in_=rowA[:]), deps=[cp], dma=True, slot="cw")
                        ev_cnt.append(dw)
                q1 = p.run("v", lambda e: e.tensor_scalar(pc128[:], totg[:], 127.0, None,
                                                          op0=ALU.add), deps=[ev_t1])
                q2 = p.run("v", lambda e: e.tensor_scalar(pc128[:], pc128[:], 1.0 / 128.0,
                                                          None, op0=ALU.mult), deps=[q1])
                q3 = p.run("v", lambda e: e.tensor_copy(pci[:], pc128[:]), deps=[q2])
                q4 = p.run("v", lambda e: e.tensor_copy(pc128[:], pci[:]), deps=[q3])
                q5 = p.run("v", lambda e: e.tensor_copy(pc16[:], pc128[:]), deps=[q4])
                mmq = p.run("t", lambda e: e.matmul(PS[5][0:64, 0:1], triu_sb[:], pc16[:],
                                                    start=True, stop=True),
                            deps=[q5, ev_cf])
                q6 = p.run("v", lambda e: e.tensor_scalar(pbase[:], PS[5][0:64, 0:1], 128.0,
                                                          None, op0=ALU.mult), deps=[mmq])
                q7 = p.run("v", lambda e: e.tensor_scalar(pendb[:], pc128[:], 128.0, None,
                                                          op0=ALU.mult), deps=[q4, q6])
                q8 = p.run("v", lambda e: e.tensor_tensor(pendb[:], pendb[:], pbase[:],
                                                          op=ALU.add), deps=[q7])
                mmt = p.run("t", lambda e: e.matmul(PS[4][0:64, 0:128], onesf_r[0:1, 0:64],
                                                    tstart_sb[:], start=True, stop=True),
                            deps=[q8, ev_cf])
                r1 = p.run("v", lambda e: e.tensor_scalar(scrE[:], PS[4][0:64, 0:128],
                                                          pbase[:, 0:1], None,
                                                          op0=ALU.subtract), deps=[mmt, q6])
                r2 = p.run("v", lambda e: e.tensor_scalar(scrE2[:], scrE[:], 0.0, None,
                                                          op0=ALU.is_ge), deps=[r1])
                r3 = p.run("v", lambda e: e.tensor_scalar(scrE[:], PS[4][0:64, 0:128],
                                                          pendb[:, 0:1], None,
                                                          op0=ALU.subtract), deps=[r2, q8])
                r4 = p.run("v", lambda e: e.tensor_scalar(scrE[:], scrE[:], 0.0, None,
                                                          op0=ALU.is_lt), deps=[r3])
                r5 = p.run("v", lambda e: e.tensor_tensor(scrE[:], scrE[:], scrE2[:],
                                                          op=ALU.mult), deps=[r4])
                r6 = p.run("v", lambda e: e.tensor_scalar(scrE[:], scrE[:],
                                                          iotap_sb[0:64, 0:1], None,
                                                          op0=ALU.mult), deps=[r5])
                mmr = p.run("t", lambda e: e.matmul(PS[4][0:1, 0:128], onescf[:], scrE[:],
                                                    start=True, stop=True), deps=[r6])
                r7 = p.run("v", lambda e: e.tensor_copy(tmpr[:], PS[4][0:1, 0:128]),
                           deps=[mmr])
                mrr = p.run("t", lambda e: e.matmul(PS[4][:, 0:128], onesf_r[0:1, :],
                                                    tmpr[:], start=True, stop=True),
                            deps=[r7])
                ev_et = p.run("v", lambda e: e.tensor_copy(ET[:], PS[4][:, 0:128]),
                              deps=[mrr] + et_free)
                ev_cpr = []
                dw = None
                lr = None
                for ch in range(16):
                    c1 = p.run("v", lambda e, ch=ch: e.tensor_copy(
                        scrC[:], OH[:, ch * 512:(ch + 1) * 512]), deps=[q6, dw] + oh_evs)
                    c2 = p.run("v", lambda e: e.tensor_scalar(scrC[:], scrC[:],
                                                              pbase[:, 0:1], None,
                                                              op0=ALU.mult), deps=[c1])
                    mm = p.run("t", lambda e: e.matmul(PS[5][0:1, 0:512], onescf[:],
                                                       scrC[:], start=True, stop=True),
                               deps=[c2])
                    lr = p.run("y", lambda e, ch=ch, rb=rb: e.dma_start(
                        out=rowA[:], in_=srt_hbm[rb:rb + 1, ch * 512:(ch + 1) * 512]),
                        deps=ev_cnt + [lr, dw], dma=True, slot="cr")
                    a1 = p.run("v", lambda e: e.tensor_tensor(rowB[:], rowA[:],
                                                              PS[5][0:1, 0:512],
                                                              op=ALU.add), deps=[mm, lr])
                    a2 = p.run("v", lambda e, so=so: e.tensor_scalar(
                        rowB[:], rowB[:], float(so - 1), None, op0=ALU.add), deps=[a1])
                    dw = p.run("y", lambda e, ch=ch, rb=rb: e.dma_start(
                        out=srt_hbm[rb + 1:rb + 2, ch * 512:(ch + 1) * 512], in_=rowB[:]),
                        deps=[a2], dma=True, slot="cw2")
                    ev_cpr.append(dw)
                ld1 = p.run("y", lambda e, rb=rb: e.dma_start(
                    out=CIDF[:],
                    in_=srt_hbm[rb + 1:rb + 2, :].rearrange("o (p c) -> (o p) c", p=128)),
                    deps=ev_cpr, dma=True, slot="rl1")
                ld2 = p.run("y", lambda e, rb=rb: e.dma_start(
                    out=PRL[:],
                    in_=srt_hbm[rb + 2:rb + 3, :].rearrange("o (p c) -> (o p) c", p=128)),
                    deps=prob_evs, dma=True, slot="rl2")
                ci = p.run("v", lambda e: e.tensor_copy(CIDX[:], CIDF[:]), deps=[ld1])
                pa1 = p.run("v", lambda e: e.tensor_copy(
                    PAY[:].rearrange("p (k two) -> p k two", two=2)[:, :, 0:1],
                    aenc_sb[:].rearrange("p (k two) -> p k two", two=1)), deps=[ev_cf])
                pa2 = p.run("v", lambda e: e.tensor_copy(
                    PAY[:].rearrange("p (k two) -> p k two", two=2)[:, :, 1:2],
                    PRL[:].rearrange("p (k two) -> p k two", two=1)), deps=[ld2, pa1])
                ms = p.run("g", lambda e, so=so: e.dma_start(
                    out=scp_hbm[so:so + 16384, :].rearrange("(p k) c -> p (k c)", p=128),
                    in_=sentp_sb[:]), deps=[ev_cf], dma=True, slot="sm")
                sc_last = [None] * 4
                for c in range(64):
                    sv = p.run("g", lambda e, c=c: e.indirect_dma_start(
                        out=scp_hbm, out_offset=IndirectOffsetOnAxis(CIDX[:, c:c + 1], 0),
                        in_=PAY[:, 2 * c:2 * c + 2], in_offset=None),
                        deps=[ci, pa2, ms, sc_last[c % 4]], dma=True, slot="sc%d" % (c % 4))
                    sc_last[c % 4] = sv
                return sc_last, ev_et

            def moe_tiles(tag, NCHK, ODIM, gsrc_hbm, out_hbm, so,
                          AB16, abrows, sel_sb, rep_sb, rep_base, arep_w,
                          sc_last, ev_et, gdep_extra):
                stt = ExitStack()
                IDIM = NCHK * 128
                NOC = ODIM // 512
                NG = max(1, NOC // 4)
                GOC = NOC // NG
                NGB = 4
                Ksb = stt.enter_context(nc.sbuf_tensor(tag + "K", [128, NCHK * ODIM], BF16))
                Gb = [stt.enter_context(nc.sbuf_tensor(tag + "G%d" % k, [128, IDIM], BF16))
                      for k in range(NGB)]
                GTb = [stt.enter_context(nc.sbuf_tensor(tag + "GT%d" % k, [128, IDIM],
                                                        BF16)) for k in range(2)]
                Yb = [stt.enter_context(nc.sbuf_tensor(tag + "Y%d" % k, [128, ODIM], BF16))
                      for k in range(2)]
                ABg = [stt.enter_context(nc.sbuf_tensor(tag + "AB%d" % k,
                                                        [128, arep_w], BF16))
                       for k in range(2)]
                arep = [stt.enter_context(nc.sbuf_tensor(tag + "ar%d" % k,
                                                         [128, NCHK * arep_w], BF16))
                        for k in range(2)]
                brep = [stt.enter_context(nc.sbuf_tensor(tag + "br%d" % k, [128, arep_w],
                                                         BF16)) for k in range(2)]
                PLall = stt.enter_context(nc.sbuf_tensor(tag + "PLa", [128, 256], F32))
                IDt = stt.enter_context(nc.sbuf_tensor(tag + "IDt", [128, 128], F32))
                PRT = stt.enter_context(nc.sbuf_tensor(tag + "PRT", [128, 128], F32))
                W1s = stt.enter_context(nc.sbuf_tensor(tag + "W1s", [128, 128], F32))
                TOKf = stt.enter_context(nc.sbuf_tensor(tag + "TOKf", [128, 128], F32))
                ABf = stt.enter_context(nc.sbuf_tensor(tag + "ABf", [128, 128], F32))
                TIKa = stt.enter_context(nc.sbuf_tensor(tag + "TIKa", [128, 128], I32))
                SIKa = stt.enter_context(nc.sbuf_tensor(tag + "SIKa", [128, 128], I32))
                ABIa = stt.enter_context(nc.sbuf_tensor(tag + "ABIa", [128, 128], I32))

                dpl = p.run("y", lambda e: e.dma_start(
                    out=PLall[:],
                    in_=scp_hbm[so:so + 16384, :].rearrange("(p k) c -> p (k c)", p=128)),
                    deps=sc_last, dma=True, slot="pla")
                c1 = p.run("v", lambda e: e.tensor_copy(
                    W1s[:], PLall[:].rearrange("p (k c) -> p k c", c=2)[:, :, 0:1]),
                    deps=[dpl])
                tr1 = p.run("t", lambda e: e.transpose(PS[5][:, 0:128], W1s[:],
                                                       identfb[:]), deps=[c1, ev_cf])
                c2 = p.run("v", lambda e: e.tensor_copy(IDt[:], PS[5][:, 0:128]),
                           deps=[tr1])
                c3 = p.run("v", lambda e: e.tensor_copy(
                    W1s[:], PLall[:].rearrange("p (k c) -> p k c", c=2)[:, :, 1:2]),
                    deps=[c2])
                tr2 = p.run("t", lambda e: e.transpose(PS[5][:, 0:128], W1s[:],
                                                       identfb[:]), deps=[c3, c2])
                c4 = p.run("v", lambda e: e.tensor_copy(PRT[:], PS[5][:, 0:128]),
                           deps=[tr2])
                q1 = p.run("v", lambda e: e.tensor_scalar(W1s[:], IDt[:], 4096.0, None,
                                                          op0=ALU.is_ge), deps=[c4])
                q2 = p.run("v", lambda e: e.tensor_scalar(W1s[:], W1s[:], -4096.0, None,
                                                          op0=ALU.mult), deps=[q1])
                q3 = p.run("v", lambda e: e.tensor_tensor(TOKf[:], IDt[:], W1s[:],
                                                          op=ALU.add), deps=[q2])
                q4 = p.run("v", lambda e: e.tensor_scalar(TOKf[:], TOKf[:], 4095.0, None,
                                                          op0=ALU.min), deps=[q3])
                q5 = p.run("v", lambda e: e.tensor_copy(TIKa[:], TOKf[:]), deps=[q4])
                q6 = p.run("v", lambda e: e.tensor_copy(SIKa[:], IDt[:]), deps=[q5])
                a1 = p.run("v", lambda e: e.tensor_scalar(ABf[:], ET[:], float(abrows),
                                                          None, op0=ALU.mult),
                           deps=[q6, ev_et])
                a2 = p.run("v", lambda e: e.tensor_scalar(ABf[:], ABf[:],
                                                          iotap_sb[:, 0:1], None,
                                                          op0=ALU.add), deps=[a1])
                a3 = p.run("v", lambda e: e.tensor_copy(ABIa[:], ABf[:]), deps=[a2])
                pre_ev = a3

                NAB = 4
                del ABg[:]
                for k in range(NAB):
                    ABg.append(stt.enter_context(nc.sbuf_tensor(
                        tag + "ABx%d" % k, [128, arep_w], BF16)))
                ev_g = [None] * NGB
                ev_gt = [None, None]
                ev_ab = [None] * NAB
                ev_y = [None, None]
                ev_K = [None] * NCHK
                ev_kw = [None] * NCHK
                ev_kwlast = [None, None]
                ev_psA = None
                ev_psB = None
                ev_psY = {}
                ev_trp = [None, None]
                sb_last = [None] * 8
                dg_ev = [None] * NGB
                dab_ev = [None] * NAB
                prep_st = {}
                nhalf = (NCHK + 15) // 16
                GKT = ([6, 7] if NCHK == 8 else [c for c in range(NCHK) if c % 4 == 3])

                def gath(t):
                    gi = t % NGB
                    ai = t % NAB
                    dab_ev[ai] = p.run("g", lambda e, t=t, ai=ai, AB16=AB16:
                                       e.indirect_dma_start(
                        out=ABg[ai][0:abrows, :], out_offset=None, in_=AB16,
                        in_offset=IndirectOffsetOnAxis(ABIa[0:abrows, t:t + 1], 0)),
                        deps=[pre_ev, ev_ab[ai], ev_ab16], dma=True, slot="gab%d" % ai)
                    dg_ev[gi] = p.run("g", lambda e, t=t, gi=gi, gsrc_hbm=gsrc_hbm:
                                      e.indirect_dma_start(
                        out=Gb[gi][:], out_offset=None, in_=gsrc_hbm,
                        in_offset=IndirectOffsetOnAxis(TIKa[:, t:t + 1], 0)),
                        deps=[pre_ev, ev_g[gi]] + gdep_extra, dma=True,
                        slot="gg%d" % gi)

                def prep(t):
                    pi = t % 2
                    gi = t % NGB
                    ai = t % NAB
                    fold = p.run("s", lambda e, t=t, gi=gi: e.activation(
                        out=Gb[gi][:], in_=Gb[gi][:], func=AF.Identity,
                        scale=PRT[:, t:t + 1]), deps=[dg_ev[gi]])
                    evt = None
                    for c in range(NCHK):
                        cb = c % 2
                        tr = p.run("t", lambda e, c=c, gi=gi, cb=cb: e.transpose(
                            PST[cb][:, 0:128], Gb[gi][:, c * 128:(c + 1) * 128],
                            identb[:]), deps=[fold, ev_trp[cb]])
                        evt = p.run("s", lambda e, c=c, pi=pi, cb=cb: e.activation(
                            out=GTb[pi][:, c * 128:(c + 1) * 128], in_=PST[cb][:, 0:128],
                            func=AF.Copy),
                            deps=[tr] + ([ev_gt[pi]] if c == 0 else []))
                        ev_trp[cb] = evt
                    ev_g[gi] = evt
                    aev = []
                    nonlocal ev_psA, ev_psB
                    for hh in range(nhalf):
                        c0, c1_ = hh * 16, min((hh + 1) * 16, NCHK)
                        mmv = None
                        for c in range(c0, c1_):
                            mmv = p.run("t", lambda e, c=c, c0=c0, ai=ai: e.matmul(
                                PS[4][:, (c - c0) * arep_w:(c - c0 + 1) * arep_w],
                                sel_sb[:, c * 128:(c + 1) * 128],
                                ABg[ai][0:abrows // 2, :],
                                start=True, stop=True),
                                deps=([dab_ev[ai], ev_psA, ev_cf] if c == c0 else []),
                                ev=(c == c1_ - 1))
                        ev_ae = p.run("s", lambda e, hh=hh, pi=pi, c0=c0, c1_=c1_:
                                      e.activation(
                            out=arep[pi][:, c0 * arep_w:c1_ * arep_w],
                            in_=PS[4][:, 0:(c1_ - c0) * arep_w], func=AF.Copy),
                            deps=[mmv, ev_kwlast[pi]])
                        aev.append(ev_ae)
                        ev_psA = ev_ae
                    mmb2 = p.run("t", lambda e, ai=ai: e.matmul(
                        PS[5][:, 0:arep_w], rep_sb[rep_base:abrows, :],
                        ABg[ai][abrows // 2:abrows, :],
                        start=True, stop=True), deps=[dab_ev[ai], ev_psB, ev_cf])
                    ev_be = p.run("s", lambda e, pi=pi: e.activation(
                        out=brep[pi][:], in_=PS[5][:, 0:arep_w], func=AF.Copy),
                        deps=[mmb2, ev_kwlast[pi]])
                    ev_psB = ev_be
                    ev_ab[ai] = ev_be
                    prep_st[t] = (aev, ev_be)

                def ktt(t):
                    pi = t % 2
                    aev, ev_be = prep_st.pop(t)
                    for c in range(NCHK):
                        hh = c // 16
                        eng = "g" if c in GKT else "v"
                        ev_kw[c] = p.run(eng, lambda e, c=c, pi=pi: e.tensor_tensor(
                            Ksb[:, c * ODIM:(c + 1) * ODIM]
                            .rearrange("p (o q) -> p o q", q=arep_w),
                            arep[pi][:, c * arep_w:(c + 1) * arep_w]
                            .rearrange("p (o u) -> p o u", u=1)
                            .to_broadcast([128, ODIM // arep_w, arep_w]),
                            brep[pi][:, 0:arep_w]
                            .rearrange("p (u q) -> p u q", u=1)
                            .to_broadcast([128, ODIM // arep_w, arep_w]),
                            op=ALU.mult), deps=[aev[hh], ev_be, ev_K[c]])
                    ev_kwlast[pi] = ev_kw[NCHK - 1]

                def mms(t):
                    pi = t % 2
                    ev_yev = None
                    for g in range(NG):
                        for c in range(NCHK):
                            for oi in range(GOC):
                                oc = g * GOC + oi
                                dd = [ev_kw[c]]
                                if c == 0:
                                    dd += [ev_g[t % NGB], ev_psY.get((g, oi))]
                                last_mm = p.run(
                                    "t", lambda e, c=c, oc=oc, oi=oi, pi=pi: e.matmul(
                                        PS[oi][:, 0:512],
                                        GTb[pi][:, c * 128:(c + 1) * 128],
                                        Ksb[:, c * ODIM + oc * 512:
                                            c * ODIM + (oc + 1) * 512],
                                        start=(c == 0), stop=(c == NCHK - 1)),
                                    deps=dd,
                                    ev=(c == NCHK - 1) or
                                       (g == NG - 1 and oi == GOC - 1))
                                if g == NG - 1 and oi == GOC - 1:
                                    ev_K[c] = last_mm
                                if c == NCHK - 1:
                                    ye = p.run("s", lambda e, oc=oc, oi=oi, pi=pi:
                                               e.activation(
                                        out=Yb[pi][:, oc * 512:(oc + 1) * 512],
                                        in_=PS[oi][:, 0:512], func=AF.Copy),
                                        deps=[last_mm] +
                                             ([ev_y[pi]] if oc == 0 else []))
                                    ev_psY[((g + 1) % NG, oi)] = ye
                                    ev_yev = ye
                    dsc = p.run("g", lambda e, t=t, pi=pi, out_hbm=out_hbm:
                                e.indirect_dma_start(
                        out=out_hbm, out_offset=IndirectOffsetOnAxis(SIKa[:, t:t + 1], 0),
                        in_=Yb[pi][:], in_offset=None),
                        deps=[ev_yev, sb_last[t % 8]], dma=True,
                        slot="sb%d" % (t % 8))
                    sb_last[t % 8] = dsc
                    ev_y[pi] = dsc
                    ev_gt[pi] = dsc

                for u in range(min(3, NTL)):
                    gath(u)
                prep(0)
                ktt(0)
                for t in range(NTL):
                    if t + 3 < NTL:
                        gath(t + 3)
                    if t + 1 < NTL:
                        prep(t + 1)
                    mms(t)
                    if t + 1 < NTL:
                        ktt(t + 1)
                stt.close()
                return sb_last, pre_ev

            ffn0_accums = []
            st_r1 = ExitStack()
            OH = st_r1.enter_context(nc.sbuf_tensor("OHs", [64, 2 * TOK], BF16))
            SCA = st_r1.enter_context(nc.sbuf_tensor("SCAs", [64, TOK], F32))
            SCB = st_r1.enter_context(nc.sbuf_tensor("SCBs", [64, TOK], F32))
            abscr = st_r1.enter_context(nc.sbuf_tensor("abscr", [128, 2048], F32))
            cvt = ev_cf
            for (srcT, dstT, cc_) in [(ABupT_h, ABup16, 64), (ABdnT_h, ABdn16, 32)]:
                c1 = p.run("g", lambda e, srcT=srcT, cc_=cc_: e.dma_start(
                    out=abscr[:].rearrange("p (k c) -> p k c", c=cc_),
                    in_=srcT.rearrange("(k p) c -> p k c", p=128)),
                    deps=[cvt] + iter_dep, dma=True, slot="cvf")
                cvt = p.run("g", lambda e, dstT=dstT, cc_=cc_: e.dma_start(
                    out=dstT.rearrange("(k p) c -> p k c", p=128),
                    in_=abscr[:].rearrange("p (k c) -> p k c", c=cc_)),
                    deps=[c1], dma=True, slot="cwf")
            ev_ab16 = cvt

            oh_evs = []
            prob_evs = []
            ev_xn = [None, None]
            ev_coll = []
            ev_tr = None
            for tt in range(NT):
                pi = tt % 2
                t0 = tt * 128
                lxn = p.run("y", lambda e, t0=t0, pi=pi: e.dma_start(
                    out=xnti[pi][:], in_=xn_hbm[t0:t0 + 128, :]),
                    deps=[ev_xn[pi], xnh_last, ln_done[-1]] + iter_dep, dma=True,
                    slot="fxr%d" % pi)
                for c in range(NC):
                    tr = p.run("t", lambda e, c=c, pi=pi: e.transpose(
                        PST[c % 2][:, 0:128], xnti[pi][:, c * 128:(c + 1) * 128],
                        identb[:]), deps=[lxn, ev_tr] + iter_dep)
                    ev_tr = p.run("v", lambda e, c=c: e.tensor_copy(
                        hTt[:, c * 128:(c + 1) * 128], PST[c % 2][:, 0:128]), deps=[tr])
                ev_xn[pi] = ev_tr
                mml = None
                for c in range(NC):
                    mml = p.run("t", lambda e, c=c: e.matmul(
                        PS[4][0:128, 0:64], hTt[:, c * 128:(c + 1) * 128],
                        wupT[:, c * 64:(c + 1) * 64], start=(c == 0), stop=(c == NC - 1)),
                        deps=([ev_tr, ev_cf] + iter_dep + ev_coll if c == 0 else []),
                        ev=(c == NC - 1))
                lge = p.run("v", lambda e: e.tensor_copy(logitsb[:], PS[4][0:128, 0:64]),
                            deps=[mml])
                tevs = top2idx(lge, ev_coll)
                ev_coll = collect_tile(tt, OH, tevs, 0)
                oh_evs += [ev_coll[0], ev_coll[1]]
                prob_evs += [ev_coll[2], ev_coll[3]]

            sc_last, ev_et = sort_stage(OH, SCA, SCB, oh_evs, prob_evs, 0, 0, [])
            st_r1.close()

            up_sb, up_a1 = moe_tiles("u", NC, 4096, xn_hbm, h01_hbm, 0,
                                     ABup16, 64, selup_sb, repbu_sb, 32, 64,
                                     sc_last, ev_et, [xnh_last])

            st_g = ExitStack()
            OH2 = st_g.enter_context(nc.sbuf_tensor("OH2s", [64, 2 * TOK], BF16))
            SCA2 = st_g.enter_context(nc.sbuf_tensor("SCA2s", [64, TOK], F32))
            SCB2 = st_g.enter_context(nc.sbuf_tensor("SCB2s", [64, TOK], F32))
            h0t = [st_g.enter_context(nc.sbuf_tensor("h0t%d" % k, [128, 4096], BF16))
                   for k in range(2)]
            h1t = [st_g.enter_context(nc.sbuf_tensor("h1t%d" % k, [128, 4096], BF16))
                   for k in range(2)]
            hsum = [st_g.enter_context(nc.sbuf_tensor("hsum%d" % k, [128, 4096], F32))
                    for k in range(2)]
            ht16 = [st_g.enter_context(nc.sbuf_tensor("ht16%d" % k, [128, 4096], BF16))
                    for k in range(2)]
            oh_evs2 = []
            prob_evs2 = []
            ev_h = [None, None]
            ev_ht = [None, None]
            ev_coll = []
            hs_last = None
            for tt in range(NT):
                pi = tt % 2
                t0 = tt * 128
                d0 = p.run("y", lambda e, t0=t0, pi=pi: e.dma_start(
                    out=h0t[pi][:], in_=h01_hbm[t0:t0 + 128, :]),
                    deps=up_sb + [ev_h[pi]], dma=True, slot="h0%d" % pi)
                d1 = p.run("y", lambda e, t0=t0, pi=pi: e.dma_start(
                    out=h1t[pi][:], in_=h01_hbm[TOK + t0:TOK + t0 + 128, :]),
                    deps=up_sb + [ev_h[pi]], dma=True, slot="h1%d" % pi)
                s1 = p.run("v", lambda e, pi=pi: e.tensor_tensor(
                    hsum[pi][:], h0t[pi][:], h1t[pi][:], op=ALU.add), deps=[d0, d1])
                s2 = p.run("v", lambda e, pi=pi: e.tensor_scalar(
                    hsum[pi][:], hsum[pi][:], scup[:, 0:1], None, op0=ALU.mult),
                    deps=[s1, ev_sc])
                s3 = p.run("v", lambda e, pi=pi: e.tensor_tensor(
                    hsum[pi][:], hsum[pi][:], bupb[:], op=ALU.add), deps=[s2, ev_cf])
                g1 = p.run("s", lambda e, pi=pi: e.activation(
                    out=ht16[pi][:], in_=hsum[pi][:], func=AF.Gelu),
                    deps=[s3, ev_ht[pi]])
                dh = p.run("y", lambda e, t0=t0, pi=pi: e.dma_start(
                    out=h_hbm[t0:t0 + 128, :], in_=ht16[pi][:]), deps=[g1], dma=True,
                    slot="hs")
                hs_last = dh
                ev_ht[pi] = dh
                evtr = None
                for c in range(32):
                    cb = c % 2
                    tr = p.run("t", lambda e, c=c, pi=pi, cb=cb: e.transpose(
                        PST[cb][:, 0:128], ht16[pi][:, c * 128:(c + 1) * 128],
                        identb[:]), deps=[g1, evtr])
                    evtr = p.run("v", lambda e, c=c, cb=cb: e.tensor_copy(
                        hTt[:, c * 128:(c + 1) * 128], PST[cb][:, 0:128]), deps=[tr])
                ev_h[pi] = evtr
                mml = None
                for c in range(32):
                    mml = p.run("t", lambda e, c=c: e.matmul(
                        PS[4][0:128, 0:64], hTt[:, c * 128:(c + 1) * 128],
                        wdnT[:, c * 64:(c + 1) * 64], start=(c == 0), stop=(c == 31)),
                        deps=([evtr, ev_cf] + ev_coll if c == 0 else []), ev=(c == 31))
                lge = p.run("v", lambda e: e.tensor_copy(logitsb[:], PS[4][0:128, 0:64]),
                            deps=[mml])
                tevs = top2idx(lge, ev_coll)
                ev_coll = collect_tile(tt, OH2, tevs, 4)
                oh_evs2 += [ev_coll[0], ev_coll[1]]
                prob_evs2 += [ev_coll[2], ev_coll[3]]
            for eng in ("v", "s", "t", "g", "y"):
                p.barrier_all(eng)
            sc_last2, ev_et2 = sort_stage(OH2, SCA2, SCB2, oh_evs2, prob_evs2, 4, 16384,
                                          [up_a1])
            st_g.close()

            dn_sb, _ = moe_tiles("d", 32, 1024, h_hbm, o01_hbm, 16384,
                                 ABdn16, 128, seldn_sb, repbd_sb, 64, 32,
                                 sc_last2, ev_et2, [hs_last])

            st_f = ExitStack()
            o0t = [st_f.enter_context(nc.sbuf_tensor("o0t%d" % k, [128, 1024], BF16))
                   for k in range(2)]
            o1t = [st_f.enter_context(nc.sbuf_tensor("o1t%d" % k, [128, 1024], BF16))
                   for k in range(2)]
            foutb = [st_f.enter_context(nc.sbuf_tensor("foutb%d" % k, [128, 1024], F32))
                     for k in range(2)]
            ev_f = [None, None]
            for tt in range(NT):
                pi = tt % 2
                t0 = tt * 128
                d0 = p.run("y", lambda e, t0=t0, pi=pi: e.dma_start(
                    out=o0t[pi][:], in_=o01_hbm[t0:t0 + 128, :]),
                    deps=dn_sb + [ev_f[pi]], dma=True, slot="o0%d" % pi)
                d1 = p.run("y", lambda e, t0=t0, pi=pi: e.dma_start(
                    out=o1t[pi][:], in_=o01_hbm[TOK + t0:TOK + t0 + 128, :]),
                    deps=dn_sb + [ev_f[pi]], dma=True, slot="o1%d" % pi)
                f0 = p.run("v", lambda e, pi=pi: e.tensor_tensor(
                    foutb[pi][:], o0t[pi][:], o1t[pi][:], op=ALU.add), deps=[d0, d1])
                f1 = p.run("v", lambda e, pi=pi: e.tensor_scalar(
                    foutb[pi][:], foutb[pi][:], scup[:, 1:2], None, op0=ALU.mult),
                    deps=[f0, ev_sc])
                f2 = p.run("v", lambda e, pi=pi: e.tensor_tensor(
                    foutb[pi][:], foutb[pi][:], bdnb[:], op=ALU.add), deps=[f1, ev_cf])
                ac = p.run("g", lambda e, t0=t0, pi=pi: e.dma_start(
                    out=x_out[t0:t0 + 128, :], in_=foutb[pi][:], accum_op=ALU.add),
                    deps=[f2, attn_accums[-1]] + ev_init, dma=True, slot="f0a")
                ffn0_accums.append(ac)
                ev_f[pi] = ac
            st_f.close()
            stf.close()

        if fused is None:
            p.barrier_all("y")
            p.emit()
    return nc


def prep_block0(i, x_shards):
    H, HD = 16, 64
    wqkv = _f32(i["L0_attn_w_qkv"])
    wout = _f32(i["L0_attn_w_out"])
    bout = _f32(i["L0_attn_b_out"])
    wq = wqkv[0:1024]
    wk = wqkv[1024:2048]
    wv = wqkv[2048:3072]
    wqkT = np.zeros((H, 128, 1024), np.float32)
    for h in range(H):
        qT = wq[h * 64:(h + 1) * 64, :].T
        kT = wk[h * 64:(h + 1) * 64, :].T
        qk = np.concatenate([qT.reshape(8, 128, 64), kT.reshape(8, 128, 64)], axis=2)
        wqkT[h] = qk.transpose(1, 0, 2).reshape(128, 1024)
    wvT = np.ascontiguousarray(wv.T.reshape(8, 128, 1024).transpose(1, 0, 2).reshape(128, 8192))
    woutT = np.zeros((128, 8 * 1024), np.float32)
    for h in range(H):
        blockw = wout[:, h * 64:(h + 1) * 64].T
        r0 = (h % 2) * 64
        woutT[r0:r0 + 64, (h // 2) * 1024:(h // 2 + 1) * 1024] = blockw
    boutb = np.ascontiguousarray(np.broadcast_to(bout[None, :], (128, 1024)))
    pos = np.arange(2048, dtype=np.float32)
    inv = 1.0 / (10000.0 ** (np.arange(0, 64, 2, dtype=np.float32) / 64.0))
    fr = pos[None, :] * inv[:, None]
    cosT = np.concatenate([np.cos(fr), np.cos(fr)], axis=0).astype(np.float32)
    sinT = np.concatenate([np.sin(fr), np.sin(fr)], axis=0).astype(np.float32)
    identf = np.eye(128, dtype=np.float32)
    onesf = np.ones((128, 512), dtype=np.float32)
    wupT = np.ascontiguousarray(_f32(i["L0_ffn_W_up"]).T.reshape(8, 128, 64)
                                .transpose(1, 0, 2).reshape(128, 512))
    wdnT = np.ascontiguousarray(_f32(i["L0_ffn_W_down"]).T.reshape(32, 128, 64)
                                .transpose(1, 0, 2).reshape(128, 2048))
    iota64 = np.ascontiguousarray(np.broadcast_to(np.arange(64, dtype=np.float32)[None, :],
                                                  (128, 64)))
    scud = np.array([[float(_f32(i["L0_ffn_scale_up"])[0]),
                      float(_f32(i["L0_ffn_scale_down"])[0])]], dtype=np.float32)
    bup = np.ascontiguousarray(np.broadcast_to(_f32(i["L0_ffn_bias_up"])[None, :],
                                               (128, 4096)))
    bdn = np.ascontiguousarray(np.broadcast_to(_f32(i["L0_ffn_bias_down"])[None, :],
                                               (128, 1024)))
    Aup3 = _f32(i["L0_ffn_A_up"]).reshape(64, 64, 32)
    Bup3 = _f32(i["L0_ffn_B_up"]).reshape(64, 64, 32)
    Adn3 = _f32(i["L0_ffn_A_down"]).reshape(64, 32, 64)
    Bdn3 = _f32(i["L0_ffn_B_down"]).reshape(64, 32, 64)
    ABupT = np.ascontiguousarray(np.concatenate(
        [Aup3.transpose(0, 2, 1), Bup3.transpose(0, 2, 1)], axis=1).reshape(4096, 64))
    ABdnT = np.ascontiguousarray(np.concatenate(
        [Adn3.transpose(0, 2, 1), Bdn3.transpose(0, 2, 1)], axis=1).reshape(8192, 32))
    selup = np.zeros((32, 1024), np.float32)
    for c in range(8):
        for a in range(4):
            for j in range(32):
                selup[4 * c + a, c * 128 + 32 * a + j] = 1.0
    repbu = np.zeros((64, 128), np.float32)
    for a in range(4):
        for j in range(32):
            repbu[32 + j, 32 * a + j] = 1.0
    seldn = np.zeros((64, 4096), np.float32)
    for c in range(32):
        for b2 in range(2):
            for j in range(64):
                seldn[2 * c + b2, c * 128 + 64 * b2 + j] = 1.0
    repbd = np.zeros((128, 128), np.float32)
    for b2 in range(2):
        for j in range(64):
            repbd[64 + j, 64 * b2 + j] = 1.0
    triuf = np.triu(np.ones((64, 64), np.float32), 1)
    aenc = np.arange(8192, dtype=np.float32).reshape(128, 64)
    iotap = np.arange(128, dtype=np.float32).reshape(128, 1)
    tstart = (np.arange(128, dtype=np.float32) * 128).reshape(1, 128)
    sentp = np.zeros((128, 256), np.float32)
    pp = np.arange(128)[:, None]
    kk = np.arange(128)[None, :]
    sentp[:, 0::2] = 8192.0 + (pp % 8) * 128 + kk
    common = dict(wqkT=wqkT, wvT=wvT, woutT=woutT, boutb=boutb, cosT=cosT, sinT=sinT,
                  identf=identf, onesf=onesf, wupT=wupT, wdnT=wdnT, iota64=iota64,
                  scud=scud, bup=bup, bdn=bdn, ABupT=ABupT, ABdnT=ABdnT, selup=selup,
                  repbu=repbu, seldn=seldn, repbd=repbd, triuf=triuf, aenc=aenc,
                  iotap=iotap, tstart=tstart, sentp=sentp)
    return [dict(common, x_in=x_shards[c]) for c in range(NCORES)]


def run_block0(i, x, with_ffn=True):
    shards = _shard_x(x)
    nc = build_block0(with_ffn=with_ffn)
    maps = prep_block0(i, shards)
    res = run_bass_kernel_spmd(nc, maps, core_ids=list(range(NCORES)))
    return _gather_x([r["x_out"] for r in res.results])


def build_block1(fused=None):
    nc = (bass.Bass("TRN2", target_bir_lowering=False) if fused is None
          else _NCP(fused[0], fused[2]))
    C = 256
    NCH = S // C
    E1 = 4
    I1 = 4096

    if fused is None:
        x_in = nc.dram_tensor("x_in", [TOK, D], F32, kind="ExternalInput").ap()
        x_out = nc.dram_tensor("x_out", [TOK, D], F32, kind="ExternalOutput").ap()
    else:
        x_in, x_out = fused[3], fused[4]
    xn_hbm = nc.dram_tensor("xn_hbm", [TOK, D], BF16, kind="Internal").ap()
    intT_hbm = nc.dram_tensor("intT_hbm", [32, 128, TOK], BF16, kind="Internal").ap()

    wwT_h = nc.dram_tensor("wwT", [128, 8 * 1024], F32, kind="ExternalInput").ap()
    wrT_h = nc.dram_tensor("wrT", [128, 8 * 1024], F32, kind="ExternalInput").ap()
    dec_h = nc.dram_tensor("dec", [1, 2], F32, kind="ExternalInput").ap()
    w1_h = nc.dram_tensor("w1r", [32, 128, E1 * 8 * 128], F32, kind="ExternalInput").ap()
    b1_h = nc.dram_tensor("b1e", [E1, I1], F32, kind="ExternalInput").ap()
    wo_h = nc.dram_tensor("wor", [128, 32 * 1024], F32, kind="ExternalInput").ap()
    bo_h = nc.dram_tensor("bo1", [1, D], F32, kind="ExternalInput").ap()
    wrouter_h = nc.dram_tensor("wrouter", [128, 8 * 4], F32, kind="ExternalInput").ap()
    sel4_h = nc.dram_tensor("sel4", [4, 512], F32, kind="ExternalInput").ap()
    gln_h = nc.dram_tensor("gln", [128, D], F32, kind="ExternalInput").ap()
    bln_h = nc.dram_tensor("bln", [128, D], F32, kind="ExternalInput").ap()
    identf = nc.dram_tensor("identf", [128, 128], F32, kind="ExternalInput").ap()
    onesf = nc.dram_tensor("onesf", [128, 512], F32, kind="ExternalInput").ap()

    with ExitStack() as st:
        def sb(name, shape, dt=BF16):
            return st.enter_context(nc.sbuf_tensor(name, shape, dt))

        xnT = sb("xnT1", [128, NC * TOK])
        tiles = mk_tiles(st, nc, with_xpx=True)
        identb = sb("identb1", [128, 128])
        onesb_r = sb("onesb_r1", [1, 128])
        onesf_r = sb("onesf_r1", [1, 128], F32)

        PS = [st.enter_context(nc.psum_tensor("b1ps%d" % i, [128, 512], F32)) for i in range(6)]
        PST = [st.enter_context(nc.psum_tensor("b1pst%d" % i, [128, 1024], BF16)) for i in range(2)]

        p = Prog(nc) if fused is None else fused[1]
        tiles["eps_ev"] = p.run("v", lambda e: e.memset(tiles["eps"][:], 1e-5))

        st2 = ExitStack()

        def sb2(name, shape, dt=BF16):
            return st2.enter_context(nc.sbuf_tensor(name, shape, dt))
        wwT = sb2("wwT_sb", [128, 8 * 1024])
        wrT = sb2("wrT_sb", [128, 8 * 1024])
        WT = sb2("WT_sb", [128, 8 * 1024])
        v_sb = sb2("v_sb1", [128, 16 * 1024])
        vts = sb2("vts", [128, 1024])
        wk_sb = [sb2("wk_sb%d" % i, [128, 1024]) for i in range(2)]
        wk0 = sb2("wk0", [128, 8 * 128])
        rkg = sb2("rkg", [128, 8 * 256])
        STm = sb2("STm", [128, 2 * 256])
        MT = sb2("MT", [128, 2 * 256], F32)
        readsT = sb2("readsT", [128, 8 * 256])
        gam = sb2("gam", [1, 8], F32)
        lg_pp = sb2("lg_pp", [128, 1], F32)
        gC_pp = sb2("gC_pp", [128, 1], F32)
        al_pp = sb2("al_pp", [128, 1], F32)
        gw2 = sb2("gw2", [128, 2], F32)
        gpb = sb2("gpb", [128, 256])
        iosc = sb2("iosc", [128, 256], F32)
        at1 = [sb2("at1_%d" % i, [128, 512], F32) for i in range(2)]

        for ld_fn in [
            lambda e: e.dma_start(out=identb[:], in_=identf),
            lambda e: e.dma_start(out=onesb_r[:], in_=onesf[0:1, 0:128]),
            lambda e: e.dma_start(out=onesf_r[:], in_=onesf[0:1, 0:128]),
            lambda e: e.dma_start(out=wwT[:], in_=wwT_h),
            lambda e: e.dma_start(out=wrT[:], in_=wrT_h),
            lambda e: e.dma_start(out=gam[0:1, 0:2], in_=dec_h),
        ]:
            ev_const = p.run("g", ld_fn, dma=True, slot="const")

        g1 = p.run("s", lambda e: e.activation(out=gam[0:1, 2:3], in_=gam[0:1, 0:1],
                                               func=AF.Sigmoid), deps=[ev_const])
        g2 = p.run("s", lambda e: e.activation(out=gam[0:1, 3:4], in_=gam[0:1, 2:3],
                                               func=AF.Ln), deps=[g1])
        g3 = p.run("s", lambda e: e.activation(out=gam[0:1, 4:5], in_=gam[0:1, 3:4],
                                               func=AF.Exp, scale=float(C)), deps=[g2])
        g4 = p.run("s", lambda e: e.activation(out=gam[0:1, 5:6], in_=gam[0:1, 1:2],
                                               func=AF.Exp), deps=[g2])
        bc1 = p.run("t", lambda e: e.matmul(PS[0][:, 0:3], onesf_r[0:1, :], gam[0:1, 3:6],
                                            start=True, stop=True), deps=[g3, g4, ev_const])
        bc2 = p.run("v", lambda e: e.tensor_copy(lg_pp[:], PS[0][:, 0:1]), deps=[bc1])
        bc3 = p.run("v", lambda e: e.tensor_copy(gC_pp[:], PS[0][:, 1:2]), deps=[bc1])
        bc4 = p.run("v", lambda e: e.tensor_copy(al_pp[:], PS[0][:, 2:3]), deps=[bc1])
        i1 = p.run("g", lambda e: e.iota(iosc[:, 0:2], pattern=[[-128, 2]], base=C - 1,
                                         channel_multiplier=-1,
                                         allow_small_or_imprecise_dtypes=True), deps=[bc2])
        i2 = p.run("s", lambda e: e.activation(out=gw2[:], in_=iosc[:, 0:2], func=AF.Exp,
                                               scale=lg_pp[:, 0:1]), deps=[i1, bc2])
        i3 = p.run("g", lambda e: e.iota(iosc[:], pattern=[[1, 256]], base=0,
                                         channel_multiplier=0,
                                         allow_small_or_imprecise_dtypes=True), deps=[i2])
        i4 = p.run("s", lambda e: e.activation(out=gpb[:], in_=iosc[:], func=AF.Exp,
                                               scale=lg_pp[:, 0:1]), deps=[i3, bc2])
        ev_MT = None
        for kh in range(2):
            j1 = p.run("g", lambda e, kh=kh: e.iota(iosc[:], pattern=[[1, 256]],
                                                    base=-1 - kh * 128, channel_multiplier=-1,
                                                    allow_small_or_imprecise_dtypes=True),
                       deps=[i4, ev_MT])
            j2 = p.run("v", lambda e: e.tensor_scalar(iosc[:], iosc[:], 0.0, None, op0=ALU.max),
                       deps=[j1])
            j3 = p.run("s", lambda e, kh=kh: e.activation(out=MT[:, kh * 256:(kh + 1) * 256],
                                                          in_=iosc[:], func=AF.Exp,
                                                          scale=lg_pp[:, 0:1]), deps=[j2])
            j4 = p.run("g", lambda e, kh=kh: e.iota(iosc[:], pattern=[[1, 256]],
                                                    base=-1 - kh * 128, channel_multiplier=-1,
                                                    allow_small_or_imprecise_dtypes=True),
                       deps=[j3])
            ev_MT = p.run("g", lambda e, kh=kh: e.affine_select(
                MT[:, kh * 256:(kh + 1) * 256], MT[:, kh * 256:(kh + 1) * 256],
                pattern=[[1, 256]], compare_op=ALU.is_ge, fill=0.0,
                base=-1 - kh * 128, channel_multiplier=-1), deps=[j4])

        ln_done = emit_ln_prologue(p, tiles, x_in, xnT, identb, PST, dep0=[ev_const],
                                   x_out_plus_xn=x_out, xn_hbm=xn_hbm)
        ev_init = [ln_done[-1], ln_done[-2]]

        attn_accums = []
        prev_state = None
        prev_iter = [None]
        ev_ro = None
        for b in range(NB):
            zs = p.run("v", lambda e: e.memset(WT[:], 0.0), deps=[prev_state] + prev_iter)
            ev_v = None
            for kt in range(16):
                for ic2 in range(2):
                    mme = None
                    for c in range(NC):
                        mme = p.run("t", lambda e, kt=kt, ic2=ic2, c=c, b=b: e.matmul(
                            PS[4][:, 0:512],
                            xnT[:, c * TOK + b * S + kt * 128: c * TOK + b * S + (kt + 1) * 128],
                            wwT[:, c * 1024 + ic2 * 512: c * 1024 + (ic2 + 1) * 512],
                            start=(c == 0), stop=(c == NC - 1)),
                            deps=(ln_done + [ev_v, ev_const] + prev_iter if c == 0 else []),
                            ev=(c == NC - 1))
                    ev_v = p.run("v", lambda e, kt=kt, ic2=ic2: e.tensor_copy(
                        v_sb[:, kt * 1024 + ic2 * 512: kt * 1024 + (ic2 + 1) * 512],
                        PS[4][:, 0:512]), deps=[mme])
            ev_state = zs
            ev_reads = None
            ev_stm = None
            for ch in range(NCH):
                base = b * S + ch * C
                ev_wk = []
                for khalf in range(2):
                    r0 = base + khalf * 128 - 1
                    if r0 < b * S:
                        ldw = p.run("y", lambda e, khalf=khalf, r0=r0: e.dma_start(
                            out=wk_sb[khalf][1:128, :], in_=xn_hbm[r0 + 1:r0 + 128, :]),
                            deps=[ev_state, ln_done[-1]], dma=True, slot="wk%d" % khalf)
                        zw = p.run("v", lambda e, khalf=khalf: e.memset(wk_sb[khalf][0:1, :], 0.0),
                                   deps=[ldw])
                        ev_wk.append(zw)
                    else:
                        ldw = p.run("y", lambda e, khalf=khalf, r0=r0: e.dma_start(
                            out=wk_sb[khalf][:], in_=xn_hbm[r0:r0 + 128, :]),
                            deps=[ev_state, ln_done[-1]], dma=True, slot="wk%d" % khalf)
                        ev_wk.append(ldw)
                ev_rkg = None
                for c in range(NC):
                    ev_rkg = p.run("v", lambda e, c=c, base=base: e.tensor_tensor(
                        rkg[:, c * 256:(c + 1) * 256],
                        xnT[:, c * TOK + base: c * TOK + base + 256], gpb[:],
                        op=ALU.mult), deps=[i4, ln_done[c % 2], ev_reads, ev_ro])
                for kh in range(2):
                    mms = None
                    for c in range(NC):
                        if ch == 0 and kh == 0:
                            cpw = p.run("v", lambda e, c=c, base=base: e.tensor_copy(
                                wk0[:, c * 128 + 1:c * 128 + 128],
                                xnT[:, c * TOK + base: c * TOK + base + 127]),
                                deps=[ln_done[c % 2], ev_stm, ev_ro, prev_iter[0]])
                            cpz = p.run("v", lambda e, c=c: e.memset(wk0[:, c * 128:c * 128 + 1],
                                                                     0.0), deps=[cpw])
                            mms = p.run("t", lambda e, c=c, base=base: e.matmul(
                                PS[2][:, 0:256], wk0[:, c * 128:(c + 1) * 128],
                                xnT[:, c * TOK + base: c * TOK + base + 256],
                                start=(c == 0), stop=(c == NC - 1)),
                                deps=[cpz], ev=(c == NC - 1))
                        else:
                            mms = p.run("t", lambda e, c=c, kh=kh, base=base: e.matmul(
                                PS[2][:, 0:256],
                                xnT[:, c * TOK + base + kh * 128 - 1: c * TOK + base + kh * 128 + 127],
                                xnT[:, c * TOK + base: c * TOK + base + 256],
                                start=(c == 0), stop=(c == NC - 1)),
                                deps=([ev_stm, ev_reads, ev_ro, ev_state] if c == 0 else []),
                                ev=(c == NC - 1))
                    ev_stm = p.run("v", lambda e, kh=kh: e.tensor_tensor(
                        STm[:, kh * 256:(kh + 1) * 256], PS[2][:, 0:256],
                        MT[:, kh * 256:(kh + 1) * 256], op=ALU.mult), deps=[mms, ev_MT])
                ev_reads_new = None
                reads_free = [ev_reads, ev_reads]
                for ic in range(NC):
                    pb = ic % 2
                    for jc in range(NC):
                        p.run("t", lambda e, ic=ic, jc=jc, pb=pb: e.matmul(
                            PS[pb][:, 0:256],
                            WT[:, jc * 1024 + ic * 128: jc * 1024 + (ic + 1) * 128],
                            rkg[:, jc * 256:(jc + 1) * 256],
                            start=(jc == 0), stop=False),
                            deps=([ev_rkg, ev_state, reads_free[pb]] if jc == 0 else []), ev=False)
                    mmr = None
                    for kh in range(2):
                        mmr = p.run("t", lambda e, ic=ic, kh=kh, ch=ch, pb=pb: e.matmul(
                            PS[pb][:, 0:256],
                            v_sb[:, (2 * ch + kh) * 1024 + ic * 128: (2 * ch + kh) * 1024 + (ic + 1) * 128],
                            STm[:, kh * 256:(kh + 1) * 256],
                            start=False, stop=(kh == 1)),
                            deps=([ev_stm, ev_v] if kh == 0 else []), ev=(kh == 1))
                    ev_reads_new = p.run("v", lambda e, ic=ic, pb=pb: e.tensor_copy(
                        readsT[:, ic * 256:(ic + 1) * 256], PS[pb][:, 0:256]),
                        deps=[mmr])
                    reads_free[pb] = ev_reads_new
                ev_reads = ev_reads_new
                ev_state_new = None
                sps_free = {2: ev_stm, 3: None, 4: None, 5: None}
                for khalf in range(2):
                    ev_vts = p.run("v", lambda e, khalf=khalf, ch=ch: e.tensor_scalar(
                        vts[:], v_sb[:, (2 * ch + khalf) * 1024: (2 * ch + khalf + 1) * 1024],
                        gw2[:, khalf:khalf + 1], None, op0=ALU.mult),
                        deps=[ev_v, i2, ev_state, ev_state_new])
                    for jc in range(NC):
                        for icol in range(2):
                            pidx = 2 + 2 * (jc % 2) + icol
                            mmd = p.run("t", lambda e, jc=jc, icol=icol, khalf=khalf, pidx=pidx: e.matmul(
                                PS[pidx][:, 0:512],
                                wk_sb[khalf][:, jc * 128:(jc + 1) * 128],
                                vts[:, icol * 512:(icol + 1) * 512],
                                start=True, stop=True),
                                deps=[ev_wk[khalf], ev_vts, ev_state, sps_free[pidx],
                                      ev_reads])
                            if khalf == 0:
                                stte = p.run("v", lambda e, jc=jc, icol=icol, pidx=pidx: e.scalar_tensor_tensor(
                                    WT[:, jc * 1024 + icol * 512: jc * 1024 + (icol + 1) * 512],
                                    WT[:, jc * 1024 + icol * 512: jc * 1024 + (icol + 1) * 512],
                                    gC_pp[:, 0:1], PS[pidx][:, 0:512],
                                    op0=ALU.mult, op1=ALU.add), deps=[mmd, bc3, ev_reads])
                            else:
                                stte = p.run("v", lambda e, jc=jc, icol=icol, pidx=pidx: e.tensor_tensor(
                                    WT[:, jc * 1024 + icol * 512: jc * 1024 + (icol + 1) * 512],
                                    WT[:, jc * 1024 + icol * 512: jc * 1024 + (icol + 1) * 512],
                                    PS[pidx][:, 0:512], op=ALU.add),
                                    deps=[mmd, ev_reads])
                            sps_free[pidx] = stte
                            ev_state_new = stte
                ev_state = ev_state_new
                for ct in range(2):
                    for ecol in range(2):
                        ab = ct
                        mmo = None
                        for ic in range(NC):
                            mmo = p.run("t", lambda e, ic=ic, ct=ct, ecol=ecol: e.matmul(
                                PS[2 + ct][:, 0:512],
                                readsT[:, ic * 256 + ct * 128: ic * 256 + (ct + 1) * 128],
                                wrT[:, ic * 1024 + ecol * 512: ic * 1024 + (ecol + 1) * 512],
                                start=(ic == 0), stop=(ic == NC - 1)),
                                deps=([ev_reads, ev_ro, ev_const, ev_stm, ev_state]
                                      if ic == 0 else []),
                                ev=(ic == NC - 1))
                        evo = p.run("s", lambda e, ab=ab, ct=ct: e.activation(
                            out=at1[ab][:], in_=PS[2 + ct][:, 0:512], func=AF.Identity,
                            scale=al_pp[:, 0:1]), deps=[mmo, bc4])
                        row0 = base + ct * 128
                        ev_ro = p.run("g", lambda e, row0=row0, ecol=ecol, ab=ab: e.dma_start(
                            out=x_out[row0:row0 + 128, ecol * 512:(ecol + 1) * 512],
                            in_=at1[ab][:], accum_op=ALU.add),
                            deps=[evo] + (ev_init if b == 0 and ch == 0 and ct == 0 and ecol == 0
                                          else []),
                            dma=True, slot="ro")
                        attn_accums.append(ev_ro)
            prev_state = ev_state
            prev_iter = [ev_ro, ev_reads, ev_stm]

        st2.close()
        st3 = ExitStack()

        def sb3(name, shape, dt=BF16):
            return st3.enter_context(nc.sbuf_tensor(name, shape, dt))
        wrt_sb = sb3("wrt_sb", [128, 8 * 4])
        logits = sb3("logits", [128, 4], F32)
        lmax = sb3("lmax", [128, 1], F32)
        onehot = sb3("onehot", [128, 4], F32)
        oh4 = sb3("oh4", [4, TOK])
        sel4 = sb3("sel4_sb", [4, 512])
        b1_sb = sb3("b1_sb", [4, I1])
        mb = [sb3("mb%d" % e_, [128, 1024]) for e_ in range(E1)]
        xm = sb3("xm", [128, 4 * 8 * 1024])
        w1b = [sb3("w1b%d" % i, [128, E1 * 8 * 128]) for i in range(2)]
        zev = [sb3("zev%d" % i, [128, 512]) for i in range(2)]

        for ld_fn in [
            lambda e: e.dma_start(out=wrt_sb[:], in_=wrouter_h),
            lambda e: e.dma_start(out=b1_sb[:], in_=b1_h),
            lambda e: e.dma_start(out=sel4[:], in_=sel4_h),
        ]:
            ev_c2 = p.run("g", ld_fn, dma=True, slot="const2")

        ffn_pre = attn_accums[-1:] + list(prev_iter) + [prev_state, ev_c2]

        ev_oh = None
        for ttile in range(NT):
            mml = None
            for c in range(NC):
                mml = p.run("t", lambda e, ttile=ttile, c=c: e.matmul(
                    PS[0][0:128, 0:4], xnT[:, c * TOK + ttile * 128: c * TOK + (ttile + 1) * 128],
                    wrt_sb[:, c * 4:(c + 1) * 4], start=(c == 0), stop=(c == NC - 1)),
                    deps=((ffn_pre if ttile == 0 else []) + [ev_oh] if c == 0 else []),
                    ev=(c == NC - 1))
            cl = p.run("v", lambda e: e.tensor_copy(logits[:], PS[0][0:128, 0:4]), deps=[mml])
            rm = p.run("v", lambda e: e.reduce_max(lmax[:], logits[:], axis=AX.X), deps=[cl])
            oh = p.run("v", lambda e: e.tensor_scalar(onehot[:], logits[:], lmax[:, 0:1], None,
                                                      op0=ALU.is_ge), deps=[rm])
            ohb = p.run("v", lambda e: e.tensor_copy(tiles["xnb"][0][:, 0:4], onehot[:]),
                        deps=[oh, ev_oh])
            tr = p.run("t", lambda e: e.transpose(
                PST[0][0:4, 0:128], tiles["xnb"][0][:, 0:4], identb[:]), deps=[ohb, ev_oh])
            ev_oh = p.run("v", lambda e, ttile=ttile: e.tensor_copy(
                oh4[0:4, ttile * 128:(ttile + 1) * 128], PST[0][0:4, 0:128]), deps=[tr])
        ev_oh4 = ev_oh

        ev_xm = None
        ev_w1 = [None, None]
        ev_int = None
        intT_stores = None
        for tc in range(4):
            t0 = tc * 1024
            ev_mb = None
            for e_ in range(E1):
                mmm = p.run("t", lambda e, e_=e_, t0=t0: e.matmul(
                    PS[1][:, 0:512], sel4[0:4, e_ * 128:(e_ + 1) * 128], oh4[0:4, t0:t0 + 512],
                    start=True, stop=True), deps=[ev_oh, ev_mb, ev_c2])
                mmm2 = p.run("t", lambda e, e_=e_, t0=t0: e.matmul(
                    PS[3][:, 0:512], sel4[0:4, e_ * 128:(e_ + 1) * 128],
                    oh4[0:4, t0 + 512:t0 + 1024],
                    start=True, stop=True), deps=[])
                c1e = p.run("v", lambda e, e_=e_: e.tensor_copy(mb[e_][:, 0:512], PS[1][:, 0:512]),
                            deps=[mmm])
                ev_mb = p.run("v", lambda e, e_=e_: e.tensor_copy(mb[e_][:, 512:1024],
                                                                  PS[3][:, 0:512]),
                              deps=[mmm2, c1e])
            ev_xm_new = None
            for e_ in range(E1):
                for c in range(NC):
                    ev_xm_new = p.run("v", lambda e, e_=e_, c=c, t0=t0: e.tensor_tensor(
                        xm[:, (e_ * 8 + c) * 1024:(e_ * 8 + c + 1) * 1024],
                        xnT[:, c * TOK + t0: c * TOK + t0 + 1024], mb[e_][:],
                        op=ALU.mult), deps=[ev_mb, ev_xm])
            ev_xm = ev_xm_new
            for ic in range(32):
                wb = ic % 2
                lw = p.run("g", lambda e, ic=ic, wb=wb: e.dma_start(
                    out=w1b[wb][:], in_=w1_h[ic]), deps=[ev_w1[wb]], dma=True,
                    slot="w1%d" % wb)
                for tch in range(2):
                    for e_ in range(E1):
                        for c in range(NC):
                            p.run("t", lambda e, e_=e_, c=c, ic=ic, tch=tch, wb=wb: e.matmul(
                                PS[4 + tch][:, 0:512],
                                w1b[wb][:, (e_ * 8 + c) * 128:(e_ * 8 + c + 1) * 128],
                                xm[:, (e_ * 8 + c) * 1024 + tch * 512:(e_ * 8 + c) * 1024 + (tch + 1) * 512],
                                start=(e_ == 0 and c == 0), stop=False),
                                deps=([lw, ev_xm, ev_int] if e_ == 0 and c == 0 else []),
                                ev=False)
                    mmz = p.run("t", lambda e, ic=ic, tch=tch, t0=t0: e.matmul(
                        PS[4 + tch][:, 0:512], b1_sb[0:4, ic * 128:(ic + 1) * 128],
                        oh4[0:4, t0 + tch * 512:t0 + (tch + 1) * 512],
                        start=False, stop=True), deps=[ev_oh4, ev_c2])
                    gl = p.run("s", lambda e, tch=tch: e.activation(
                        out=zev[tch][:], in_=PS[4 + tch][:, 0:512], func=AF.Gelu),
                        deps=[mmz, ev_int])
                    ev_int = p.run("y", lambda e, ic=ic, tch=tch, t0=t0: e.dma_start(
                        out=intT_hbm[ic, :, t0 + tch * 512:t0 + (tch + 1) * 512],
                        in_=zev[tch][:]), deps=[gl], dma=True, slot="it%d" % tch)
                ev_w1[wb] = ev_int
        st3.close()

        st4 = ExitStack()

        def sb4(name, shape, dt=BF16):
            return st4.enter_context(nc.sbuf_tensor(name, shape, dt))
        wo_sb = sb4("wo_sb", [128, 32 * 1024])
        itb = [sb4("itb%d" % i, [128, 32 * 128]) for i in range(2)]
        glnb = sb4("glnb", [128, D], F32)
        blnb = sb4("blnb", [128, D], F32)
        bob = sb4("bob", [1, D])
        xnrow = sb4("xnrow", [128, D])
        yt = sb4("yt", [128, D], F32)
        ffn_accums = []
        ev_c3 = None
        for ld_fn in [
            lambda e: e.dma_start(out=wo_sb[:], in_=wo_h),
            lambda e: e.dma_start(out=glnb[:], in_=gln_h),
            lambda e: e.dma_start(out=blnb[:], in_=bln_h),
            lambda e: e.dma_start(out=bob[:], in_=bo_h),
        ]:
            ev_c3 = p.run("g", ld_fn, deps=[ev_int], dma=True, slot="const3")
        eps12 = p.run("v", lambda e: e.memset(tiles["eps"][:], 1e-12), deps=[ev_int])
        tiles["eps_ev"] = eps12

        ev_itb = [None, None]
        ev_y = None
        prev_lnf = None
        prev_acc = None
        for ttile in range(NT):
            ib = ttile % 2
            li = p.run("y", lambda e, ttile=ttile, ib=ib: e.dma_start(
                out=itb[ib][:].rearrange("p (a t) -> p a t", a=32),
                in_=intT_hbm[:, :, ttile * 128:(ttile + 1) * 128].rearrange("a p t -> p a t")),
                deps=[ev_itb[ib], ev_int], dma=True, slot="itb%d" % ib)
            lx = p.run("y", lambda e, ttile=ttile: e.dma_start(
                out=xnrow[:], in_=xn_hbm[ttile * 128:(ttile + 1) * 128, :]),
                deps=[ev_y, ln_done[-1]], dma=True, slot="xnr")
            for ecol in range(2):
                mmy = None
                for ic in range(32):
                    mmy = p.run("t", lambda e, ic=ic, ecol=ecol, ib=ib: e.matmul(
                        PS[ecol][:, 0:512], itb[ib][:, ic * 128:(ic + 1) * 128],
                        wo_sb[:, ic * 1024 + ecol * 512: ic * 1024 + (ecol + 1) * 512],
                        start=(ic == 0), stop=False),
                        deps=([li, ev_c3, ev_y, prev_acc] if ic == 0 else []), ev=False)
                mmy = p.run("t", lambda e, ecol=ecol: e.matmul(
                    PS[ecol][:, 0:512], onesb_r[0:1, :], bob[0:1, ecol * 512:(ecol + 1) * 512],
                    start=False, stop=True), deps=[ev_c3])
                ev_y = p.run("v", lambda e, ecol=ecol: e.scalar_tensor_tensor(
                    yt[:, ecol * 512:(ecol + 1) * 512], xnrow[:, ecol * 512:(ecol + 1) * 512],
                    1.0, PS[ecol][:, 0:512], op0=ALU.mult, op1=ALU.add), deps=[mmy, lx])
            rsf = emit_ln_stats(p, tiles, ev_y, yt, prev_lnf)
            n1 = p.run("v", lambda e: e.tensor_scalar(
                tiles["scr0"][:], yt[:], tiles["mean"][:, 0:1], tiles["rstd"][:, 0:1],
                op0=ALU.subtract, op1=ALU.mult), deps=[rsf])
            n2 = p.run("v", lambda e: e.tensor_tensor(tiles["scr0"][:], tiles["scr0"][:],
                                                      glnb[:], op=ALU.mult), deps=[n1, ev_c3])
            n3 = p.run("v", lambda e: e.tensor_tensor(tiles["xpx"][0][:], tiles["scr0"][:],
                                                      blnb[:], op=ALU.add),
                       deps=[n2, prev_acc])
            prev_lnf = n3
            ac = p.run("g", lambda e, ttile=ttile: e.dma_start(
                out=x_out[ttile * 128:(ttile + 1) * 128, :], in_=tiles["xpx"][0][:],
                accum_op=ALU.add),
                deps=[n3] + (attn_accums[-1:] + ev_init if ttile == 0 else []),
                dma=True, slot="fa")
            prev_acc = ac
            ffn_accums.append(ac)
            ev_itb[ib] = ev_y

        if fused is None:
            p.barrier_all("y")
            p.emit()
        st4.close()
    return nc


def prep_block1(i, x_shards):
    E1, I1 = 4, 4096
    wwT = np.ascontiguousarray(
        _f32(i["L1_attn_W_write"]).T.reshape(8, 128, 1024).transpose(1, 0, 2).reshape(128, 8192))
    wrT = np.ascontiguousarray(
        _f32(i["L1_attn_W_read"]).T.reshape(8, 128, 1024).transpose(1, 0, 2).reshape(128, 8192))
    dec = np.array([[float(np.asarray(i["L1_attn_decay"])),
                     float(np.asarray(i["L1_attn_log_alpha"]))]], dtype=np.float32)
    W1 = _f32(i["L1_ffn_W1"])
    w1r = np.ascontiguousarray(
        W1.reshape(E1, 8, 128, 32, 128).transpose(3, 2, 0, 1, 4).reshape(32, 128, E1 * 8 * 128))
    b1 = _f32(i["L1_ffn_b1"])
    Wo = _f32(i["L1_ffn_Wo"])
    wor = np.ascontiguousarray(Wo.reshape(32, 128, 1024).transpose(1, 0, 2).reshape(128, 32 * 1024))
    bo = _f32(i["L1_ffn_bo"]).reshape(1, D)
    wrouter = np.ascontiguousarray(
        _f32(i["L1_ffn_W_router"]).reshape(8, 128, 4).transpose(1, 0, 2).reshape(128, 32))
    gln = np.ascontiguousarray(np.broadcast_to(_f32(i["L1_ffn_ln_gamma"])[None, :], (128, D)))
    bln = np.ascontiguousarray(np.broadcast_to(_f32(i["L1_ffn_ln_beta"])[None, :], (128, D)))
    identf = np.eye(128, dtype=np.float32)
    onesf = np.ones((128, 512), dtype=np.float32)
    sel4 = np.zeros((4, 512), np.float32)
    for e_ in range(4):
        sel4[e_, e_ * 128:(e_ + 1) * 128] = 1.0
    common = dict(wwT=wwT, wrT=wrT, dec=dec, w1r=w1r, b1e=b1, wor=wor, bo1=bo,
                  wrouter=wrouter, sel4=sel4, gln=gln, bln=bln, identf=identf, onesf=onesf)
    return [dict(common, x_in=x_shards[c]) for c in range(NCORES)]


def run_block1(i, x):
    shards = _shard_x(x)
    nc = build_block1()
    maps = prep_block1(i, shards)
    res = run_bass_kernel_spmd(nc, maps, core_ids=list(range(NCORES)))
    return _gather_x([r["x_out"] for r in res.results])


def build_block2(fused=None):
    nc = (bass.Bass("TRN2", target_bir_lowering=False) if fused is None
          else _NCP(fused[0], fused[2]))
    DS = 64
    CC = 16
    NCHU = 128

    if fused is None:
        x_in = nc.dram_tensor("x_in", [TOK, D], F32, kind="ExternalInput").ap()
        x_out = nc.dram_tensor("x_out", [TOK, D], F32, kind="ExternalOutput").ap()
    else:
        x_in, x_out = fused[3], fused[4]
    xn_hbm = nc.dram_tensor("xn_hbm", [TOK, D], BF16, kind="Internal").ap()
    xmid_hbm = nc.dram_tensor("xmid_hbm", [TOK, D], F32, kind="Internal").ap()
    dwn16 = nc.dram_tensor("dwn16", [16384, 1024], BF16, kind="Internal").ap()
    up16 = nc.dram_tensor("up16", [16384, 1024], BF16, kind="Internal").ap()
    guT_hbm = nc.dram_tensor("guT_hbm", [32, 128, TOK], BF16, kind="Internal").ap()

    w1T_h = nc.dram_tensor("w1T2", [128, 8 * 1024], F32, kind="ExternalInput").ap()
    v1T_h = nc.dram_tensor("v1T2", [128, 8 * 1024], F32, kind="ExternalInput").ap()
    w2T_h = nc.dram_tensor("w2T2", [128, 8 * 1024], F32, kind="ExternalInput").ap()
    wb12_h = nc.dram_tensor("wb12", [128, 16], F32, kind="ExternalInput").ap()
    w2br_h = nc.dram_tensor("w2br", [1, D], F32, kind="ExternalInput").ap()
    convw_h = nc.dram_tensor("convw", [128, 24], F32, kind="ExternalInput").ap()
    A_h = nc.dram_tensor("Am", [64, 64], F32, kind="ExternalInput").ap()
    BmT_h = nc.dram_tensor("BmT", [128, 8 * 64], F32, kind="ExternalInput").ap()
    CmT_h = nc.dram_tensor("CmT", [64, 1024], F32, kind="ExternalInput").ap()
    ln1g_h = nc.dram_tensor("ln1g2", [128, D], F32, kind="ExternalInput").ap()
    ln1b_h = nc.dram_tensor("ln1b2", [128, D], F32, kind="ExternalInput").ap()
    ln2g_h = nc.dram_tensor("ln2g2", [128, D], F32, kind="ExternalInput").ap()
    ln2b_h = nc.dram_tensor("ln2b2", [128, D], F32, kind="ExternalInput").ap()
    ff1T_h = nc.dram_tensor("ff1T2", [128, 8 * 2048], F32, kind="ExternalInput").ap()
    ff1b_h = nc.dram_tensor("ff1b2", [128, 16], F32, kind="ExternalInput").ap()
    ff2T_h = nc.dram_tensor("ff2T2", [128, 16 * 1024], F32, kind="ExternalInput").ap()
    ff2br_h = nc.dram_tensor("ff2br", [1, D], F32, kind="ExternalInput").ap()
    identf = nc.dram_tensor("identf", [128, 128], F32, kind="ExternalInput").ap()
    onesf = nc.dram_tensor("onesf", [128, 512], F32, kind="ExternalInput").ap()
    wqT_h = nc.dram_tensor("wqT2", [128, 8 * 512], F32, kind="ExternalInput").ap()
    keysT_h = nc.dram_tensor("keysT2", [128, 8 * 128], F32, kind="ExternalInput").ap()
    dn_h = nc.dram_tensor("dn_f32", [16384, 1024], F32, kind="ExternalInput").ap()
    upe_h = nc.dram_tensor("up_f32", [16384, 1024], F32, kind="ExternalInput").ap()
    wg_h = nc.dram_tensor("wg_r", [32, 128, 8 * 128], F32, kind="ExternalInput").ap()
    wu_h = nc.dram_tensor("wu_r", [32, 128, 8 * 128], F32, kind="ExternalInput").ap()
    wd_h = nc.dram_tensor("wd_r", [32, 128, 1024], F32, kind="ExternalInput").ap()

    with ExitStack() as st:
        def sb(name, shape, dt=BF16):
            return st.enter_context(nc.sbuf_tensor(name, shape, dt))

        tiles = mk_tiles(st, nc, with_xpx=False)
        identb = sb("identb2", [128, 128])
        onesb_r = sb("onesb_r2", [1, 128])

        PS = [st.enter_context(nc.psum_tensor("b2ps%d" % i, [128, 512], F32)) for i in range(6)]
        PST = [st.enter_context(nc.psum_tensor("b2pst%d" % i, [128, 1024], BF16)) for i in range(2)]

        p = Prog(nc) if fused is None else fused[1]
        tiles["eps_ev"] = p.run("v", lambda e: e.memset(tiles["eps"][:], 1e-5))

        for ld_fn in [
            lambda e: e.dma_start(out=identb[:], in_=identf),
            lambda e: e.dma_start(out=onesb_r[:], in_=onesf[0:1, 0:128]),
        ]:
            ev_const = p.run("g", ld_fn, dma=True, slot="const")

        ev_init = p.run("y", lambda e: e.dma_start(out=x_out, in_=x_in), dma=True, slot="init")
        ln_done = emit_ln_prologue(p, tiles, x_in, None, identb, PST, dep0=[ev_const],
                                   xn_hbm=xn_hbm)

        cvt_ev = None
        cvt_b = [None, None]
        for (srcT, dstT) in [(dn_h, dwn16), (upe_h, up16)]:
            for blk in range(32):
                bsel = blk % 2
                lde = p.run("g", lambda e, srcT=srcT, blk=blk: e.dma_start(
                    out=tiles["xnb"][blk % 2][:, 0:1024].rearrange("p (a b) -> p a b", a=1)
                    if False else tiles["xnb"][blk % 2][:],
                    in_=srcT[blk * 512:blk * 512 + 128, :]), deps=[cvt_b[bsel], ln_done[-1]],
                    dma=True, slot="cv%d" % bsel)
                st_ev = lde
                for sub in range(4):
                    if sub > 0:
                        lde = p.run("g", lambda e, srcT=srcT, blk=blk, sub=sub, bsel=bsel: e.dma_start(
                            out=tiles["xnb"][bsel][:],
                            in_=srcT[blk * 512 + sub * 128: blk * 512 + (sub + 1) * 128, :]),
                            deps=[st_ev], dma=True, slot="cv%d" % bsel)
                    st_ev = p.run("y", lambda e, dstT=dstT, blk=blk, sub=sub, bsel=bsel: e.dma_start(
                        out=dstT[blk * 512 + sub * 128: blk * 512 + (sub + 1) * 128, :],
                        in_=tiles["xnb"][bsel][:]), deps=[lde], dma=True, slot="cw%d" % bsel)
                cvt_b[bsel] = st_ev
                cvt_ev = st_ev

        st2 = ExitStack()

        def sb2(name, shape, dt=BF16):
            return st2.enter_context(nc.sbuf_tensor(name, shape, dt))
        w1T = sb2("w1T_sb2", [128, 8 * 1024])
        v1T = sb2("v1T_sb2", [128, 8 * 1024])
        w2T = sb2("w2T_sb2", [128, 8 * 1024])
        wb12 = sb2("wb12_sb", [128, 16], F32)
        w2br = sb2("w2br_sb", [1, D])
        convw = sb2("convw_sb", [128, 24], F32)
        A_sb = sb2("A_sb", [64, 64])
        BmT = sb2("BmT_sb", [128, 8 * 64])
        CmT = sb2("CmT_sb", [64, 1024])
        ln1g = sb2("ln1g_sb", [128, D])
        ln1b = sb2("ln1b_sb", [128, D])
        hT = sb2("hT2", [128, 8 * S])
        x1T = sb2("x1T", [128, 8 * S])
        cnv = sb2("cnv", [128, S])
        APT = sb2("APT", [64, 17 * 64])
        SQT = sb2("SQT", [64, 7 * 64])
        SQN = sb2("SQN", [64, 7 * 64])
        TT2 = sb2("TT2_sb", [128, 8 * 1024])
        uarr = sb2("uarr", [128, 8 * 128])
        uT = sb2("uT", [64, S])
        intraT = sb2("intraT", [64, S])
        Dcum = sb2("Dcum", [64, 128])
        Hin = sb2("Hin", [64, 128])
        at3 = [sb2("at3_%d" % i, [128, 512], F32) for i in range(2)]
        xnrow2 = sb2("xnrow2", [128, D])

        ev_c2 = None
        for ld_fn in [
            lambda e: e.dma_start(out=w1T[:], in_=w1T_h),
            lambda e: e.dma_start(out=v1T[:], in_=v1T_h),
            lambda e: e.dma_start(out=w2T[:], in_=w2T_h),
            lambda e: e.dma_start(out=wb12[:], in_=wb12_h),
            lambda e: e.dma_start(out=w2br[:], in_=w2br_h),
            lambda e: e.dma_start(out=convw[:], in_=convw_h),
            lambda e: e.dma_start(out=A_sb[:], in_=A_h),
            lambda e: e.dma_start(out=BmT[:], in_=BmT_h),
            lambda e: e.dma_start(out=CmT[:], in_=CmT_h),
            lambda e: e.dma_start(out=ln1g[:], in_=ln1g_h),
            lambda e: e.dma_start(out=ln1b[:], in_=ln1b_h),
        ]:
            ev_c2 = p.run("g", ld_fn, dma=True, slot="const2")

        ev_ap = p.run("v", lambda e: e.tensor_copy(APT[0:64, 0:64], identb[0:64, 0:64]),
                      deps=[ev_const])
        ap_evs = [ev_ap]
        for d_ in range(16):
            mm = p.run("t", lambda e, d_=d_: e.matmul(
                PS[0][0:64, 0:64], A_sb[:], APT[0:64, d_ * 64:(d_ + 1) * 64],
                start=True, stop=True), deps=[ev_c2, ev_ap])
            ev_ap = p.run("v", lambda e, d_=d_: e.tensor_copy(
                APT[0:64, (d_ + 1) * 64:(d_ + 2) * 64], PS[0][0:64, 0:64]), deps=[mm])
            ap_evs.append(ev_ap)
        sqt_e = p.run("v", lambda e: e.tensor_copy(SQT[0:64, 0:64], APT[0:64, 16 * 64:]),
                      deps=[ev_ap])
        mm = p.run("t", lambda e: e.matmul(PS[1][0:64, 0:64], APT[0:64, 15 * 64:16 * 64],
                                           A_sb[:], start=True, stop=True), deps=[ev_ap])
        sqn_e = p.run("v", lambda e: e.tensor_copy(SQN[0:64, 0:64], PS[1][0:64, 0:64]), deps=[mm])
        for l_ in range(6):
            mm1 = p.run("t", lambda e, l_=l_: e.matmul(
                PS[0][0:64, 0:64], SQN[0:64, l_ * 64:(l_ + 1) * 64],
                SQT[0:64, l_ * 64:(l_ + 1) * 64], start=True, stop=True),
                deps=[sqt_e, sqn_e])
            sqt_e = p.run("v", lambda e, l_=l_: e.tensor_copy(
                SQT[0:64, (l_ + 1) * 64:(l_ + 2) * 64], PS[0][0:64, 0:64]), deps=[mm1])
            mm2 = p.run("t", lambda e, l_=l_: e.matmul(
                PS[1][0:64, 0:64], SQT[0:64, l_ * 64:(l_ + 1) * 64],
                SQN[0:64, l_ * 64:(l_ + 1) * 64], start=True, stop=True),
                deps=[sqt_e])
            sqn_e = p.run("v", lambda e, l_=l_: e.tensor_copy(
                SQN[0:64, (l_ + 1) * 64:(l_ + 2) * 64], PS[1][0:64, 0:64]), deps=[mm2])
        ztt = p.run("v", lambda e: e.memset(TT2[:], 0.0), deps=[ln_done[-1]])
        ev_tt = ztt
        for j_ in range(16):
            for c_ in range(j_, 16):
                ev_tt = p.run("v", lambda e, j_=j_, c_=c_: e.tensor_copy(
                    TT2[(j_ % 2) * 64:(j_ % 2) * 64 + 64,
                        (j_ // 2) * 1024 + c_ * 64:(j_ // 2) * 1024 + (c_ + 1) * 64],
                    APT[0:64, (c_ - j_) * 64:(c_ - j_ + 1) * 64]), deps=[ztt, ev_ap])

        attn_accums = []
        ev_prev = [None]
        ev_x1 = None
        ev_w2acc = None
        for b in range(NB):
            ev_h = None
            prev_ln1 = None
            for tl in range(16):
                row0 = b * S + tl * 128
                lx = p.run("y", lambda e, row0=row0: e.dma_start(
                    out=xnrow2[:], in_=xn_hbm[row0:row0 + 128, :]),
                    deps=[ln_done[-1], prev_ln1] + ev_prev, dma=True, slot="lxr")
                cpf = p.run("v", lambda e: e.tensor_copy(tiles["xt"][0][:], xnrow2[:]), deps=[lx])
                rsx = emit_ln_stats(p, tiles, cpf, tiles["xt"][0], prev_ln1)
                n1 = p.run("v", lambda e: e.tensor_scalar(
                    tiles["scr0"][:], tiles["xt"][0][:], tiles["mean"][:, 0:1],
                    tiles["rstd"][:, 0:1], op0=ALU.subtract, op1=ALU.mult), deps=[rsx])
                n2 = p.run("v", lambda e: e.tensor_tensor(tiles["scr0"][:], tiles["scr0"][:],
                                                          ln1g[:], op=ALU.mult), deps=[n1, ev_c2])
                n3 = p.run("v", lambda e: e.tensor_tensor(tiles["xnb"][0][:], tiles["scr0"][:],
                                                          ln1b[:], op=ALU.add), deps=[n2])
                prev_ln1 = n3
                for c in range(NC):
                    pi = c % 2
                    tr = p.run("t", lambda e, c=c, pi=pi: e.transpose(
                        PST[pi][:, 0:128], tiles["xnb"][0][:, c * 128:(c + 1) * 128], identb[:]),
                        deps=[n3, ev_h])
                    ev_h = p.run("v", lambda e, c=c, tl=tl, pi=pi: e.tensor_copy(
                        hT[:, c * S + tl * 128: c * S + (tl + 1) * 128], PST[pi][:, 0:128]),
                        deps=[tr])
            ev_x1 = None
            for oc in range(NC):
                for qc in range(4):
                    mm = None
                    for c in range(NC):
                        mm = p.run("t", lambda e, oc=oc, qc=qc, c=c: e.matmul(
                            PS[oc % 2][:, 0:512], w1T[:, c * 1024 + oc * 128: c * 1024 + (oc + 1) * 128],
                            hT[:, c * S + qc * 512: c * S + (qc + 1) * 512],
                            start=(c == 0), stop=(c == NC - 1)),
                            deps=([ev_h, ev_c2, ev_x1] if c == 0 else []), ev=(c == NC - 1))
                    ev_x1 = p.run("s", lambda e, oc=oc, qc=qc: e.activation(
                        out=x1T[:, oc * S + qc * 512: oc * S + (qc + 1) * 512],
                        in_=PS[oc % 2][:, 0:512], func=AF.Identity, bias=wb12[:, oc:oc + 1]),
                        deps=[mm])
            ev_cv = None
            for oc in range(NC):
                c1 = p.run("v", lambda e, oc=oc: e.tensor_scalar(
                    cnv[:], x1T[:, oc * S:(oc + 1) * S], convw[:, 8 + oc:8 + oc + 1], None,
                    op0=ALU.mult), deps=[ev_x1, ev_c2, ev_cv])
                c2_ = p.run("v", lambda e, oc=oc: e.scalar_tensor_tensor(
                    cnv[:, 1:S], x1T[:, oc * S: oc * S + S - 1], convw[:, oc:oc + 1],
                    cnv[:, 1:S], op0=ALU.mult, op1=ALU.add), deps=[c1])
                c3_ = p.run("v", lambda e, oc=oc: e.scalar_tensor_tensor(
                    cnv[:, 0:S - 1], x1T[:, oc * S + 1: oc * S + S], convw[:, 16 + oc:16 + oc + 1],
                    cnv[:, 0:S - 1], op0=ALU.mult, op1=ALU.add), deps=[c2_])
                ev_cv = p.run("s", lambda e, oc=oc: e.activation(
                    out=x1T[:, oc * S:(oc + 1) * S], in_=cnv[:], func=AF.Silu), deps=[c3_])
            ev_u = None
            for qc in range(4):
                mm = None
                for c in range(NC):
                    mm = p.run("t", lambda e, qc=qc, c=c: e.matmul(
                        PS[2][0:64, 0:512], BmT[:, c * 64:(c + 1) * 64],
                        x1T[:, c * S + qc * 512: c * S + (qc + 1) * 512],
                        start=(c == 0), stop=(c == NC - 1)),
                        deps=([ev_cv, ev_c2, ev_u] if c == 0 else []), ev=(c == NC - 1))
                ev_u = p.run("v", lambda e, qc=qc: e.tensor_copy(
                    uT[0:64, qc * 512:(qc + 1) * 512], PS[2][0:64, 0:512]), deps=[mm])
            ev_ua = None
            for cpos in range(16):
                ev_ua = p.run("v", lambda e, cpos=cpos: e.tensor_copy(
                    uarr[(cpos % 2) * 64:(cpos % 2) * 64 + 64,
                         (cpos // 2) * 128:(cpos // 2) * 128 + 128],
                    uT[0:64, :].rearrange("p (a b) -> p a b", b=16)[:, :, cpos]),
                    deps=[ev_u, ev_ua] + ([ev_w2acc] if cpos == 0 else []))
            ev_ia = None
            for oc2 in range(8):
                mm = None
                for pc in range(8):
                    mm = p.run("t", lambda e, oc2=oc2, pc=pc: e.matmul(
                        PS[3][:, 0:128], TT2[:, pc * 1024 + oc2 * 128: pc * 1024 + (oc2 + 1) * 128]
                        if False else TT2[:, pc * 1024 + oc2 * 128: pc * 1024 + (oc2 + 1) * 128],
                        uarr[:, pc * 128:(pc + 1) * 128],
                        start=(pc == 0), stop=(pc == 7)),
                        deps=([ev_ua, ev_tt, ev_ia] if pc == 0 else []), ev=(pc == 7))
                for half in range(2):
                    ev_ia = p.run("v", lambda e, oc2=oc2, half=half: e.tensor_copy(
                        intraT[0:64, :].rearrange("p (a b) -> p a b", b=16)[:, :, oc2 * 2 + half],
                        PS[3][half * 64:half * 64 + 64, 0:128]), deps=[mm])
            dc = p.run("v", lambda e: e.tensor_copy(
                Dcum[0:64, :], intraT[0:64, :].rearrange("p (a b) -> p a b", b=16)[:, :, 15]),
                deps=[ev_ia])
            for l_ in range(7):
                sh = 1 << l_
                mm = p.run("t", lambda e, l_=l_, sh=sh: e.matmul(
                    PS[0][0:64, 0:128 - sh], SQT[0:64, l_ * 64:(l_ + 1) * 64],
                    Dcum[0:64, 0:128 - sh], start=True, stop=True), deps=[dc, sqt_e])
                dc = p.run("v", lambda e, l_=l_, sh=sh: e.tensor_tensor(
                    Dcum[0:64, sh:128], Dcum[0:64, sh:128], PS[0][0:64, 0:128 - sh],
                    op=ALU.add), deps=[mm])
            hi1 = p.run("v", lambda e: e.tensor_copy(Hin[0:64, 1:128], Dcum[0:64, 0:127]),
                        deps=[dc])
            hi2 = p.run("v", lambda e: e.memset(Hin[0:64, 0:1], 0.0), deps=[hi1])
            ev_it = hi2
            for cpos in range(16):
                mm = p.run("t", lambda e, cpos=cpos: e.matmul(
                    PS[1][:, 0:64], Hin[0:64, :], APT[0:64, (cpos + 1) * 64:(cpos + 2) * 64],
                    start=True, stop=True), deps=[hi2, ev_it])
                cpb = p.run("v", lambda e: e.tensor_copy(tiles["xnb"][1][:, 0:64], PS[1][:, 0:64]),
                            deps=[mm])
                tr = p.run("t", lambda e: e.transpose(PST[0][0:64, 0:128],
                                                      tiles["xnb"][1][:, 0:64], identb[:]),
                           deps=[cpb])
                ev_it = p.run("v", lambda e, cpos=cpos: e.tensor_tensor(
                    intraT[0:64, :].rearrange("p (a b) -> p a b", b=16)[:, :, cpos],
                    intraT[0:64, :].rearrange("p (a b) -> p a b", b=16)[:, :, cpos],
                    PST[0][0:64, 0:128], op=ALU.add), deps=[tr])
            hcv = ev_it
            ev_vb = None
            for oc in range(NC):
                for qc in range(4):
                    mm = None
                    for c in range(NC):
                        mm = p.run("t", lambda e, oc=oc, qc=qc, c=c: e.matmul(
                            PS[4][:, 0:512], v1T[:, c * 1024 + oc * 128: c * 1024 + (oc + 1) * 128],
                            hT[:, c * S + qc * 512: c * S + (qc + 1) * 512],
                            start=(c == 0), stop=(c == NC - 1)),
                            deps=([ev_h, ev_c2, ev_vb, ev_u, ev_ua] if c == 0 else []),
                            ev=(c == NC - 1))
                    ev_vb = p.run("s", lambda e, oc=oc, qc=qc: e.activation(
                        out=x1T[:, oc * S + qc * 512: oc * S + (qc + 1) * 512],
                        in_=PS[4][:, 0:512], func=AF.Silu, bias=wb12[:, 8 + oc:8 + oc + 1]),
                        deps=[mm])
            ev_g = None
            for oc in range(NC):
                for qc in range(4):
                    mm = p.run("t", lambda e, oc=oc, qc=qc: e.matmul(
                        PS[5][:, 0:512], CmT[:, oc * 128:(oc + 1) * 128],
                        intraT[0:64, qc * 512:(qc + 1) * 512], start=True, stop=True),
                        deps=[hcv, ev_c2, ev_g, ev_vb])
                    ev_g = p.run("v", lambda e, oc=oc, qc=qc: e.tensor_tensor(
                        hT[:, oc * S + qc * 512: oc * S + (qc + 1) * 512],
                        x1T[:, oc * S + qc * 512: oc * S + (qc + 1) * 512],
                        PS[5][:, 0:512], op=ALU.mult), deps=[mm, ev_vb])
            sm1_prev = None
            ev_w2acc = None
            for tl in range(16):
                row0 = b * S + tl * 128
                lx = p.run("y", lambda e, row0=row0: e.dma_start(
                    out=xnrow2[:], in_=xn_hbm[row0:row0 + 128, :]),
                    deps=[ev_w2acc, ln_done[-1], prev_ln1, sm1_prev], dma=True, slot="lxr")
                for ecol in range(2):
                    mm = None
                    for c in range(NC):
                        mm = p.run("t", lambda e, tl=tl, ecol=ecol, c=c: e.matmul(
                            PS[ecol][:, 0:512], hT[:, c * S + tl * 128: c * S + (tl + 1) * 128],
                            w2T[:, c * 1024 + ecol * 512: c * 1024 + (ecol + 1) * 512],
                            start=(c == 0), stop=False),
                            deps=([ev_g, ev_w2acc] if c == 0 else []), ev=False)
                    mm = p.run("t", lambda e, ecol=ecol: e.matmul(
                        PS[ecol][:, 0:512], onesb_r[0:1, :], w2br[0:1, ecol * 512:(ecol + 1) * 512],
                        start=False, stop=True), deps=[ev_c2])
                    sm1 = p.run("v", lambda e, ecol=ecol, tl=tl: e.tensor_tensor(
                        at3[ecol][:], xnrow2[:, ecol * 512:(ecol + 1) * 512], PS[ecol][:, 0:512],
                        op=ALU.add), deps=[mm, lx])
                    sm1_prev = sm1
                    sm2 = p.run("y", lambda e, row0=row0, ecol=ecol: e.dma_start(
                        out=xmid_hbm[row0:row0 + 128, ecol * 512:(ecol + 1) * 512],
                        in_=at3[ecol][:]), deps=[sm1], dma=True, slot="xm%d" % ecol)
                    ev_w2acc = p.run("g", lambda e, row0=row0, ecol=ecol: e.dma_start(
                        out=x_out[row0:row0 + 128, ecol * 512:(ecol + 1) * 512],
                        in_=at3[ecol][:], accum_op=ALU.add),
                        deps=[sm2, ev_init], dma=True, slot="am%d" % ecol)
                    attn_accums.append(ev_w2acc)
            ev_prev = [ev_w2acc, ev_g, ev_it]

        ff1T = w1T
        ev_f0 = None
        for ld_fn in [
            lambda e: e.dma_start(out=w1T[:], in_=ff1T_h[:, 0:8192]),
            lambda e: e.dma_start(out=v1T[:], in_=ff1T_h[:, 8192:16384]),
            lambda e: e.dma_start(out=wb12[:], in_=ff1b_h),
            lambda e: e.dma_start(out=w2br[:], in_=ff2br_h),
            lambda e: e.dma_start(out=ln1g[:], in_=ln2g_h),
            lambda e: e.dma_start(out=ln1b[:], in_=ln2b_h),
            lambda e: e.dma_start(out=w2T[:], in_=ff2T_h[:, 0:8192]),
        ]:
            ev_f0 = p.run("g", ld_fn, deps=list(ev_prev) + [ev_c2], dma=True, slot="const3")
        ev_f1 = p.run("g", lambda e: e.dma_start(out=hT[:, 8192:16384], in_=ff2T_h[:, 8192:16384]),
                      deps=list(ev_prev) + [ev_f0], dma=True, slot="const3")

        ffn_accums = []
        prev_t = None
        prev_ac = None
        for b in range(NB):
            for th in range(2):
                ev_l2T = None
                prev_lnx = None
                for tl in range(8):
                    row0 = b * S + th * 1024 + tl * 128
                    lx = p.run("y", lambda e, row0=row0: e.dma_start(
                        out=tiles["xt"][0][:], in_=xmid_hbm[row0:row0 + 128, :]),
                        deps=[prev_lnx, ev_f1, prev_t], dma=True, slot="lx2")
                    rsx = emit_ln_stats(p, tiles, lx, tiles["xt"][0], prev_lnx)
                    n1 = p.run("v", lambda e: e.tensor_scalar(
                        tiles["scr0"][:], tiles["xt"][0][:], tiles["mean"][:, 0:1],
                        tiles["rstd"][:, 0:1], op0=ALU.subtract, op1=ALU.mult), deps=[rsx])
                    n2 = p.run("v", lambda e: e.tensor_tensor(
                        tiles["scr0"][:], tiles["scr0"][:], ln1g[:], op=ALU.mult),
                        deps=[n1, ev_f0])
                    n3 = p.run("v", lambda e: e.tensor_tensor(
                        tiles["xnb"][0][:], tiles["scr0"][:], ln1b[:], op=ALU.add), deps=[n2])
                    prev_lnx = n3
                    for c in range(NC):
                        pi = c % 2
                        tr = p.run("t", lambda e, c=c, pi=pi: e.transpose(
                            PST[pi][:, 0:128], tiles["xnb"][0][:, c * 128:(c + 1) * 128],
                            identb[:]), deps=[n3, ev_l2T])
                        ev_l2T = p.run("v", lambda e, c=c, tl=tl, pi=pi: e.tensor_copy(
                            hT[:, c * 1024 + tl * 128: c * 1024 + (tl + 1) * 128],
                            PST[pi][:, 0:128]), deps=[tr])
                ev_ff1 = None
                for oc in range(16):
                    for qc in range(2):
                        mm = None
                        for c in range(NC):
                            mm = p.run("t", lambda e, oc=oc, qc=qc, c=c: e.matmul(
                                PS[oc % 2][:, 0:512],
                                (w1T if oc < 8 else v1T)[:, ((oc % 8) * 8 + c) * 128:
                                                         ((oc % 8) * 8 + c) * 128 + 128],
                                hT[:, c * 1024 + qc * 512: c * 1024 + (qc + 1) * 512],
                                start=(c == 0), stop=(c == NC - 1)),
                                deps=([ev_l2T, ev_f0, ev_ff1, prev_t] if c == 0 else []),
                                ev=(c == NC - 1))
                        ev_ff1 = p.run("s", lambda e, oc=oc, qc=qc: e.activation(
                            out=x1T[:, oc * 1024 + qc * 512: oc * 1024 + (qc + 1) * 512],
                            in_=PS[oc % 2][:, 0:512], func=AF.Gelu, bias=wb12[:, oc:oc + 1]),
                            deps=[mm])
                for tl in range(8):
                    row0 = b * S + th * 1024 + tl * 128
                    for ecol in range(2):
                        mm = None
                        for hc in range(16):
                            off2 = 0 if hc < 8 else 8192
                            w2src = w2T if hc < 8 else hT
                            mm = p.run("t", lambda e, tl=tl, ecol=ecol, hc=hc, w2src=w2src,
                                       off2=off2: e.matmul(
                                PS[2 + ecol][:, 0:512],
                                x1T[:, hc * 1024 + tl * 128: hc * 1024 + (tl + 1) * 128],
                                w2src[:, off2 + (hc % 8) * 1024 + ecol * 512:
                                      off2 + (hc % 8) * 1024 + (ecol + 1) * 512],
                                start=(hc == 0), stop=False),
                                deps=([ev_ff1, prev_ac, ev_f1] if hc == 0 else []), ev=False)
                        mm = p.run("t", lambda e, ecol=ecol: e.matmul(
                            PS[2 + ecol][:, 0:512], onesb_r[0:1, :],
                            w2br[0:1, ecol * 512:(ecol + 1) * 512],
                            start=False, stop=True), deps=[ev_f0])
                        cpe = p.run("v", lambda e, ecol=ecol: e.tensor_copy(
                            at3[ecol][:], PS[2 + ecol][:, 0:512]), deps=[mm, prev_ac])
                        prev_ac = p.run("g", lambda e, row0=row0, ecol=ecol: e.dma_start(
                            out=x_out[row0:row0 + 128, ecol * 512:(ecol + 1) * 512],
                            in_=at3[ecol][:], accum_op=ALU.add),
                            deps=[cpe, attn_accums[-1], ev_init], dma=True,
                            slot="fa%d" % ecol)
                        ffn_accums.append(prev_ac)
                prev_t = prev_ac


        st2.close()
        stp = ExitStack()

        def sbp(name, shape, dt=BF16):
            return stp.enter_context(nc.sbuf_tensor(name, shape, dt))
        xnT = sbp("xnT2", [128, NC * TOK])
        qT = sbp("qT2", [128, 4 * TOK])
        keysT = sbp("keysT_sb", [128, 8 * 128])
        wqT = sbp("wqT_sb", [128, 8 * 512])
        simev = sbp("simev", [128, 128])
        simtok = sbp("simtok", [128, 128])
        sx = [sbp("sx%d" % i, [128, 8], F32) for i in range(2)]
        ixv = [sbp("ix%d" % i, [128, 8], mybir.dt.uint32) for i in range(2)]
        ixf = [sbp("ixf%d" % i, [128, 8], F32) for i in range(2)]
        ix0m = sbp("ix0m", [128, 8], F32)
        alls = sbp("alls", [128, 64], F32)
        alli = sbp("alli", [128, 64], F32)
        eqs = sbp("eqs", [128, 64], F32)
        sc8 = sbp("sc8", [128, 8], F32)
        nm8 = sbp("nm8", [128, 1], F32)
        sm8 = sbp("sm8", [128, 8], F32)
        w8 = sbp("w8", [128, 8], F32)
        idxf = sbp("idxf", [128, 8], F32)
        idxi = sbp("idxi", [128, 8], I32)
        Dg = [sbp("Dg%d" % i, [128, 1024]) for i in range(2)]
        Ug8 = sbp("Ug8", [128, 8 * 1024])
        prodscr = sbp("prodscr", [128, 1024])
        accp = [sbp("accp%d" % i, [128, 1024], F32) for i in range(2)]
        xnrow3 = sbp("xnrow3", [128, 1024])
        wgb = [sbp("wgb%d" % i, [128, 1024]) for i in range(2)]
        wub = [sbp("wub%d" % i, [128, 1024]) for i in range(2)]
        gscr = sbp("gscr", [128, 512])
        gub = sbp("gub", [128, 512])

        ev_cp = None
        for ld_fn in [
            lambda e: e.dma_start(out=keysT[:], in_=keysT_h),
            lambda e: e.dma_start(out=wqT[:], in_=wqT_h),
        ]:
            ev_cp = p.run("g", ld_fn, deps=[prev_ac, prev_t], dma=True, slot="constp")

        ev_xnT = None
        ev_xl = None
        for tt in range(NT):
            lx = p.run("y", lambda e, tt=tt: e.dma_start(
                out=xnrow3[:], in_=xn_hbm[tt * 128:(tt + 1) * 128, :]),
                deps=[ev_xl, ln_done[-1], prev_ac, prev_t], dma=True, slot="pxr")
            ev_xl = lx
            for c in range(NC):
                tr = p.run("t", lambda e, c=c: e.transpose(
                    PST[c % 2][:, 0:128], xnrow3[:, c * 128:(c + 1) * 128], identb[:]),
                    deps=[lx, ev_xnT])
                ev_xnT = p.run("v", lambda e, tt=tt, c=c: e.tensor_copy(
                    xnT[:, c * TOK + tt * 128: c * TOK + (tt + 1) * 128],
                    PST[c % 2][:, 0:128]), deps=[tr])
            ev_xl = ev_xnT

        ev_q = None
        for rc_ in range(4):
            for qc in range(8):
                mm = None
                for c in range(NC):
                    mm = p.run("t", lambda e, rc_=rc_, qc=qc, c=c: e.matmul(
                        PS[0][:, 0:512], wqT[:, c * 512 + rc_ * 128: c * 512 + (rc_ + 1) * 128],
                        xnT[:, c * TOK + qc * 512: c * TOK + (qc + 1) * 512],
                        start=(c == 0), stop=(c == NC - 1)),
                        deps=([ev_cp, ev_q, prev_ac, ev_xnT] if c == 0 else []),
                        ev=(c == NC - 1))
                ev_q = p.run("v", lambda e, rc_=rc_, qc=qc: e.tensor_copy(
                    qT[:, rc_ * TOK + qc * 512: rc_ * TOK + (qc + 1) * 512], PS[0][:, 0:512]),
                    deps=[mm])

        peer_accums = []
        ev_idx = None
        ev_dg = [None, None]
        ev_ug = None
        ev_acc = None
        ev_xr = ev_xnT
        prev_pa = ffn_accums[-1]
        for ttile in range(NT):
            t0 = ttile * 128
            lxr = p.run("y", lambda e, t0=t0: e.dma_start(
                out=xnrow3[:], in_=xn_hbm[t0:t0 + 128, :]), deps=[ev_xr, ln_done[-1]],
                dma=True, slot="pxr")
            za = p.run("v", lambda e: e.memset(accp[0][:], 0.0), deps=[ev_acc, prev_pa])
            acc_par = 0
            ev_accl = za
            for h in range(4):
                hb = (h % 2) * 64
                for pp in range(2):
                    rc_ = 2 * pp + h // 2
                    mm = p.run("t", lambda e, h=h, pp=pp, rc_=rc_, hb=hb, t0=t0: e.matmul(
                        PS[1][:, 0:128], keysT[hb:hb + 64, (2 * h + pp) * 128:(2 * h + pp + 1) * 128],
                        qT[hb:hb + 64, rc_ * TOK + t0: rc_ * TOK + t0 + 128],
                        start=True, stop=True), deps=[ev_q, ev_cp, ev_idx])
                    cpe = p.run("v", lambda e: e.tensor_copy(simev[:], PS[1][:, 0:128]), deps=[mm])
                    tr = p.run("t", lambda e: e.transpose(PST[0][:, 0:128], simev[:], identb[:]),
                               deps=[cpe])
                    cpe2 = p.run("v", lambda e: e.tensor_copy(simtok[:], PST[0][:, 0:128]),
                                 deps=[tr])
                    mx = p.run("v", lambda e, pp=pp: e.max(out=sx[pp][:], in_=simtok[:]),
                               deps=[cpe2])
                    mi = p.run("v", lambda e, pp=pp: e.max_index(
                        out=ixv[pp][:], in_max=sx[pp][:], in_values=simtok[:]), deps=[mx])
                    ev_idx = p.run("v", lambda e, pp=pp: e.tensor_copy(ixf[pp][:], ixv[pp][:]),
                                   deps=[mi])
                im = p.run("v", lambda e: e.tensor_scalar(ix0m[:], ixf[0][:], 128.0, None,
                                                          op0=ALU.mult), deps=[ev_idx])
                ev_cb = None
                for j0 in range(8):
                    c1_ = p.run("v", lambda e, j0=j0: e.tensor_scalar(
                        alls[:, j0 * 8:(j0 + 1) * 8], sx[1][:], sx[0][:, j0:j0 + 1], None,
                        op0=ALU.add), deps=[ev_idx, ev_cb])
                    ev_cb = p.run("v", lambda e, j0=j0: e.tensor_scalar(
                        alli[:, j0 * 8:(j0 + 1) * 8], ixf[1][:], ix0m[:, j0:j0 + 1], None,
                        op0=ALU.add), deps=[im, c1_])
                mx8 = p.run("v", lambda e: e.max(out=sc8[:], in_=alls[:]), deps=[ev_cb])
                ev_sel = None
                for j in range(8):
                    e1 = p.run("v", lambda e, j=j: e.tensor_scalar(
                        eqs[:], alls[:], sc8[:, j:j + 1], None, op0=ALU.is_ge),
                        deps=[mx8, ev_sel])
                    e2 = p.run("v", lambda e: e.tensor_tensor(eqs[:], eqs[:], alli[:],
                                                              op=ALU.mult), deps=[e1])
                    ev_sel = p.run("v", lambda e, j=j: e.reduce_max(
                        idxf[:, j:j + 1], eqs[:], axis=AX.X), deps=[e2])
                cvi = p.run("v", lambda e: e.tensor_copy(idxi[:], idxf[:]), deps=[ev_sel])
                rm8 = p.run("v", lambda e: e.reduce_max(nm8[:], sc8[:], axis=AX.X), deps=[mx8])
                ng8 = p.run("v", lambda e: e.tensor_scalar(nm8[:], nm8[:], -1.0, None,
                                                           op0=ALU.mult), deps=[rm8])
                ex8 = p.run("s", lambda e: e.activation(out=sm8[:], in_=sc8[:], func=AF.Exp,
                                                        bias=nm8[:, 0:1]), deps=[ng8])
                su8 = p.run("v", lambda e: e.reduce_sum(nm8[:], sm8[:], axis=AX.X), deps=[ex8])
                rc8 = p.run("v", lambda e: e.reciprocal(nm8[:], nm8[:]), deps=[su8])
                smn = p.run("v", lambda e: e.tensor_scalar(sm8[:], sm8[:], nm8[:, 0:1], None,
                                                           op0=ALU.mult), deps=[rc8])
                ev_w = None
                for j in range(8):
                    jb = j % 2
                    gd = p.run("g", lambda e, j=j, jb=jb: e.indirect_dma_start(
                        out=Dg[jb][:], out_offset=None, in_=dwn16,
                        in_offset=IndirectOffsetOnAxis(idxi[:, j:j + 1], 0)),
                        deps=[cvi, ev_dg[jb], cvt_ev], dma=True, slot="gd%d" % jb)
                    gu_ = p.run("g", lambda e, j=j: e.indirect_dma_start(
                        out=Ug8[:, j * 1024:(j + 1) * 1024], out_offset=None, in_=up16,
                        in_offset=IndirectOffsetOnAxis(idxi[:, j:j + 1], 0)),
                        deps=[cvi, (ev_accl if j == 0 else None), ev_ug, cvt_ev],
                        dma=True, slot="gu")
                    ev_dg[jb] = p.run("v", lambda e, j=j, jb=jb: e.scalar_tensor_tensor(
                        prodscr[:], Dg[jb][:], 1.0, xnrow3[:], op0=ALU.mult, op1=ALU.mult,
                        accum_out=w8[:, j:j + 1]), deps=[gd, lxr, ev_w])
                    ev_w = ev_dg[jb]
                ev_ug = gu_
                wsl = p.run("s", lambda e: e.activation(out=w8[:], in_=w8[:], func=AF.Silu),
                            deps=[ev_w])
                wti = p.run("v", lambda e: e.tensor_tensor(w8[:], w8[:], sm8[:], op=ALU.mult),
                            deps=[wsl, smn])
                for j in range(8):
                    nx = 1 - acc_par
                    ev_accl = p.run("v", lambda e, j=j, acc_par=acc_par, nx=nx: e.scalar_tensor_tensor(
                        accp[nx][:], Ug8[:, j * 1024:(j + 1) * 1024], w8[:, j:j + 1],
                        accp[acc_par][:], op0=ALU.mult, op1=ALU.add),
                        deps=[wti, ev_ug, ev_accl])
                    acc_par = nx
            ev_acc = ev_accl
            pa = p.run("g", lambda e, t0=t0, acc_par=acc_par: e.dma_start(
                out=x_out[t0:t0 + 128, :], in_=accp[acc_par][:], accum_op=ALU.add),
                deps=[ev_accl, prev_pa, ev_init], dma=True, slot="pa")
            prev_pa = pa
            peer_accums.append(pa)
            ev_xr = ev_w

        ev_wg = [None, None]
        ev_wu = [None, None]
        ev_gu = None
        for tc8 in range(8):
            t0 = tc8 * 512
            for ib in range(32):
                wb = ib % 2
                lg_ = p.run("g", lambda e, ib=ib, wb=wb: e.dma_start(
                    out=wgb[wb][:], in_=wg_h[ib]), deps=[ev_wg[wb], prev_pa], dma=True,
                    slot="wg%d" % wb)
                lu_ = p.run("g", lambda e, ib=ib, wb=wb: e.dma_start(
                    out=wub[wb][:], in_=wu_h[ib]), deps=[ev_wu[wb]], dma=True,
                    slot="wu%d" % wb)
                mmg = None
                for c in range(NC):
                    mmg = p.run("t", lambda e, ib=ib, wb=wb, c=c, t0=t0: e.matmul(
                        PS[4][:, 0:512], wgb[wb][:, c * 128:(c + 1) * 128],
                        xnT[:, c * TOK + t0: c * TOK + t0 + 512],
                        start=(c == 0), stop=(c == NC - 1)),
                        deps=([lg_, ev_gu] if c == 0 else []), ev=(c == NC - 1))
                sg = p.run("s", lambda e: e.activation(out=gscr[:], in_=PS[4][:, 0:512],
                                                       func=AF.Silu), deps=[mmg])
                mmu = None
                for c in range(NC):
                    mmu = p.run("t", lambda e, ib=ib, wb=wb, c=c, t0=t0: e.matmul(
                        PS[5][:, 0:512], wub[wb][:, c * 128:(c + 1) * 128],
                        xnT[:, c * TOK + t0: c * TOK + t0 + 512],
                        start=(c == 0), stop=(c == NC - 1)),
                        deps=([lu_, ev_gu] if c == 0 else []), ev=(c == NC - 1))
                gm = p.run("v", lambda e: e.tensor_tensor(gub[:], gscr[:], PS[5][:, 0:512],
                                                          op=ALU.mult), deps=[mmu, sg])
                ev_gu = p.run("y", lambda e, ib=ib, t0=t0: e.dma_start(
                    out=guT_hbm[ib, :, t0:t0 + 512], in_=gub[:]), deps=[gm], dma=True,
                    slot="gw")
                ev_wg[wb] = mmg
                ev_wu[wb] = mmu
        stp.close()

        stw = ExitStack()

        def sbw(name, shape, dt=BF16):
            return stw.enter_context(nc.sbuf_tensor(name, shape, dt))
        wd_sb = sbw("wd_sb", [128, 32 * 1024])
        itb2 = [sbw("itb2_%d" % i, [128, 32 * 128]) for i in range(2)]
        aw = [sbw("aw%d" % i, [128, 512], F32) for i in range(2)]
        ev_wd = None
        for ib in range(32):
            ev_wd = p.run("g", lambda e, ib=ib: e.dma_start(
                out=wd_sb[:, ib * 1024:(ib + 1) * 1024], in_=wd_h[ib]),
                deps=[ev_gu, prev_pa] if ib == 0 else [], dma=True, slot="wd")
        ev_it2 = [None, None]
        prev_ga = None
        for ttile in range(NT):
            ib2 = ttile % 2
            li = p.run("y", lambda e, ttile=ttile, ib2=ib2: e.dma_start(
                out=itb2[ib2][:].rearrange("p (a t) -> p a t", a=32),
                in_=guT_hbm[:, :, ttile * 128:(ttile + 1) * 128].rearrange("a p t -> p a t")),
                deps=[ev_it2[ib2], ev_gu], dma=True, slot="gi%d" % ib2)
            for ecol in range(2):
                mm = None
                for ic in range(32):
                    mm = p.run("t", lambda e, ic=ic, ecol=ecol, ib2=ib2: e.matmul(
                        PS[ecol][:, 0:512], itb2[ib2][:, ic * 128:(ic + 1) * 128],
                        wd_sb[:, ic * 1024 + ecol * 512: ic * 1024 + (ecol + 1) * 512],
                        start=(ic == 0), stop=(ic == 31)),
                        deps=([li, ev_wd, prev_ga] if ic == 0 else []), ev=(ic == 31))
                cpe = p.run("v", lambda e, ecol=ecol: e.tensor_copy(aw[ecol][:],
                                                                    PS[ecol][:, 0:512]),
                            deps=[mm, prev_ga])
                prev_ga = p.run("g", lambda e, ttile=ttile, ecol=ecol: e.dma_start(
                    out=x_out[ttile * 128:(ttile + 1) * 128, ecol * 512:(ecol + 1) * 512],
                    in_=aw[ecol][:], accum_op=ALU.add),
                    deps=[cpe, prev_pa, ev_init], dma=True, slot="ga")
            ev_it2[ib2] = prev_ga
        stw.close()
        if fused is None:
            p.barrier_all("y")
            p.emit()
    return nc


def prep_block2(i, x_shards):
    def chunkT(W):
        WT = _f32(W).T
        din, dout = WT.shape
        return np.ascontiguousarray(WT.reshape(din // 128, 128, dout).transpose(1, 0, 2)
                                    .reshape(128, (din // 128) * dout))
    w1T2 = chunkT(i["L2_attn_w1_w"])
    v1T2 = chunkT(i["L2_attn_v1_w"])
    w2T2 = chunkT(i["L2_attn_w2_w"])
    wb12 = np.zeros((128, 16), np.float32)
    wb12[:, 0:8] = _f32(i["L2_attn_w1_b"]).reshape(8, 128).T
    wb12[:, 8:16] = _f32(i["L2_attn_v1_b"]).reshape(8, 128).T
    w2br = _f32(i["L2_attn_w2_b"]).reshape(1, D)
    cw = _f32(i["L2_attn_conv_w"])
    convw = np.zeros((128, 24), np.float32)
    for tap in range(3):
        convw[:, tap * 8:(tap + 1) * 8] = cw[:, 0, tap].reshape(8, 128).T
    Am = _f32(i["L2_attn_A"])
    BmT = chunkT(i["L2_attn_Bm"])
    CmT = np.ascontiguousarray(_f32(i["L2_attn_Cm"]).T)
    ln1g = np.ascontiguousarray(np.broadcast_to(_f32(i["L2_attn_ln1_g"])[None, :], (128, D)))
    ln1b = np.ascontiguousarray(np.broadcast_to(_f32(i["L2_attn_ln1_b"])[None, :], (128, D)))
    ln2g = np.ascontiguousarray(np.broadcast_to(_f32(i["L2_attn_ln2_g"])[None, :], (128, D)))
    ln2b = np.ascontiguousarray(np.broadcast_to(_f32(i["L2_attn_ln2_b"])[None, :], (128, D)))
    ff1w = _f32(i["L2_attn_ff1_w"])
    ff1T = np.ascontiguousarray(
        ff1w.reshape(16, 128, 8, 128).transpose(3, 0, 2, 1).reshape(128, 16 * 8 * 128))
    ff1b = np.ascontiguousarray(_f32(i["L2_attn_ff1_b"]).reshape(16, 128).T)
    ff2T = chunkT(i["L2_attn_ff2_w"])
    ff2br = _f32(i["L2_attn_ff2_b"]).reshape(1, D)
    wqT2 = chunkT(i["L2_ffn_Wq"])
    keys = _f32(i["L2_ffn_keys"])
    keysT2 = np.zeros((128, 8 * 128), np.float32)
    for h in range(4):
        for pp in range(2):
            base = (h % 2) * 64
            keysT2[base:base + 64, (2 * h + pp) * 128:(2 * h + pp + 1) * 128] = \
                keys[h, :, pp, :].T
    dn = _f32(i["L2_ffn_down_embed"])
    up = _f32(i["L2_ffn_up_embed"])
    Wg = _f32(i["L2_ffn_Wg"])
    wg_r = np.ascontiguousarray(
        Wg.reshape(32, 128, 8, 128).transpose(1, 3, 2, 0)[0] if False else
        Wg.reshape(32, 128, 8, 128).transpose(0, 3, 2, 1).reshape(32, 128, 1024))
    Wu = _f32(i["L2_ffn_Wu"])
    wu_r = np.ascontiguousarray(Wu.reshape(32, 128, 8, 128).transpose(0, 3, 2, 1)
                                .reshape(32, 128, 1024))
    Wd = _f32(i["L2_ffn_Wd"])
    wd_r = np.ascontiguousarray(Wd.T.reshape(32, 128, 1024))
    identf = np.eye(128, dtype=np.float32)
    onesf = np.ones((128, 512), dtype=np.float32)
    common = dict(w1T2=w1T2, v1T2=v1T2, w2T2=w2T2, wb12=wb12, w2br=w2br, convw=convw,
                  Am=Am, BmT=BmT, CmT=CmT, ln1g2=ln1g, ln1b2=ln1b, ln2g2=ln2g, ln2b2=ln2b,
                  ff1T2=ff1T, ff1b2=ff1b, ff2T2=ff2T, ff2br=ff2br, identf=identf,
                  onesf=onesf, wqT2=wqT2, keysT2=keysT2, dn_f32=dn, up_f32=up,
                  wg_r=wg_r, wu_r=wu_r, wd_r=wd_r)
    return [dict(common, x_in=x_shards[c]) for c in range(NCORES)]


def run_block2(i, x):
    shards = _shard_x(x)
    nc = build_block2()
    maps = prep_block2(i, shards)
    res = run_bass_kernel_spmd(nc, maps, core_ids=list(range(NCORES)))
    return _gather_x([r["x_out"] for r in res.results])


def build_fused():
    nc = bass.Bass("TRN2", target_bir_lowering=False)
    x_in = nc.dram_tensor("x_in", [TOK, D], F32, kind="ExternalInput").ap()
    x1h = nc.dram_tensor("x1h", [TOK, D], F32, kind="Internal").ap()
    x2h = nc.dram_tensor("x2h", [TOK, D], F32, kind="Internal").ap()
    x3h = nc.dram_tensor("x3h", [TOK, D], F32, kind="Internal").ap()
    x_out = nc.dram_tensor("x_out", [TOK, D], F32, kind="ExternalOutput").ap()
    p = Prog(nc)
    build_block0(with_ffn=True, fused=(nc, p, "b0_", x_in, x1h))
    p.block_boundary()
    build_block1(fused=(nc, p, "b1_", x1h, x2h))
    p.block_boundary()
    build_block2(fused=(nc, p, "b2_", x2h, x3h))
    p.block_boundary()
    build_block3(fused=(nc, p, "b3_", x3h, x_out))
    p.barrier_all("y")
    p.emit()
    return nc


def prep_fused(i, x_shards):
    zshards = [np.zeros((2, 2), np.float32)] * NCORES
    parts = [("b0_", prep_block0(i, zshards)), ("b1_", prep_block1(i, zshards)),
             ("b2_", prep_block2(i, zshards)), ("b3_", prep_block3(i, zshards))]
    maps = []
    for c in range(NCORES):
        d = {"x_in": x_shards[c]}
        for pfx, ms in parts:
            for k, v in ms[c].items():
                if k == "x_in":
                    continue
                d[pfx + k] = v
        maps.append(d)
    return maps


_NC_CACHE = {}


def kernel(**inputs):
    x = np.ascontiguousarray(np.asarray(inputs["L0_attn_x"], dtype=np.float32))
    if "fused" not in _NC_CACHE:
        _NC_CACHE["fused"] = build_fused()
    shards = _shard_x(x)
    maps = prep_fused(inputs, shards)
    res = run_bass_kernel_spmd(_NC_CACHE["fused"], maps, core_ids=list(range(NCORES)))
    x = _gather_x([r["x_out"] for r in res.results])
    return np.ascontiguousarray(x.reshape(16, 2048, 1024), dtype=np.float32)
